# version 05
# baseline (speedup 1.0000x reference)
import numpy as np
import ml_dtypes
import concourse.bass as bass
import concourse.mybir as mybir
from concourse.bass import IndirectOffsetOnAxis
from concourse.bass_utils import run_bass_kernel_spmd

F32 = mybir.dt.float32
BF16 = mybir.dt.bfloat16
U32 = mybir.dt.uint32
AF = mybir.ActivationFunctionType
ALU = mybir.AluOpType
AX = mybir.AxisListType
BF16NP = ml_dtypes.bfloat16

CORES = 8
D = 1024
L = 8
NSEQ = 4096
NLOC = NSEQ // CORES
TOK = L * NLOC
TILES = TOK // 128
C4 = 4
HA = 8
HM = 4
E = 8
FF = 4096
CAP = 4096
CPC = 1024
EPS = 1e-5
OOB = float(1 << 24)


class Emit:
    def __init__(self):
        self.cnt = {}

    def mark(self, instr, sem, k=1):
        instr.then_inc(sem, k)
        self.cnt[sem.name] = self.cnt.get(sem.name, 0) + k
        return self.cnt[sem.name]

    def n(self, sem):
        return self.cnt.get(sem.name, 0)


class Scope:

    def __init__(self, nc):
        self.nc = nc
        self.cms = []

    def T(self, name, shape, dt):
        cm = self.nc.sbuf_tensor(name, shape, dt)
        t = cm.__enter__()
        self.cms.append(cm)
        return t

    def close(self):
        for cm in reversed(self.cms):
            cm.__exit__(None, None, None)
        self.cms = []


def barrier(block, em, sems_waits):
    def _b(e):
        for sem, val in sems_waits:
            e.wait_ge(sem, val)
    block.vector(_b)
    block.scalar(_b)
    block.tensor(_b)
    block.sync(_b)
    block.gpsimd(_b)


def _ln_chain(block, em, sDv, sSe, src, xhat_out, ln):
    def _v1(v):
        v.wait_ge(sDv, em.n(sDv))
        n0 = em.mark(v.reduce_sum(ln["sx"][:], src, axis=AX.X), sDv)
        v.wait_ge(sDv, n0)
        em.mark(v.tensor_scalar(ln["negmu"][:], ln["sx"][:], -1.0 / D, None, op0=ALU.mult), sDv)
    block.vector(_v1)

    def _s1(s):
        s.wait_ge(sDv, em.n(sDv))
        em.mark(s.activation(ln["dummy"][:], src, AF.Square, bias=ln["negmu"][:, 0:1],
                             scale=1.0, accum_out=ln["svar"][:]), sSe)
    block.scalar(_s1)

    def _v2(v):
        v.wait_ge(sSe, em.n(sSe))
        em.mark(v.tensor_scalar(ln["vep"][:], ln["svar"][:], 1.0 / D, EPS,
                                op0=ALU.mult, op1=ALU.add), sDv)
    block.vector(_v2)

    def _s2a(s):
        s.wait_ge(sDv, em.n(sDv))
        em.mark(s.activation(ln["rstd"][:], ln["vep"][:], AF.Ln, bias=0.0, scale=1.0), sSe)
    block.scalar(_s2a)

    def _v2b(v):
        v.wait_ge(sSe, em.n(sSe))
        em.mark(v.tensor_scalar(ln["rstd"][:], ln["rstd"][:], -0.5, None,
                                op0=ALU.mult), sDv)
    block.vector(_v2b)

    def _s2b(s):
        s.wait_ge(sDv, em.n(sDv))
        em.mark(s.activation(ln["rstd"][:], ln["rstd"][:], AF.Exp, bias=0.0, scale=1.0), sSe)
    block.scalar(_s2b)

    def _v3(v):
        v.wait_ge(sSe, em.n(sSe))
        em.mark(v.tensor_scalar(ln["nmr"][:], ln["negmu"][:], ln["rstd"][:, 0:1], None,
                                op0=ALU.mult), sDv)
    block.vector(_v3)

    def _s3(s):
        s.wait_ge(sDv, em.n(sDv))
        em.mark(s.activation(xhat_out, src, AF.Identity, bias=ln["nmr"][:, 0:1],
                             scale=ln["rstd"][:, 0:1]), sSe)
    block.scalar(_s3)
    return em.n(sSe)



def build_launch1():
    nc = bass.Bass("TRN2", target_bir_lowering=False)

    x_in = nc.dram_tensor("x", [TOK, D], F32, kind="ExternalInput").ap()
    wrT = nc.dram_tensor("wrT", [D, D], F32, kind="ExternalInput").ap()
    attr = nc.dram_tensor("attr", [1, 128], F32, kind="ExternalInput").ap()
    ipT = nc.dram_tensor("ipT", [D, 3 * D], F32, kind="ExternalInput").ap()
    ipb = nc.dram_tensor("ipb", [1, 3 * D], F32, kind="ExternalInput").ap()
    opT = nc.dram_tensor("opT", [D, D], F32, kind="ExternalInput").ap()
    opb = nc.dram_tensor("opb", [1, D], F32, kind="ExternalInput").ap()
    g1 = nc.dram_tensor("g1", [1, D], F32, kind="ExternalInput").ap()
    b1r = nc.dram_tensor("b1r", [1, D], F32, kind="ExternalInput").ap()
    wg = nc.dram_tensor("wg", [D, E], F32, kind="ExternalInput").ap()
    w1T = nc.dram_tensor("w1T", [FF, D], F32, kind="ExternalInput").ap()
    w2e = nc.dram_tensor("w2e", [FF, D], F32, kind="ExternalInput").ap()
    b1e = nc.dram_tensor("b1e", [128, FF // 128], F32, kind="ExternalInput").ap()
    b2e = nc.dram_tensor("b2e", [1, D], F32, kind="ExternalInput").ap()
    idf = nc.dram_tensor("idf", [128, 128], F32, kind="ExternalInput").ap()
    idb = nc.dram_tensor("idb", [128, 128], BF16, kind="ExternalInput").ap()
    onesrow_b = nc.dram_tensor("onesrow_b", [1, 128], BF16, kind="ExternalInput").ap()
    onesrow_f = nc.dram_tensor("onesrow_f", [1, 128], F32, kind="ExternalInput").ap()
    onescol_f = nc.dram_tensor("onescol_f", [128, 1], F32, kind="ExternalInput").ap()
    iotaE_in = nc.dram_tensor("iotaE", [128, E], F32, kind="ExternalInput").ap()

    o_w12 = nc.dram_tensor("o_w12", [D, D], BF16, kind="ExternalOutput").ap()
    o_b12 = nc.dram_tensor("o_b12", [1, D], F32, kind="ExternalOutput").ap()
    o_m = nc.dram_tensor("o_m", [HA, L], F32, kind="ExternalOutput").ap()
    o_z = nc.dram_tensor("o_z", [HA, L], F32, kind="ExternalOutput").ap()
    o_s = nc.dram_tensor("o_s", [1, L * D], F32, kind="ExternalOutput").ap()
    o_cnt = nc.dram_tensor("o_cnt", [1, L * E], F32, kind="ExternalOutput").ap()
    o_y1 = nc.dram_tensor("o_y1", [TOK, D], BF16, kind="ExternalOutput").ap()
    o_xn = nc.dram_tensor("o_xn", [TOK, D], BF16, kind="ExternalOutput").ap()
    o_idx = nc.dram_tensor("o_idx", [128, TILES], F32, kind="ExternalOutput").ap()
    o_gate = nc.dram_tensor("o_gate", [128, TILES], F32, kind="ExternalOutput").ap()

    from contextlib import ExitStack
    es = ExitStack()
    P = lambda name, shape, dt: es.enter_context(nc.psum_tensor(name, shape, dt))

    G = Scope(nc)
    xn_sb = G.T("xn_sb", [128, TILES * D], BF16)
    idf_sb = G.T("idf_sb", [128, 128], F32)
    idb_sb = G.T("idb_sb", [128, 128], BF16)
    onesb_sb = G.T("onesb_sb", [1, 128], BF16)
    onesf_sb = G.T("onesf_sb", [1, 128], F32)
    onescol_sb = G.T("onescol_sb", [128, 1], F32)
    iotaE_sb = G.T("iotaE_sb", [128, E], F32)
    attrB_sb = G.T("attrB_sb", [128, 128], BF16)
    wgp_sb = G.T("wgp_sb", [128, 8 * E], F32)
    cvec_sb = G.T("cvec_sb", [1, E], F32)
    g1B_sb = G.T("g1B_sb", [128, D], BF16)
    b1B_sb = G.T("b1B_sb", [128, D], BF16)
    idx_sb = G.T("idx_sb", [128, TILES], F32)
    gate_sb = G.T("gate_sb", [128, TILES], F32)
    cnt_sb = G.T("cnt_sb", [1, L * E], F32)
    bf2k = G.T("bf2k", [128, D], BF16)
    ln = {k: G.T("ln_" + k, [128, 1], F32) for k in
          ("sx", "negmu", "svar", "vep", "rstd", "nmr")}
    ln["dummy"] = G.T("ln_dummy", [128, D], F32)

    psA = P("psA", [128, 512], F32)
    psB = P("psB", [128, 512], F32)
    psT = P("psT", [128, D], BF16)
    psF = P("psF", [128, D], F32)
    psS = P("psS", [1, D], F32)
    psC = P("psC", [1, 512], F32)

    sLd = es.enter_context(nc.semaphore("sLd"))
    sGw2 = es.enter_context(nc.semaphore("sGw2"))
    sGw1 = es.enter_context(nc.semaphore("sGw1"))
    sGrows = es.enter_context(nc.semaphore("sGrows"))
    sGwr = es.enter_context(nc.semaphore("sGwr"))
    sGip = es.enter_context(nc.semaphore("sGip"))
    sGop = es.enter_context(nc.semaphore("sGop"))
    sLdX = [es.enter_context(nc.semaphore(f"sLdX{i}")) for i in range(2)]
    sPe = es.enter_context(nc.semaphore("sPe"))
    sDv = es.enter_context(nc.semaphore("sDv"))
    sSe = es.enter_context(nc.semaphore("sSe"))
    sSt = es.enter_context(nc.semaphore("sSt"))
    block = es.enter_context(nc.Block())
    em = Emit()

    S0 = Scope(nc)
    w2_sb = S0.T("w2_sb", [128, 32 * D], BF16)
    w1m_sb = S0.T("w1m_sb", [128, FF], BF16)
    b1c_sb = S0.T("b1c_sb", [128, FF // 128], BF16)
    w12_sb = S0.T("w12_sb", [128, D], BF16)
    b12_sb = S0.T("b12_sb", [1, D], F32)
    b2_sb = S0.T("b2_sb", [1, D], F32)
    g1f_sb = S0.T("g1f_sb", [1, D], F32)
    b1f_sb = S0.T("b1f_sb", [1, D], F32)
    wg_sb = S0.T("wg_sb", [128, 8 * E], F32)
    col_sb = S0.T("col_sb", [128, 8], F32)

    def _ld(sync):
        for dst, src in ((idf_sb[:], idf), (idb_sb[:], idb), (onesb_sb[:], onesrow_b),
                         (onesf_sb[:], onesrow_f), (onescol_sb[:], onescol_f),
                         (iotaE_sb[:], iotaE_in), (b2_sb[:], b2e),
                         (g1f_sb[:], g1), (b1f_sb[:], b1r)):
            em.mark(sync.dma_start(out=dst, in_=src), sLd, 16)
        for k in range(8):
            em.mark(sync.dma_start(out=wg_sb[:, k * E:(k + 1) * E],
                                   in_=wg[k * 128:(k + 1) * 128, :]), sLd, 16)
    block.sync(_ld)
    LD_PRE = em.n(sLd)

    def _gw2(g):
        for k in range(32):
            em.mark(g.dma_start(out=w2_sb[:, k * D:(k + 1) * D],
                                in_=w2e[k * 128:(k + 1) * 128, :]), sGw2, 16)
        em.mark(g.dma_start(out=b1c_sb[:], in_=b1e), sGw2, 16)
    block.gpsimd(_gw2)
    G_W2 = em.n(sGw2)

    pe_after_M = []
    for M in range(8):
        def _gw1(g, M=M):
            if M >= 1:
                g.wait_ge(sPe, pe_after_M[M - 1])
            em.mark(g.dma_start(
                out=w1m_sb[:].rearrange("p (k i) -> p k i", k=32),
                in_=w1T[:, M * 128:(M + 1) * 128].rearrange("(k p) i -> p k i", p=128)),
                sGw1, 16)
        block.gpsimd(_gw1)
        g_w1m = em.n(sGw1)

        def _tmm(t, M=M, g_w1m=g_w1m):
            t.wait_ge(sGw2, G_W2)
            t.wait_ge(sGw1, g_w1m)
            if M >= 1:
                t.wait_ge(sDv, em.n(sDv))
            for K in range(32):
                t.matmul(psA[:, :], w1m_sb[:, K * 128:(K + 1) * 128],
                         w2_sb[:, K * D:K * D + 512], start=(K == 0), stop=(K == 31))
            for K in range(32):
                i = t.matmul(psB[:, :], w1m_sb[:, K * 128:(K + 1) * 128],
                             w2_sb[:, K * D + 512:(K + 1) * D],
                             start=(K == 0), stop=(K == 31))
                if K == 31 and M != 0:
                    em.mark(i, sPe)
            if M == 0:
                for K in range(32):
                    t.matmul(psS[0:1, 0:512], b1c_sb[:, K:K + 1],
                             w2_sb[:, K * D:K * D + 512],
                             start=(K == 0), stop=(K == 31), skip_group_check=True)
                for K in range(32):
                    i = t.matmul(psS[0:1, 512:1024], b1c_sb[:, K:K + 1],
                                 w2_sb[:, K * D + 512:(K + 1) * D],
                                 start=(K == 0), stop=(K == 31), skip_group_check=True)
                    if K == 31:
                        em.mark(i, sPe)
        block.tensor(_tmm)
        pe_after_M.append(em.n(sPe))

        def _vev(v, M=M):
            v.wait_ge(sPe, pe_after_M[M])
            if M >= 1:
                v.wait_ge(sSt, em.n(sSt))
            em.mark(v.tensor_copy(w12_sb[:, 0:512], psA[:, :]), sDv)
            em.mark(v.tensor_copy(w12_sb[:, 512:1024], psB[:, :]), sDv)
        block.vector(_vev)

        def _st(sync, M=M):
            sync.wait_ge(sDv, em.n(sDv))
            em.mark(sync.dma_start(out=o_w12[M * 128:(M + 1) * 128, :],
                                   in_=w12_sb[:]), sSt, 16)
        block.sync(_st)

    def _b12(v):
        v.wait_ge(sPe, pe_after_M[0])
        v.wait_ge(sLd, LD_PRE)
        em.mark(v.tensor_tensor(b12_sb[0:1, 0:512], psS[0:1, 0:512],
                                b2_sb[0:1, 0:512], op=ALU.add), sDv)
        em.mark(v.tensor_tensor(b12_sb[0:1, 512:1024], psS[0:1, 512:1024],
                                b2_sb[0:1, 512:1024], op=ALU.add), sDv)
    block.vector(_b12)

    def _b12st(sync):
        sync.wait_ge(sDv, em.n(sDv))
        em.mark(sync.dma_start(out=o_b12, in_=b12_sb[:]), sSt, 16)
    block.sync(_b12st)

    def _g1c(t):
        t.wait_ge(sLd, LD_PRE)
        t.wait_ge(sPe, pe_after_M[-1])
        for k in range(8):
            i = t.transpose(psF[:, k:k + 1], g1f_sb[0:1, k * 128:(k + 1) * 128],
                            idf_sb[0:1, 0:1])
        em.mark(i, sPe)
    block.tensor(_g1c)

    def _g1cv(v):
        v.wait_ge(sPe, em.n(sPe))
        n0 = em.mark(v.tensor_copy(col_sb[:], psF[:, 0:8]), sDv)
        v.wait_ge(sDv, n0)
        for k in range(8):
            em.mark(v.tensor_scalar(wgp_sb[:, k * E:(k + 1) * E],
                                    wg_sb[:, k * E:(k + 1) * E],
                                    col_sb[:, k:k + 1], None, op0=ALU.mult), sDv)
    block.vector(_g1cv)

    def _b1c2(t):
        t.wait_ge(sDv, em.n(sDv))
        for k in range(8):
            i = t.transpose(psF[:, 8 + k:9 + k], b1f_sb[0:1, k * 128:(k + 1) * 128],
                            idf_sb[0:1, 0:1])
        em.mark(i, sPe)
    block.tensor(_b1c2)

    def _b1cv(v):
        v.wait_ge(sPe, em.n(sPe))
        em.mark(v.tensor_copy(col_sb[:], psF[:, 8:16]), sDv)
    block.vector(_b1cv)

    def _cvec(t):
        t.wait_ge(sDv, em.n(sDv))
        for k in range(8):
            i = t.matmul(psC[0:1, 0:E], col_sb[:, k:k + 1], wg_sb[:, k * E:(k + 1) * E],
                         start=(k == 0), stop=(k == 7))
        em.mark(i, sPe)
    block.tensor(_cvec)

    def _cvecv(v):
        v.wait_ge(sPe, em.n(sPe))
        em.mark(v.tensor_copy(cvec_sb[:], psC[0:1, 0:E]), sDv)
    block.vector(_cvecv)

    def _gldg(g):
        em.mark(g.dma_start(out=g1B_sb[0:1, :], in_=g1), sGrows, 16)
        em.mark(g.dma_start(out=b1B_sb[0:1, :], in_=b1r), sGrows, 16)
        em.mark(g.dma_start(out=attrB_sb[0:1, :], in_=attr), sGrows, 16)
    block.gpsimd(_gldg)
    G_ROWS = em.n(sGrows)

    for dst in (g1B_sb, b1B_sb):
        def _bct(t, dst=dst):
            t.wait_ge(sGrows, G_ROWS)
            t.wait_ge(sDv, em.n(sDv))
            t.matmul(psF[:, 0:512], onesb_sb[0:1, :], dst[0:1, 0:512],
                     start=True, stop=True)
            em.mark(t.matmul(psF[:, 512:1024], onesb_sb[0:1, :], dst[0:1, 512:1024],
                             start=True, stop=True), sPe)
        block.tensor(_bct)

        def _bcv(v, dst=dst):
            v.wait_ge(sPe, em.n(sPe))
            em.mark(v.tensor_copy(dst[:], psF[:]), sDv)
        block.vector(_bcv)

    def _bca(t):
        t.wait_ge(sGrows, G_ROWS)
        t.wait_ge(sDv, em.n(sDv))
        em.mark(t.matmul(psF[:, 0:128], onesb_sb[0:1, :], attrB_sb[0:1, 0:128],
                         start=True, stop=True), sPe)
    block.tensor(_bca)

    def _bcav(v):
        v.wait_ge(sPe, em.n(sPe))
        em.mark(v.tensor_copy(attrB_sb[:], psF[:, 0:128]), sDv)
    block.vector(_bcav)

    ST_P0 = em.n(sSt)
    DV_P0 = em.n(sDv)
    S0.close()

    SA = Scope(nc)
    wrT_sb = SA.T("wrT_sb", [128, 8 * D], BF16)
    fr_sb = SA.T("fr_sb", [128, TILES * D], BF16)
    srT_sb = SA.T("srT_sb", [HA, TOK], F32)
    mloc_sb = SA.T("mloc_sb", [HA, L], F32)
    negm_sb = SA.T("negm_sb", [HA, L], F32)
    zloc_sb = SA.T("zloc_sb", [HA, L], F32)
    Sst_sb = SA.T("Sst_sb", [1, D], F32)
    xnF_sb = [SA.T(f"xnF{i}", [128, D], BF16) for i in range(2)]
    xbuf = [SA.T(f"xbuf{i}", [128, D], F32) for i in range(2)]
    sr_sb = SA.T("sr_sb", [128, HA], F32)
    e4_bf = SA.T("e4_bf", [128, C4 * HA], BF16)

    barrier(block, em, [(sSt, ST_P0), (sDv, DV_P0)])

    def _gwr(g):
        for k in range(8):
            em.mark(g.dma_start(out=wrT_sb[:, k * D:(k + 1) * D],
                                in_=wrT[k * 128:(k + 1) * 128, :]), sGwr, 16)
    block.gpsimd(_gwr)
    G_WR = em.n(sGwr)

    def _x0(sync):
        em.mark(sync.dma_start(out=xbuf[0][:], in_=x_in[0:128, :]), sLdX[0], 16)
    block.sync(_x0)

    se_xn = []
    x_ld = [em.n(sLdX[0])]
    for t_i in range(TILES):
        buf = t_i % 2

        def _pf(sync, t_i=t_i, buf=buf):
            if t_i + 1 < TILES:
                if t_i >= 1:
                    sync.wait_ge(sSe, se_xn[t_i - 1])
                em.mark(sync.dma_start(out=xbuf[1 - buf][:],
                                       in_=x_in[(t_i + 1) * 128:(t_i + 2) * 128, :]),
                        sLdX[1 - buf], 16)
                x_ld.append(em.n(sLdX[1 - buf]))
        block.sync(_pf)

        def _w(v, t_i=t_i, buf=buf):
            v.wait_ge(sLdX[buf], x_ld[t_i])
        block.vector(_w)

        _ln_chain(block, em, sDv, sSe, xbuf[buf][:], xn_sb[:, t_i * D:(t_i + 1) * D], ln)
        se_xn.append(em.n(sSe))

        def _xnst(sync, t_i=t_i):
            sync.wait_ge(sSe, se_xn[t_i])
            em.mark(sync.dma_start(out=o_xn[t_i * 128:(t_i + 1) * 128, :],
                                   in_=xn_sb[:, t_i * D:(t_i + 1) * D]), sSt, 16)
        block.sync(_xnst)

        def _tr(t, t_i=t_i):
            t.wait_ge(sSe, se_xn[t_i])
            t.wait_ge(sDv, em.n(sDv))
            for k in range(8):
                i = t.transpose(psT[:, k * 128:(k + 1) * 128],
                                xn_sb[:, t_i * D + k * 128:t_i * D + (k + 1) * 128],
                                idb_sb[:])
            em.mark(i, sPe)
        block.tensor(_tr)

        def _trv(v, buf=buf):
            v.wait_ge(sPe, em.n(sPe))
            em.mark(v.tensor_copy(xnF_sb[buf][:], psT[:]), sDv)
        block.vector(_trv)

        def _fr(t, buf=buf):
            t.wait_ge(sGwr, G_WR)
            t.wait_ge(sDv, em.n(sDv))
            for k in range(8):
                t.matmul(psA[:, :], xnF_sb[buf][:, k * 128:(k + 1) * 128],
                         wrT_sb[:, k * D:k * D + 512], start=(k == 0), stop=(k == 7))
            for k in range(8):
                i = t.matmul(psB[:, :], xnF_sb[buf][:, k * 128:(k + 1) * 128],
                             wrT_sb[:, k * D + 512:(k + 1) * D],
                             start=(k == 0), stop=(k == 7))
            em.mark(i, sPe)
        block.tensor(_fr)

        def _frv(v, t_i=t_i):
            v.wait_ge(sPe, em.n(sPe))
            em.mark(v.tensor_copy(fr_sb[:, t_i * D:t_i * D + 512], psA[:]), sDv)
            nn = em.mark(v.tensor_copy(fr_sb[:, t_i * D + 512:(t_i + 1) * D], psB[:]), sDv)
            v.wait_ge(sDv, nn)
            nn = em.mark(v.scalar_tensor_tensor(
                bf2k[:], fr_sb[:, t_i * D:(t_i + 1) * D], 0.01,
                fr_sb[:, t_i * D:(t_i + 1) * D], op0=ALU.mult, op1=ALU.max), sDv)
            v.wait_ge(sDv, nn)
            for h in range(HA):
                em.mark(v.scalar_tensor_tensor(
                    ln["dummy"][:, h * 128:(h + 1) * 128],
                    bf2k[:, h * 128:(h + 1) * 128], 1.0,
                    attrB_sb[:, 0:128], op0=ALU.mult, op1=ALU.mult,
                    accum_out=sr_sb[:, h:h + 1]), sDv)
        block.vector(_frv)

        def _srT(t, t_i=t_i):
            t.wait_ge(sDv, em.n(sDv))
            em.mark(t.transpose(psF[0:HA, 0:128], sr_sb[:], idf_sb[:]), sPe)
        block.tensor(_srT)

        def _srTv(v, t_i=t_i):
            v.wait_ge(sPe, em.n(sPe))
            em.mark(v.tensor_copy(srT_sb[:, t_i * 128:(t_i + 1) * 128],
                                  psF[0:HA, 0:128]), sDv)
        block.vector(_srTv)

    def _p2a(v):
        v.wait_ge(sDv, em.n(sDv))
        n0 = em.mark(v.reduce_max(
            mloc_sb[:], srT_sb[:].rearrange("h (l n) -> h l n", l=L), axis=AX.X), sDv)
        v.wait_ge(sDv, n0)
        em.mark(v.tensor_scalar(negm_sb[:], mloc_sb[:], -1.0, None, op0=ALU.mult), sDv)
    block.vector(_p2a)

    def _p2m(sync):
        sync.wait_ge(sDv, em.n(sDv))
        em.mark(sync.dma_start(out=o_m, in_=mloc_sb[:]), sSt, 16)
    block.sync(_p2m)

    def _p2e(s):
        s.wait_ge(sDv, em.n(sDv))
        for l in range(L):
            em.mark(s.activation(srT_sb[:, l * NLOC:(l + 1) * NLOC],
                                 srT_sb[:, l * NLOC:(l + 1) * NLOC], AF.Exp,
                                 bias=negm_sb[:, l:l + 1], scale=1.0), sSe)
    block.scalar(_p2e)
    SE_ET = em.n(sSe)

    def _p2z(v):
        v.wait_ge(sSe, SE_ET)
        em.mark(v.reduce_sum(
            zloc_sb[:], srT_sb[:].rearrange("h (l n) -> h l n", l=L), axis=AX.X), sDv)
    block.vector(_p2z)

    def _p2zs(sync):
        sync.wait_ge(sDv, em.n(sDv))
        em.mark(sync.dma_start(out=o_z, in_=zloc_sb[:]), sSt, 16)
    block.sync(_p2zs)

    for l in range(L):
        for c4 in range(C4):
            t_i = l * C4 + c4

            def _etr(t, t_i=t_i):
                t.wait_ge(sSe, SE_ET)
                t.wait_ge(sDv, em.n(sDv))
                em.mark(t.transpose(psF[:, 0:HA], srT_sb[0:HA, t_i * 128:(t_i + 1) * 128],
                                    idf_sb[0:HA, 0:HA]), sPe)
            block.tensor(_etr)

            def _etv(v, c4=c4):
                v.wait_ge(sPe, em.n(sPe))
                em.mark(v.tensor_copy(e4_bf[:, c4 * HA:(c4 + 1) * HA], psF[:, 0:HA]), sDv)
            block.vector(_etv)

        def _smm(t, l=l):
            t.wait_ge(sDv, em.n(sDv))
            for h in range(HA):
                for c4 in range(C4):
                    t_i = l * C4 + c4
                    i = t.matmul(psS[0:1, h * 128:(h + 1) * 128],
                                 e4_bf[:, c4 * HA + h:c4 * HA + h + 1],
                                 fr_sb[:, t_i * D + h * 128:t_i * D + (h + 1) * 128],
                                 start=(c4 == 0), stop=(c4 == C4 - 1),
                                 skip_group_check=True)
            em.mark(i, sPe)
        block.tensor(_smm)

        def _sev(v, l=l):
            v.wait_ge(sPe, em.n(sPe))
            if l >= 1:
                v.wait_ge(sSt, em.n(sSt))
            em.mark(v.tensor_copy(Sst_sb[0:1, :], psS[0:1, :]), sDv)
        block.vector(_sev)

        def _sst(sync, l=l):
            sync.wait_ge(sDv, em.n(sDv))
            em.mark(sync.dma_start(out=o_s[0:1, l * D:(l + 1) * D],
                                   in_=Sst_sb[:]), sSt, 16)
        block.sync(_sst)

    ST_ATTN = em.n(sSt)
    DV_ATTN = em.n(sDv)
    PE_ATTN = em.n(sPe)
    SA.close()

    SP = Scope(nc)
    ipT_sb = SP.T("ipT_sb", [128, 8 * 3 * 512], BF16)
    ipb_sb = SP.T("ipb_sb", [1, 3 * 512], BF16)
    opT_sb = SP.T("opT_sb", [128, 8 * D], BF16)
    opb_sb = SP.T("opb_sb", [1, D], BF16)
    qkv_sb = SP.T("qkv_sb", [128, L * 3 * D], BF16)
    xnFall = SP.T("xnFall", [128, L * D], BF16)
    oacc_sb = SP.T("oacc_sb", [128, D], F32)
    oF_sb = SP.T("oF_sb", [128, D], BF16)
    scr_sc = SP.T("scr_sc", [128, HM, 8], F32)
    scre_sc = SP.T("scre_sc", [128, HM, 8], F32)
    mxm_sc = SP.T("mxm_sc", [128, HM], F32)
    negmx_sc = SP.T("negmx_sc", [128, HM], F32)
    zm_sc = SP.T("zm_sc", [128, HM], F32)
    zinv_sc = SP.T("zinv_sc", [128, HM], F32)
    zsum_sb = SP.T("zsum_sb", [128, D], F32)
    xhat_sb = SP.T("xhat_sb", [128, D], F32)
    y1bf_sb = SP.T("y1bf_sb", [128, D], BF16)
    logits_sb = SP.T("logits_sb", [128, E], F32)
    mx8_sb = SP.T("mx8_sb", [128, E], F32)
    idxu_sb = SP.T("idxu_sb", [128, E], U32)
    e8_sb = SP.T("e8_sb", [128, E], F32)
    z8_sb = SP.T("z8_sb", [128, 1], F32)
    oh_sb = SP.T("oh_sb", [128, E], F32)

    barrier(block, em, [(sSt, ST_ATTN), (sDv, DV_ATTN), (sPe, PE_ATTN)])

    def _gop(g):
        for k in range(8):
            em.mark(g.dma_start(out=opT_sb[:, k * D:(k + 1) * D],
                                in_=opT[k * 128:(k + 1) * 128, :]), sGop, 16)
        em.mark(g.dma_start(out=opb_sb[:], in_=opb), sGop, 16)
    block.gpsimd(_gop)
    G_OP = em.n(sGop)

    for c4 in range(C4):
        for l in range(L):
            t_i = l * C4 + c4

            def _qtr(t, t_i=t_i, l=l):
                t.wait_ge(sDv, em.n(sDv))
                for k in range(8):
                    i = t.transpose(psT[:, k * 128:(k + 1) * 128],
                                    xn_sb[:, t_i * D + k * 128:t_i * D + (k + 1) * 128],
                                    idb_sb[:])
                em.mark(i, sPe)
            block.tensor(_qtr)

            def _qtrv(v, l=l):
                v.wait_ge(sPe, em.n(sPe))
                em.mark(v.tensor_copy(xnFall[:, l * D:(l + 1) * D], psT[:]), sDv)
            block.vector(_qtrv)

        for half in range(2):
            def _ghalf(g, half=half, c4=c4):
                if not (c4 == 0 and half == 0):
                    g.wait_ge(sPe, em.n(sPe))
                for k in range(8):
                    em.mark(g.dma_start(
                        out=ipT_sb[:, k * 1536:(k + 1) * 1536],
                        in_=ipT[k * 128:(k + 1) * 128,
                                half * 1536:(half + 1) * 1536]), sGip, 16)
                em.mark(g.dma_start(out=ipb_sb[:],
                                    in_=ipb[0:1, half * 1536:(half + 1) * 1536]), sGip, 16)
            block.gpsimd(_ghalf)
            g_half = em.n(sGip)

            for l in range(L):
                for ns in range(3):
                    nseg = half * 3 + ns
                    tgt = psA if ns % 2 == 0 else psB

                    def _qmm(t, l=l, ns=ns, nseg=nseg, tgt=tgt, g_half=g_half):
                        t.wait_ge(sGip, g_half)
                        t.wait_ge(sDv, em.n(sDv))
                        for k in range(8):
                            t.matmul(tgt[:, :], xnFall[:, l * D + k * 128:
                                                       l * D + (k + 1) * 128],
                                     ipT_sb[:, k * 1536 + ns * 512:
                                            k * 1536 + (ns + 1) * 512],
                                     start=(k == 0), stop=False)
                        em.mark(t.matmul(tgt[:, :], onesb_sb[0:1, 0:128],
                                         ipb_sb[0:1, ns * 512:(ns + 1) * 512],
                                         start=False, stop=True), sPe)
                    block.tensor(_qmm)

                    def _qev(v, l=l, nseg=nseg, tgt=tgt):
                        v.wait_ge(sPe, em.n(sPe))
                        em.mark(v.tensor_copy(
                            qkv_sb[:, l * 3 * D + nseg * 512:
                                   l * 3 * D + (nseg + 1) * 512], tgt[:, :]), sDv)
                    block.vector(_qev)

        for l in range(L):
            t_i = l * C4 + c4

            def _sc(v, l=l):
                v.wait_ge(sDv, em.n(sDv))
                scm = []
                for m in range(L):
                    for h in range(HM):
                        i = m * HM + h
                        if i >= 4:
                            v.wait_ge(sDv, scm[i - 4])
                        scm.append(em.mark(v.scalar_tensor_tensor(
                            ln["dummy"][:, (i % 4) * 256:(i % 4 + 1) * 256],
                            qkv_sb[:, l * 3 * D + h * 256:l * 3 * D + (h + 1) * 256],
                            0.0625,
                            qkv_sb[:, m * 3 * D + D + h * 256:
                                   m * 3 * D + D + (h + 1) * 256],
                            op0=ALU.mult, op1=ALU.mult,
                            accum_out=scr_sc[:, h, m:m + 1]), sDv))
                nn = em.n(sDv)
                v.wait_ge(sDv, nn)
                n0 = em.mark(v.reduce_max(mxm_sc[:], scr_sc[:], axis=AX.X), sDv)
                v.wait_ge(sDv, n0)
                em.mark(v.tensor_scalar(negmx_sc[:], mxm_sc[:], -1.0, None,
                                        op0=ALU.mult), sDv)
            block.vector(_sc)

            def _sce(s):
                s.wait_ge(sDv, em.n(sDv))
                for h in range(HM):
                    em.mark(s.activation(scre_sc[:, h, :], scr_sc[:, h, :], AF.Exp,
                                         bias=negmx_sc[:, h:h + 1], scale=1.0), sSe)
            block.scalar(_sce)

            def _so(v, l=l):
                v.wait_ge(sSe, em.n(sSe))
                n0 = em.mark(v.reduce_sum(zm_sc[:], scre_sc[:], axis=AX.X), sDv)
                v.wait_ge(sDv, n0)
                n1 = em.mark(v.reciprocal(zinv_sc[:], zm_sc[:]), sDv)
                v.wait_ge(sDv, n1)
                for h in range(HM):
                    em.mark(v.tensor_scalar(scre_sc[:, h, :], scre_sc[:, h, :],
                                            zinv_sc[:, h:h + 1], None, op0=ALU.mult), sDv)
                nn = em.n(sDv)
                v.wait_ge(sDv, nn)
                marks = {}
                for m in range(L):
                    for h in range(HM):
                        oseg = oacc_sb[:, h * 256:(h + 1) * 256]
                        vseg = qkv_sb[:, m * 3 * D + 2 * D + h * 256:
                                      m * 3 * D + 2 * D + (h + 1) * 256]
                        if m > 0:
                            v.wait_ge(sDv, marks[(m - 1, h)])
                        if m == 0:
                            marks[(m, h)] = em.mark(v.tensor_scalar(
                                oseg, vseg, scre_sc[:, h, m:m + 1], None,
                                op0=ALU.mult), sDv)
                        else:
                            marks[(m, h)] = em.mark(v.scalar_tensor_tensor(
                                oseg, vseg, scre_sc[:, h, m:m + 1], oseg,
                                op0=ALU.mult, op1=ALU.add), sDv)
                nn = em.n(sDv)
                v.wait_ge(sDv, nn)
                em.mark(v.tensor_copy(bf2k[:], oacc_sb[:]), sDv)
            block.vector(_so)

            def _otr(t):
                t.wait_ge(sDv, em.n(sDv))
                for k in range(8):
                    i = t.transpose(psT[:, k * 128:(k + 1) * 128],
                                    bf2k[:, k * 128:(k + 1) * 128], idb_sb[:])
                em.mark(i, sPe)
            block.tensor(_otr)

            def _otrv(v):
                v.wait_ge(sPe, em.n(sPe))
                em.mark(v.tensor_copy(oF_sb[:], psT[:]), sDv)
            block.vector(_otrv)

            def _omm(t):
                t.wait_ge(sGop, G_OP)
                t.wait_ge(sDv, em.n(sDv))
                for k in range(8):
                    t.matmul(psA[:, :], oF_sb[:, k * 128:(k + 1) * 128],
                             opT_sb[:, k * D:k * D + 512], start=(k == 0), stop=False)
                t.matmul(psA[:, :], onesb_sb[0:1, 0:128], opb_sb[0:1, 0:512],
                         start=False, stop=True)
                for k in range(8):
                    t.matmul(psB[:, :], oF_sb[:, k * 128:(k + 1) * 128],
                             opT_sb[:, k * D + 512:(k + 1) * D],
                             start=(k == 0), stop=False)
                em.mark(t.matmul(psB[:, :], onesb_sb[0:1, 0:128], opb_sb[0:1, 512:1024],
                                 start=False, stop=True), sPe)
            block.tensor(_omm)

            def _zs(v, t_i=t_i):
                v.wait_ge(sPe, em.n(sPe))
                em.mark(v.tensor_tensor(zsum_sb[:, 0:512],
                                        xn_sb[:, t_i * D:t_i * D + 512], psA[:, :],
                                        op=ALU.add), sDv)
                em.mark(v.tensor_tensor(zsum_sb[:, 512:1024],
                                        xn_sb[:, t_i * D + 512:(t_i + 1) * D], psB[:, :],
                                        op=ALU.add), sDv)
            block.vector(_zs)

            _ln_chain(block, em, sDv, sSe, zsum_sb[:], xhat_sb[:], ln)

            def _y1(v, t_i=t_i):
                v.wait_ge(sSe, em.n(sSe))
                v.wait_ge(sSt, em.n(sSt))
                n0 = em.mark(v.tensor_tensor(y1bf_sb[:], xhat_sb[:], g1B_sb[:],
                                             op=ALU.mult), sDv)
                v.wait_ge(sDv, n0)
                em.mark(v.tensor_tensor(y1bf_sb[:], y1bf_sb[:], b1B_sb[:],
                                        op=ALU.add), sDv)
            block.vector(_y1)

            def _y1st(sync, t_i=t_i):
                sync.wait_ge(sDv, em.n(sDv))
                em.mark(sync.dma_start(out=o_y1[t_i * 128:(t_i + 1) * 128, :],
                                       in_=y1bf_sb[:]), sSt, 16)
            block.sync(_y1st)

            def _xtr(t):
                t.wait_ge(sSe, em.n(sSe))
                t.wait_ge(sDv, em.n(sDv))
                for k in range(8):
                    i = t.transpose(psF[:, k * 128:(k + 1) * 128],
                                    xhat_sb[:, k * 128:(k + 1) * 128], idf_sb[:])
                em.mark(i, sPe)
            block.tensor(_xtr)

            def _xtrv(v):
                v.wait_ge(sPe, em.n(sPe))
                em.mark(v.tensor_copy(ln["dummy"][:], psF[:]), sDv)
            block.vector(_xtrv)

            def _lg(t):
                t.wait_ge(sDv, em.n(sDv))
                for k in range(8):
                    t.matmul(psF[0:128, 0:E], ln["dummy"][:, k * 128:(k + 1) * 128],
                             wgp_sb[:, k * E:(k + 1) * E],
                             start=(k == 0), stop=False, skip_group_check=True)
                em.mark(t.matmul(psF[0:128, 0:E], onesf_sb[0:1, 0:128], cvec_sb[0:1, :],
                                 start=False, stop=True, skip_group_check=True), sPe)
            block.tensor(_lg)

            def _am(v):
                v.wait_ge(sPe, em.n(sPe))
                n0 = em.mark(v.tensor_copy(logits_sb[:], psF[0:128, 0:E]), sDv)
                v.wait_ge(sDv, n0)
                n1 = em.mark(v.max(mx8_sb[:], logits_sb[:]), sDv)
                v.wait_ge(sDv, n1)
                n2 = em.mark(v.max_index(idxu_sb[:], mx8_sb[:], logits_sb[:]), sDv)
                v.wait_ge(sDv, n2)
                em.mark(v.tensor_scalar(ln["negmu"][:], mx8_sb[:, 0:1], -1.0, None,
                                        op0=ALU.mult), sDv)
            block.vector(_am)

            def _cv(s, t_i=t_i):
                s.wait_ge(sDv, em.n(sDv))
                em.mark(s.activation(idx_sb[:, t_i:t_i + 1], idxu_sb[:, 0:1],
                                     AF.Identity, bias=0.0, scale=1.0), sSe)
                em.mark(s.activation(e8_sb[:], logits_sb[:], AF.Exp,
                                     bias=ln["negmu"][:, 0:1], scale=1.0), sSe)
            block.scalar(_cv)

            def _gt(v, t_i=t_i):
                v.wait_ge(sSe, em.n(sSe))
                n0 = em.mark(v.reduce_sum(z8_sb[:], e8_sb[:], axis=AX.X), sDv)
                v.wait_ge(sDv, n0)
                n1 = em.mark(v.reciprocal(gate_sb[:, t_i:t_i + 1], z8_sb[:]), sDv)
                v.wait_ge(sDv, n1)
                em.mark(v.tensor_scalar(oh_sb[:], iotaE_sb[:], idx_sb[:, t_i:t_i + 1],
                                        0.0, op0=ALU.subtract, op1=ALU.is_equal), sDv)
            block.vector(_gt)

            def _cnt(t, t_i=t_i):
                t.wait_ge(sDv, em.n(sDv))
                em.mark(t.matmul(psC[0:1, t_i * E:(t_i + 1) * E], onescol_sb[:],
                                 oh_sb[:], start=True, stop=True,
                                 skip_group_check=True), sPe)
            block.tensor(_cnt)

    def _cfin(v):
        v.wait_ge(sPe, em.n(sPe))
        for l in range(L):
            v.wait_ge(sDv, em.n(sDv))
            em.mark(v.tensor_copy(cnt_sb[0:1, l * E:(l + 1) * E],
                                  psC[0:1, (l * C4) * E:(l * C4) * E + E]), sDv)
            for c4 in range(1, C4):
                v.wait_ge(sDv, em.n(sDv))
                em.mark(v.tensor_tensor(
                    cnt_sb[0:1, l * E:(l + 1) * E],
                    cnt_sb[0:1, l * E:(l + 1) * E],
                    psC[0:1, (l * C4 + c4) * E:(l * C4 + c4) * E + E],
                    op=ALU.add), sDv)
    block.vector(_cfin)

    def _fin(sync):
        sync.wait_ge(sDv, em.n(sDv))
        em.mark(sync.dma_start(out=o_cnt, in_=cnt_sb[:]), sSt, 16)
        em.mark(sync.dma_start(out=o_idx, in_=idx_sb[:]), sSt, 16)
        em.mark(sync.dma_start(out=o_gate, in_=gate_sb[:]), sSt, 16)
        sync.wait_ge(sSt, em.n(sSt))
    block.sync(_fin)

    SP.close()
    G.close()
    es.close()
    return nc



def build_launch2():
    nc = bass.Bass("TRN2", target_bir_lowering=False)

    x_in = nc.dram_tensor("x", [TOK, D], F32, kind="ExternalInput").ap()
    xn_in = nc.dram_tensor("xn", [TOK, D], BF16, kind="ExternalInput").ap()
    y1_in = nc.dram_tensor("y1", [TOK, D], BF16, kind="ExternalInput").ap()
    idx_in = nc.dram_tensor("idxv", [128, TILES], F32, kind="ExternalInput").ap()
    gate_in = nc.dram_tensor("gatev", [128, TILES], F32, kind="ExternalInput").ap()
    mz_in = nc.dram_tensor("mzall", [CORES, 2 * HA * L], F32, kind="ExternalInput").ap()
    s_in = nc.dram_tensor("sall", [CORES, L * D], F32, kind="ExternalInput").ap()
    cnt_in = nc.dram_tensor("cntall", [CORES * L, E], F32, kind="ExternalInput").ap()
    wcm_in = nc.dram_tensor("wcm", [CORES * L, L], F32, kind="ExternalInput").ap()
    w12_in = nc.dram_tensor("w12all", [E * D, D], BF16, kind="ExternalInput").ap()
    b12_in = nc.dram_tensor("b12all", [E, D], F32, kind="ExternalInput").ap()
    wfT_in = nc.dram_tensor("wfT", [D, D], F32, kind="ExternalInput").ap()
    lng_in = nc.dram_tensor("lng", [1, D], F32, kind="ExternalInput").ap()
    lnb_in = nc.dram_tensor("lnb", [1, D], F32, kind="ExternalInput").ap()
    g2_in = nc.dram_tensor("g2r", [1, D], F32, kind="ExternalInput").ap()
    b2r_in = nc.dram_tensor("b2r", [1, D], F32, kind="ExternalInput").ap()
    idf = nc.dram_tensor("idf", [128, 128], F32, kind="ExternalInput").ap()
    idb = nc.dram_tensor("idb", [128, 128], BF16, kind="ExternalInput").ap()
    onesrow_b = nc.dram_tensor("onesrow_b", [1, 128], BF16, kind="ExternalInput").ap()
    onesrow_f = nc.dram_tensor("onesrow_f", [1, 128], F32, kind="ExternalInput").ap()
    onescol_f = nc.dram_tensor("onescol_f", [128, 1], F32, kind="ExternalInput").ap()
    ltri_in = nc.dram_tensor("ltri", [128, 128], F32, kind="ExternalInput").ap()
    iotaE_in = nc.dram_tensor("iotaE", [128, E], F32, kind="ExternalInput").ap()

    o_out = nc.dram_tensor("o_out", [TOK, D], F32, kind="ExternalOutput").ap()

    staging = nc.dram_tensor("staging", [E * CPC, D], BF16)
    outstage = nc.dram_tensor("outstage", [E * CPC, D], BF16)

    from contextlib import ExitStack
    es = ExitStack()
    P = lambda name, shape, dt: es.enter_context(nc.psum_tensor(name, shape, dt))

    G = Scope(nc)
    xn_sb = G.T("xn_sb", [128, TILES * D], BF16)
    y1_sb = G.T("y1_sb", [128, TILES * D], BF16)
    fhB_sb = G.T("fhB_sb", [128, L * D], BF16)
    lngB = G.T("lngB", [128, D], BF16)
    lnbB = G.T("lnbB", [128, D], BF16)
    g2B = G.T("g2B", [128, D], BF16)
    b2B = G.T("b2B", [128, D], BF16)
    idf_sb = G.T("idf_sb", [128, 128], F32)
    idb_sb = G.T("idb_sb", [128, 128], BF16)
    onesb_sb = G.T("onesb_sb", [1, 128], BF16)
    onesf_sb = G.T("onesf_sb", [1, 128], F32)
    onescol_sb = G.T("onescol_sb", [128, 1], F32)
    ltri_sb = G.T("ltri_sb", [128, 128], F32)
    iotaE_sb = G.T("iotaE_sb", [128, E], F32)
    idx_sb = G.T("idx_sb", [128, TILES], F32)
    gate_sb = G.T("gate_sb", [128, TILES], F32)
    slotu_sb = G.T("slotu_sb", [128, TILES], U32)
    cs_sb = G.T("cs_sb", [1, TILES * E], F32)
    runl_sb = G.T("runl_sb", [1, TILES * E], F32)
    runa_sb = G.T("runa_sb", [1, TILES * E], F32)
    zero_sb = G.T("zero_sb", [128, D], BF16)
    ln = {k: G.T("ln_" + k, [128, 1], F32) for k in
          ("sx", "negmu", "svar", "vep", "rstd", "nmr")}
    ln["dummy"] = G.T("ln_dummy", [128, D], F32)
    smalls = {k: G.T("sm_" + k, [128, 1], F32) for k in
              ("loc", "lpos", "keep", "gv", "slot", "ovf")}
    oh_sb = G.T("oh_sb", [128, E], F32)
    ohrun_sb = G.T("ohrun_sb", [128, E], F32)

    psA = P("psA", [128, 512], F32)
    psB = P("psB", [128, 512], F32)
    psT = P("psT", [128, D], BF16)
    psF = P("psF", [128, D], F32)
    psS = P("psS", [1, D], F32)
    psC = P("psC", [L, 512], F32)

    sLd = es.enter_context(nc.semaphore("sLd"))
    sG = es.enter_context(nc.semaphore("sG"))
    sGz = es.enter_context(nc.semaphore("sGz"))
    sGsc = es.enter_context(nc.semaphore("sGsc"))
    sGb = es.enter_context(nc.semaphore("sGb"))
    sGg = es.enter_context(nc.semaphore("sGg"))
    sLdsl = es.enter_context(nc.semaphore("sLdsl"))
    sLdfh = es.enter_context(nc.semaphore("sLdfh"))
    sLdb = es.enter_context(nc.semaphore("sLdb"))
    sLdw = es.enter_context(nc.semaphore("sLdw"))
    sLdst = es.enter_context(nc.semaphore("sLdst"))
    sLdX = [es.enter_context(nc.semaphore(f"sLdX{i}")) for i in range(2)]
    sPe = es.enter_context(nc.semaphore("sPe"))
    sDv = es.enter_context(nc.semaphore("sDv"))
    sSe = es.enter_context(nc.semaphore("sSe"))
    sSt = es.enter_context(nc.semaphore("sSt"))
    block = es.enter_context(nc.Block())
    em = Emit()

    SA = Scope(nc)
    wbig_sb = SA.T("wbig_sb", [128, 8 * D], BF16)
    mz_sb = SA.T("mz_sb", [CORES, 2 * HA * L], F32)
    mzT_sb = SA.T("mzT_sb", [HA * L, CORES], F32)
    zT0_sb = SA.T("zT0_sb", [HA * L, CORES], F32)
    mg_sb = SA.T("mg_sb", [HA * L, 1], F32)
    negmg_sb = SA.T("negmg_sb", [HA * L, 1], F32)
    wT_sb = SA.T("wT_sb", [HA * L, CORES], F32)
    zt_sb = SA.T("zt_sb", [HA * L, CORES], F32)
    zg_sb = SA.T("zg_sb", [HA * L, 1], F32)
    zgi_sb = SA.T("zgi_sb", [HA * L, 1], F32)
    w8_sb = SA.T("w8_sb", [CORES, HA * L], F32)
    s_l = SA.T("s_l", [CORES, D], F32)
    ctx_l = SA.T("ctx_l", [1, D], F32)
    ctxF_sb = SA.T("ctxF_sb", [128, 8 * L], BF16)
    fh_sb = SA.T("fh_sb", [L, D], F32)
    fhrow_sb = SA.T("fhrow_sb", [1, D], F32)
    cnt2_sb = SA.T("cnt2_sb", [CORES * L, E], F32)
    wcm_sb = SA.T("wcm_sb", [CORES * L, L], F32)
    base_sb = SA.T("base_sb", [L, E], F32)
    baser_sb = SA.T("baser_sb", [1, L * E], F32)
    stage_sb = SA.T("stage_sb", [128, D], BF16)
    stF_sb = SA.T("stF_sb", [128, D], BF16)
    eo_sb = SA.T("eo_sb", [128, D], BF16)
    b12e_sb = SA.T("b12e_sb", [1, D], BF16)

    def _ld(sync):
        for t_i in range(TILES):
            em.mark(sync.dma_start(out=xn_sb[:, t_i * D:(t_i + 1) * D],
                                   in_=xn_in[t_i * 128:(t_i + 1) * 128, :]), sLd, 16)
            em.mark(sync.dma_start(out=y1_sb[:, t_i * D:(t_i + 1) * D],
                                   in_=y1_in[t_i * 128:(t_i + 1) * 128, :]), sLd, 16)
        for dst, src in ((idf_sb[:], idf), (idb_sb[:], idb), (onesb_sb[:], onesrow_b),
                         (onesf_sb[:], onesrow_f), (onescol_sb[:], onescol_f),
                         (ltri_sb[:], ltri_in), (iotaE_sb[:], iotaE_in),
                         (idx_sb[:], idx_in), (gate_sb[:], gate_in),
                         (mz_sb[:], mz_in), (wcm_sb[:], wcm_in), (cnt2_sb[:], cnt_in)):
            em.mark(sync.dma_start(out=dst, in_=src), sLd, 16)
    block.sync(_ld)
    LD0 = em.n(sLd)

    def _gld(g):
        for k in range(8):
            em.mark(g.dma_start(out=wbig_sb[:, k * D:(k + 1) * D],
                                in_=wfT_in[k * 128:(k + 1) * 128, :]), sG, 16)
        for dst, src in ((lngB, lng_in), (lnbB, lnb_in), (g2B, g2_in), (b2B, b2r_in)):
            em.mark(g.dma_start(out=dst[0:1, :], in_=src), sG, 16)
    block.gpsimd(_gld)
    G0 = em.n(sG)

    def _z(v):
        em.mark(v.memset(zero_sb[:], 0.0), sDv)
    block.vector(_z)

    bc_cell = []

    def _zs(g):
        bc_cell.append(g.to_reg(E * CPC - 1))
        g.wait_ge(sDv, em.n(sDv))
        for r in range(E * CPC // 128):
            em.mark(g.dma_start(out=staging.ap()[r * 128:(r + 1) * 128, :],
                                in_=zero_sb[:]), sGz, 16)
    block.gpsimd(_zs)
    G_ZERO = em.n(sGz)

    for dst in (lngB, lnbB, g2B, b2B):
        def _bct(t, dst=dst):
            t.wait_ge(sG, G0)
            t.wait_ge(sLd, LD0)
            t.wait_ge(sDv, em.n(sDv))
            t.matmul(psF[:, 0:512], onesb_sb[0:1, :], dst[0:1, 0:512],
                     start=True, stop=True)
            em.mark(t.matmul(psF[:, 512:1024], onesb_sb[0:1, :], dst[0:1, 512:1024],
                             start=True, stop=True), sPe)
        block.tensor(_bct)

        def _bcv(v, dst=dst):
            v.wait_ge(sPe, em.n(sPe))
            em.mark(v.tensor_copy(dst[:], psF[:]), sDv)
        block.vector(_bcv)

    def _t1(t):
        t.wait_ge(sDv, em.n(sDv))
        em.mark(t.transpose(psF[0:128, 0:CORES], mz_sb[:], idf_sb[0:CORES, 0:CORES]), sPe)
    block.tensor(_t1)

    def _v1(v):
        v.wait_ge(sPe, em.n(sPe))
        em.mark(v.tensor_copy(mzT_sb[:], psF[0:HA * L, 0:CORES]), sDv)
        n0 = em.mark(v.tensor_copy(zT0_sb[:], psF[HA * L:2 * HA * L, 0:CORES]), sDv)
        v.wait_ge(sDv, n0)
        n1 = em.mark(v.reduce_max(mg_sb[:], mzT_sb[0:HA * L, :], axis=AX.X), sDv)
        v.wait_ge(sDv, n1)
        em.mark(v.tensor_scalar(negmg_sb[:], mg_sb[:], -1.0, None, op0=ALU.mult), sDv)
    block.vector(_v1)

    def _s1(s):
        s.wait_ge(sDv, em.n(sDv))
        em.mark(s.activation(wT_sb[:], mzT_sb[0:HA * L, :], AF.Exp,
                             bias=negmg_sb[:, 0:1], scale=1.0), sSe)
    block.scalar(_s1)

    def _v2(v):
        v.wait_ge(sSe, em.n(sSe))
        n0 = em.mark(v.tensor_tensor(zt_sb[:], wT_sb[:], zT0_sb[:], op=ALU.mult), sDv)
        v.wait_ge(sDv, n0)
        n1 = em.mark(v.reduce_sum(zg_sb[:], zt_sb[:], axis=AX.X), sDv)
        v.wait_ge(sDv, n1)
        n2 = em.mark(v.reciprocal(zgi_sb[:], zg_sb[:]), sDv)
        v.wait_ge(sDv, n2)
        em.mark(v.tensor_scalar(wT_sb[:], wT_sb[:], zgi_sb[:, 0:1], None,
                                op0=ALU.mult), sDv)
    block.vector(_v2)

    def _t2(t):
        t.wait_ge(sDv, em.n(sDv))
        em.mark(t.transpose(psF[0:CORES, 0:HA * L], wT_sb[:],
                            idf_sb[0:HA * L, 0:HA * L]), sPe)
    block.tensor(_t2)

    def _v3(v):
        v.wait_ge(sPe, em.n(sPe))
        em.mark(v.tensor_copy(w8_sb[:], psF[0:CORES, 0:HA * L]), sDv)
    block.vector(_v3)

    for l in range(L):
        def _sll(sync, l=l):
            if l >= 1:
                sync.wait_ge(sPe, em.n(sPe))
            em.mark(sync.dma_start(out=s_l[:], in_=s_in[:, l * D:(l + 1) * D]), sLdsl, 16)
        block.sync(_sll)
        ld_sl = em.n(sLdsl)

        def _ctx(t, l=l, ld_sl=ld_sl):
            t.wait_ge(sLdsl, ld_sl)
            t.wait_ge(sDv, em.n(sDv))
            for h in range(HA):
                i = t.matmul(psS[0:1, h * 128:(h + 1) * 128],
                             w8_sb[:, l * HA + h:l * HA + h + 1],
                             s_l[:, h * 128:(h + 1) * 128],
                             start=True, stop=True, skip_group_check=True)
            em.mark(i, sPe)
        block.tensor(_ctx)

        def _ctxv(v, l=l):
            v.wait_ge(sPe, em.n(sPe))
            em.mark(v.tensor_copy(ctx_l[0:1, :], psS[0:1, :]), sDv)
        block.vector(_ctxv)

        for k in range(8):
            def _cf(t, l=l, k=k):
                t.wait_ge(sDv, em.n(sDv))
                em.mark(t.transpose(psF[:, 0:1],
                                    ctx_l[0:1, k * 128:(k + 1) * 128],
                                    idf_sb[0:1, 0:1]), sPe)
            block.tensor(_cf)

            def _cfv(v, l=l, k=k):
                v.wait_ge(sPe, em.n(sPe))
                em.mark(v.tensor_copy(ctxF_sb[:, k * L + l:k * L + l + 1],
                                      psF[:, 0:1]), sDv)
            block.vector(_cfv)

    def _fh(t):
        t.wait_ge(sG, G0)
        t.wait_ge(sDv, em.n(sDv))
        for k in range(8):
            t.matmul(psA[0:L, :], ctxF_sb[:, k * L:(k + 1) * L],
                     wbig_sb[:, k * D:k * D + 512], start=(k == 0), stop=(k == 7))
        for k in range(8):
            i = t.matmul(psB[0:L, :], ctxF_sb[:, k * L:(k + 1) * L],
                         wbig_sb[:, k * D + 512:(k + 1) * D],
                         start=(k == 0), stop=(k == 7))
        em.mark(i, sPe)
    block.tensor(_fh)
    PE_FH = em.n(sPe)

    def _fhv(v):
        v.wait_ge(sPe, PE_FH)
        em.mark(v.tensor_copy(fh_sb[:, 0:512], psA[0:L, :]), sDv)
        em.mark(v.tensor_copy(fh_sb[:, 512:1024], psB[0:L, :]), sDv)
    block.vector(_fhv)
    DV_FH = em.n(sDv)

    for l in range(L):
        def _fhr(sync, l=l):
            sync.wait_ge(sDv, DV_FH)
            if l >= 1:
                sync.wait_ge(sPe, em.n(sPe))
            em.mark(sync.dma_start(out=fhrow_sb[0:1, :], in_=fh_sb[l:l + 1, :]), sLdfh, 16)
        block.sync(_fhr)
        ld_f = em.n(sLdfh)

        def _fb(t, l=l, ld_f=ld_f):
            t.wait_ge(sLdfh, ld_f)
            t.wait_ge(sDv, em.n(sDv))
            t.matmul(psA[:, :], onesf_sb[0:1, :], fhrow_sb[0:1, 0:512],
                     start=True, stop=True)
            em.mark(t.matmul(psB[:, :], onesf_sb[0:1, :], fhrow_sb[0:1, 512:1024],
                             start=True, stop=True), sPe)
        block.tensor(_fb)

        def _fbv(v, l=l):
            v.wait_ge(sPe, em.n(sPe))
            em.mark(v.tensor_copy(fhB_sb[:, l * D:l * D + 512], psA[:]), sDv)
            em.mark(v.tensor_copy(fhB_sb[:, l * D + 512:(l + 1) * D], psB[:]), sDv)
        block.vector(_fbv)

    def _base(t):
        t.wait_ge(sDv, em.n(sDv))
        em.mark(t.matmul(psC[0:L, 0:E], wcm_sb[:], cnt2_sb[:],
                         start=True, stop=True), sPe)
    block.tensor(_base)

    def _basev(v):
        v.wait_ge(sPe, em.n(sPe))
        em.mark(v.tensor_copy(base_sb[:], psC[0:L, 0:E]), sDv)
    block.vector(_basev)

    def _baser(sync):
        sync.wait_ge(sDv, em.n(sDv))
        em.mark(sync.dma_start(out=baser_sb[:], in_=base_sb[:]), sLdb, 16)
    block.sync(_baser)
    LD_BASE = em.n(sLdb)

    for t_i in range(TILES):
        def _oh(v, t_i=t_i):
            v.wait_ge(sPe, em.n(sPe))
            v.wait_ge(sDv, em.n(sDv))
            em.mark(v.tensor_scalar(oh_sb[:], iotaE_sb[:], idx_sb[:, t_i:t_i + 1],
                                    0.0, op0=ALU.subtract, op1=ALU.is_equal), sDv)
        block.vector(_oh)

        def _cs(t, t_i=t_i):
            t.wait_ge(sDv, em.n(sDv))
            em.mark(t.matmul(psC[0:1, t_i * E:(t_i + 1) * E], onescol_sb[:], oh_sb[:],
                             start=True, stop=True, skip_group_check=True), sPe)
        block.tensor(_cs)

    def _csv(v):
        v.wait_ge(sPe, em.n(sPe))
        em.mark(v.tensor_copy(cs_sb[0:1, :], psC[0:1, 0:TILES * E]), sDv)
    block.vector(_csv)

    def _pre(v):
        v.wait_ge(sDv, em.n(sDv))
        v.wait_ge(sLdb, LD_BASE)
        for t_i in range(TILES):
            l, c4 = divmod(t_i, C4)
            if c4 == 0:
                em.mark(v.tensor_copy(runl_sb[0:1, t_i * E:(t_i + 1) * E],
                                      baser_sb[0:1, l * E:(l + 1) * E]), sDv)
            else:
                v.wait_ge(sDv, em.n(sDv))
                em.mark(v.tensor_tensor(runl_sb[0:1, t_i * E:(t_i + 1) * E],
                                        runl_sb[0:1, (t_i - 1) * E:t_i * E],
                                        cs_sb[0:1, (t_i - 1) * E:t_i * E],
                                        op=ALU.add), sDv)
            if t_i == 0:
                em.mark(v.memset(runa_sb[0:1, 0:E], 0.0), sDv)
            else:
                v.wait_ge(sDv, em.n(sDv))
                em.mark(v.tensor_tensor(runa_sb[0:1, t_i * E:(t_i + 1) * E],
                                        runa_sb[0:1, (t_i - 1) * E:t_i * E],
                                        cs_sb[0:1, (t_i - 1) * E:t_i * E],
                                        op=ALU.add), sDv)
    block.vector(_pre)

    for t_i in range(TILES):
        def _oh2(v, t_i=t_i):
            v.wait_ge(sDv, em.n(sDv))
            v.wait_ge(sPe, em.n(sPe))
            em.mark(v.tensor_scalar(oh_sb[:], iotaE_sb[:], idx_sb[:, t_i:t_i + 1],
                                    0.0, op0=ALU.subtract, op1=ALU.is_equal), sDv)
        block.vector(_oh2)

        def _lt(t, t_i=t_i):
            t.wait_ge(sDv, em.n(sDv))
            t.matmul(psF[0:128, 0:E], ltri_sb[:], oh_sb[:],
                     start=True, stop=False, skip_group_check=True)
            t.matmul(psF[0:128, 0:E], onesf_sb[0:1, 0:128],
                     runl_sb[0:1, t_i * E:(t_i + 1) * E],
                     start=False, stop=True, skip_group_check=True)
            t.matmul(psF[0:128, E:2 * E], ltri_sb[:], oh_sb[:],
                     start=True, stop=False, skip_group_check=True)
            em.mark(t.matmul(psF[0:128, E:2 * E], onesf_sb[0:1, 0:128],
                             runa_sb[0:1, t_i * E:(t_i + 1) * E],
                             start=False, stop=True, skip_group_check=True), sPe)
        block.tensor(_lt)

        def _sel(v, t_i=t_i):
            v.wait_ge(sPe, em.n(sPe))
            v.wait_ge(sSe, em.n(sSe))
            n0 = em.mark(v.tensor_tensor(ohrun_sb[:], psF[0:128, 0:E], oh_sb[:],
                                         op=ALU.mult), sDv)
            v.wait_ge(sDv, n0)
            n1 = em.mark(v.reduce_sum(smalls["loc"][:], ohrun_sb[:], axis=AX.X), sDv)
            v.wait_ge(sDv, n1)
            n2 = em.mark(v.tensor_tensor(ohrun_sb[:], psF[0:128, E:2 * E], oh_sb[:],
                                         op=ALU.mult), sDv)
            v.wait_ge(sDv, n2)
            n3 = em.mark(v.reduce_sum(smalls["lpos"][:], ohrun_sb[:], axis=AX.X), sDv)
            v.wait_ge(sDv, n3)
            n4 = em.mark(v.tensor_scalar(smalls["keep"][:], smalls["loc"][:], float(CAP),
                                         None, op0=ALU.is_lt), sDv)
            v.wait_ge(sDv, n4)
            em.mark(v.tensor_tensor(smalls["gv"][:], smalls["keep"][:],
                                    gate_sb[:, t_i:t_i + 1], op=ALU.mult), sDv)
            n6 = em.mark(v.tensor_scalar(smalls["slot"][:], idx_sb[:, t_i:t_i + 1],
                                         float(CPC), smalls["lpos"][:, 0:1],
                                         op0=ALU.mult, op1=ALU.add), sDv)
            n7 = em.mark(v.tensor_scalar(smalls["ovf"][:], smalls["lpos"][:], float(CPC),
                                         None, op0=ALU.is_ge), sDv)
            v.wait_ge(sDv, n7)
            n8 = em.mark(v.scalar_tensor_tensor(smalls["slot"][:], smalls["ovf"][:], OOB,
                                                smalls["slot"][:], op0=ALU.mult,
                                                op1=ALU.add), sDv)
            v.wait_ge(sDv, n8)
            em.mark(v.tensor_copy(gate_sb[:, t_i:t_i + 1], smalls["gv"][:]), sDv)
        block.vector(_sel)

        def _cvt(s, t_i=t_i):
            s.wait_ge(sDv, em.n(sDv))
            em.mark(s.activation(slotu_sb[:, t_i:t_i + 1], smalls["slot"][:], AF.Identity,
                                 bias=0.0, scale=1.0), sSe)
        block.scalar(_cvt)

        def _scat(g, t_i=t_i):
            g.wait_ge(sSe, em.n(sSe))
            g.wait_ge(sGz, G_ZERO)
            em.mark(g.indirect_dma_start(
                out=staging.ap(),
                out_offset=IndirectOffsetOnAxis(slotu_sb[:, t_i:t_i + 1], axis=0),
                in_=y1_sb[:, t_i * D:(t_i + 1) * D],
                in_offset=None,
                bounds_check=bc_cell[0],
                oob_is_err=False), sGsc, 16)
        block.gpsimd(_scat)

    G_SCAT = em.n(sGsc)

    for e in range(E):
        def _wld(sync, e=e):
            if e == 0:
                sync.wait_ge(sGsc, G_SCAT)
                sync.wait_ge(sPe, PE_FH)
            else:
                sync.wait_ge(sPe, em.n(sPe))
            for k in range(8):
                em.mark(sync.dma_start(
                    out=wbig_sb[:, k * D:(k + 1) * D],
                    in_=w12_in[e * D + k * 128:e * D + (k + 1) * 128, :]), sLdw, 16)
        block.sync(_wld)
        ld_w = em.n(sLdw)

        def _bld(g, e=e):
            if e == 0:
                g.wait_ge(sGsc, G_SCAT)
            else:
                g.wait_ge(sPe, em.n(sPe))
            em.mark(g.dma_start(out=b12e_sb[0:1, :], in_=b12_in[e:e + 1, :]), sGb, 16)
        block.gpsimd(_bld)
        g_b = em.n(sGb)

        for r in range(CPC // 128):
            def _sld(sync, e=e, r=r):
                sync.wait_ge(sPe, em.n(sPe))
                em.mark(sync.dma_start(
                    out=stage_sb[:],
                    in_=staging.ap()[e * CPC + r * 128:e * CPC + (r + 1) * 128, :]),
                    sLdst, 16)
            block.sync(_sld)
            ld_s = em.n(sLdst)

            def _str(t, ld_s=ld_s):
                t.wait_ge(sLdst, ld_s)
                t.wait_ge(sDv, em.n(sDv))
                for k in range(8):
                    i = t.transpose(psT[:, k * 128:(k + 1) * 128],
                                    stage_sb[:, k * 128:(k + 1) * 128], idb_sb[:])
                em.mark(i, sPe)
            block.tensor(_str)

            def _stv(v):
                v.wait_ge(sPe, em.n(sPe))
                em.mark(v.tensor_copy(stF_sb[:], psT[:]), sDv)
            block.vector(_stv)

            def _emm(t, e=e, ld_w=ld_w, g_b=g_b):
                t.wait_ge(sLdw, ld_w)
                t.wait_ge(sGb, g_b)
                t.wait_ge(sDv, em.n(sDv))
                for k in range(8):
                    t.matmul(psA[:, :], stF_sb[:, k * 128:(k + 1) * 128],
                             wbig_sb[:, k * D:k * D + 512], start=(k == 0), stop=False)
                t.matmul(psA[:, :], onesb_sb[0:1, 0:128], b12e_sb[0:1, 0:512],
                         start=False, stop=True)
                for k in range(8):
                    t.matmul(psB[:, :], stF_sb[:, k * 128:(k + 1) * 128],
                             wbig_sb[:, k * D + 512:(k + 1) * D],
                             start=(k == 0), stop=False)
                em.mark(t.matmul(psB[:, :], onesb_sb[0:1, 0:128], b12e_sb[0:1, 512:1024],
                                 start=False, stop=True), sPe)
            block.tensor(_emm)

            def _eev(v):
                v.wait_ge(sPe, em.n(sPe))
                v.wait_ge(sSt, em.n(sSt))
                em.mark(v.tensor_copy(eo_sb[:, 0:512], psA[:]), sDv)
                em.mark(v.tensor_copy(eo_sb[:, 512:1024], psB[:]), sDv)
            block.vector(_eev)

            def _est(sync, e=e, r=r):
                sync.wait_ge(sDv, em.n(sDv))
                em.mark(sync.dma_start(
                    out=outstage.ap()[e * CPC + r * 128:e * CPC + (r + 1) * 128, :],
                    in_=eo_sb[:]), sSt, 16)
            block.sync(_est)

    ST_EXP = em.n(sSt)
    DV_EXP = em.n(sDv)
    SA.close()

    SC = Scope(nc)
    xbuf = [SC.T(f"xbuf{i}", [128, D], F32) for i in range(2)]
    gz_sb = SC.T("gz_sb", [128, D], BF16)
    moe_sb = SC.T("moe_sb", [128, D], F32)
    z2_sb = SC.T("z2_sb", [128, D], F32)
    xh2_sb = SC.T("xh2_sb", [128, D], F32)
    y2_sb = SC.T("y2_sb", [128, D], F32)
    aout_sb = SC.T("aout_sb", [128, D], F32)
    out_sb = SC.T("out_sb", [128, D], F32)

    barrier(block, em, [(sSt, ST_EXP), (sDv, DV_EXP)])

    for t_i in range(TILES):
        l = t_i // C4
        buf = t_i % 2

        def _xl(sync, t_i=t_i, buf=buf):
            if t_i >= 2:
                sync.wait_ge(sDv, em.n(sDv))
            em.mark(sync.dma_start(out=xbuf[buf][:],
                                   in_=x_in[t_i * 128:(t_i + 1) * 128, :]), sLdX[buf], 16)
        block.sync(_xl)
        ldx = em.n(sLdX[buf])

        def _gzz(v, t_i=t_i):
            v.wait_ge(sGg, em.n(sGg))
            v.wait_ge(sDv, em.n(sDv))
            em.mark(v.memset(gz_sb[:], 0.0), sDv)
        block.vector(_gzz)

        def _gth(g, t_i=t_i):
            g.wait_ge(sSt, ST_EXP)
            g.wait_ge(sDv, em.n(sDv))
            em.mark(g.indirect_dma_start(
                out=gz_sb[:],
                out_offset=None,
                in_=outstage.ap(),
                in_offset=IndirectOffsetOnAxis(slotu_sb[:, t_i:t_i + 1], axis=0),
                bounds_check=bc_cell[0],
                oob_is_err=False), sGg, 16)
        block.gpsimd(_gth)
        gn = em.n(sGg)

        def _moe(v, t_i=t_i, gn=gn):
            v.wait_ge(sGg, gn)
            n0 = em.mark(v.tensor_scalar(moe_sb[:], gz_sb[:], gate_sb[:, t_i:t_i + 1],
                                         None, op0=ALU.mult), sDv)
            v.wait_ge(sDv, n0)
            em.mark(v.tensor_tensor(z2_sb[:], y1_sb[:, t_i * D:(t_i + 1) * D],
                                    moe_sb[:], op=ALU.add), sDv)
        block.vector(_moe)

        _ln_chain(block, em, sDv, sSe, z2_sb[:], xh2_sb[:], ln)

        def _y2(v):
            v.wait_ge(sSe, em.n(sSe))
            n0 = em.mark(v.tensor_tensor(y2_sb[:], xh2_sb[:], g2B[:], op=ALU.mult), sDv)
            v.wait_ge(sDv, n0)
            em.mark(v.tensor_tensor(y2_sb[:], y2_sb[:], b2B[:], op=ALU.add), sDv)
        block.vector(_y2)

        def _ya(v, t_i=t_i, l=l):
            v.wait_ge(sDv, em.n(sDv))
            em.mark(v.tensor_tensor(z2_sb[:], xn_sb[:, t_i * D:(t_i + 1) * D],
                                    fhB_sb[:, l * D:(l + 1) * D], op=ALU.add), sDv)
        block.vector(_ya)

        _ln_chain(block, em, sDv, sSe, z2_sb[:], xh2_sb[:], ln)

        def _ao(v):
            v.wait_ge(sSe, em.n(sSe))
            n0 = em.mark(v.tensor_tensor(aout_sb[:], xh2_sb[:], lngB[:], op=ALU.mult), sDv)
            v.wait_ge(sDv, n0)
            em.mark(v.tensor_tensor(aout_sb[:], aout_sb[:], lnbB[:], op=ALU.add), sDv)
        block.vector(_ao)

        def _sum(v, buf=buf, ldx=ldx):
            v.wait_ge(sLdX[buf], ldx)
            v.wait_ge(sDv, em.n(sDv))
            v.wait_ge(sSt, em.n(sSt))
            n0 = em.mark(v.tensor_tensor(out_sb[:], xbuf[buf][:], aout_sb[:],
                                         op=ALU.add), sDv)
            v.wait_ge(sDv, n0)
            em.mark(v.tensor_tensor(out_sb[:], out_sb[:], y2_sb[:], op=ALU.add), sDv)
        block.vector(_sum)

        def _ost(sync, t_i=t_i):
            sync.wait_ge(sDv, em.n(sDv))
            em.mark(sync.dma_start(out=o_out[t_i * 128:(t_i + 1) * 128, :],
                                   in_=out_sb[:]), sSt, 16)
        block.sync(_ost)

    def _end(sync):
        sync.wait_ge(sSt, em.n(sSt))
    block.sync(_end)

    SC.close()
    G.close()
    es.close()
    return nc



def _consts():
    idf = np.eye(128, dtype=np.float32)
    idb = np.eye(128, dtype=BF16NP)
    onesrow_b = np.ones((1, 128), dtype=BF16NP)
    onesrow_f = np.ones((1, 128), dtype=np.float32)
    onescol_f = np.ones((128, 1), dtype=np.float32)
    iotaE = np.tile(np.arange(E, dtype=np.float32), (128, 1))
    ltri = np.triu(np.ones((128, 128), dtype=np.float32), 1)
    return idf, idb, onesrow_b, onesrow_f, onescol_f, iotaE, ltri


def kernel(L0_attn_h, L0_attn_Wl, L0_attn_Wr, L0_attn_att_l, L0_attn_att_r,
           L0_attn_Wf, L0_attn_ln_gamma, L0_attn_ln_beta, L0_ffn_in_proj_w,
           L0_ffn_in_proj_b, L0_ffn_out_proj_w, L0_ffn_out_proj_b, L0_ffn_ln1_g,
           L0_ffn_ln1_b, L0_ffn_ln2_g, L0_ffn_ln2_b, L0_ffn_wg, L0_ffn_w1,
           L0_ffn_b1, L0_ffn_w2, L0_ffn_b2):
    f32 = np.float32
    x = np.asarray(L0_attn_h, f32)
    idf, idb, onesrow_b, onesrow_f, onescol_f, iotaE, ltri = _consts()

    wrT = np.ascontiguousarray(np.asarray(L0_attn_Wr, f32).T)
    attr = np.ascontiguousarray(np.asarray(L0_attn_att_r, f32).reshape(1, 128))
    ipT = np.ascontiguousarray(np.asarray(L0_ffn_in_proj_w, f32).T)
    ipb = np.asarray(L0_ffn_in_proj_b, f32).reshape(1, 3 * D)
    opT = np.ascontiguousarray(np.asarray(L0_ffn_out_proj_w, f32).T)
    opb = np.asarray(L0_ffn_out_proj_b, f32).reshape(1, D)
    g1 = np.asarray(L0_ffn_ln1_g, f32).reshape(1, D)
    b1r = np.asarray(L0_ffn_ln1_b, f32).reshape(1, D)
    wgA = np.ascontiguousarray(np.asarray(L0_ffn_wg, f32))
    w1 = np.asarray(L0_ffn_w1, f32)
    b1 = np.asarray(L0_ffn_b1, f32)
    w2 = np.asarray(L0_ffn_w2, f32)
    b2 = np.asarray(L0_ffn_b2, f32)
    wfT = np.ascontiguousarray(np.asarray(L0_attn_Wf, f32).T)
    lng = np.asarray(L0_attn_ln_gamma, f32).reshape(1, D)
    lnb = np.asarray(L0_attn_ln_beta, f32).reshape(1, D)
    g2r = np.asarray(L0_ffn_ln2_g, f32).reshape(1, D)
    b2r = np.asarray(L0_ffn_ln2_b, f32).reshape(1, D)

    xs = [np.ascontiguousarray(x[:, c * NLOC:(c + 1) * NLOC, :].reshape(TOK, D))
          for c in range(CORES)]

    nc1 = build_launch1()
    in_maps1 = []
    for c in range(CORES):
        in_maps1.append({
            "x": xs[c], "wrT": wrT, "attr": attr, "ipT": ipT, "ipb": ipb,
            "opT": opT, "opb": opb, "g1": g1, "b1r": b1r, "wg": wgA,
            "w1T": np.ascontiguousarray(w1[c].T),
            "w2e": np.ascontiguousarray(w2[c]),
            "b1e": np.ascontiguousarray(b1[c].reshape(FF // 128, 128).T),
            "b2e": b2[c].reshape(1, D),
            "idf": idf, "idb": idb, "onesrow_b": onesrow_b,
            "onesrow_f": onesrow_f, "onescol_f": onescol_f, "iotaE": iotaE,
        })
    res1 = run_bass_kernel_spmd(nc1, in_maps1, core_ids=list(range(CORES)))
    r1 = res1.results

    mz_all = np.stack([
        np.concatenate([np.asarray(r1[c]["o_m"], f32).T.reshape(-1),
                        np.asarray(r1[c]["o_z"], f32).T.reshape(-1)])
        for c in range(CORES)], axis=0).astype(f32)
    s_all = np.ascontiguousarray(
        np.stack([np.asarray(r1[c]["o_s"], f32).reshape(-1) for c in range(CORES)]))
    cnt_all = np.ascontiguousarray(
        np.stack([np.asarray(r1[c]["o_cnt"], f32).reshape(L, E)
                  for c in range(CORES)]).reshape(CORES * L, E))
    w12_all = np.ascontiguousarray(
        np.concatenate([np.asarray(r1[c]["o_w12"]) for c in range(CORES)], axis=0))
    b12_all = np.ascontiguousarray(
        np.stack([np.asarray(r1[c]["o_b12"], f32).reshape(-1) for c in range(CORES)]))

    nc2 = build_launch2()
    in_maps2 = []
    for c in range(CORES):
        wcm = np.zeros((CORES * L, L), dtype=f32)
        for cp in range(CORES):
            for lp in range(L):
                for l in range(L):
                    if lp < l or (lp == l and cp < c):
                        wcm[cp * L + lp, l] = 1.0
        in_maps2.append({
            "x": xs[c],
            "xn": np.asarray(r1[c]["o_xn"]),
            "y1": np.asarray(r1[c]["o_y1"]),
            "idxv": np.asarray(r1[c]["o_idx"], f32),
            "gatev": np.asarray(r1[c]["o_gate"], f32),
            "mzall": mz_all, "sall": s_all, "cntall": cnt_all, "wcm": wcm,
            "w12all": w12_all, "b12all": b12_all,
            "wfT": wfT, "lng": lng, "lnb": lnb, "g2r": g2r, "b2r": b2r,
            "idf": idf, "idb": idb, "onesrow_b": onesrow_b,
            "onesrow_f": onesrow_f, "onescol_f": onescol_f,
            "ltri": ltri, "iotaE": iotaE,
        })
    res2 = run_bass_kernel_spmd(nc2, in_maps2, core_ids=list(range(CORES)))

    out = np.empty((L, NSEQ, D), dtype=f32)
    for c in range(CORES):
        oc = np.asarray(res2.results[c]["o_out"], f32).reshape(L, NLOC, D)
        out[:, c * NLOC:(c + 1) * NLOC, :] = oc
    return out


# version 06
# speedup vs baseline: 1.0485x; 1.0485x over previous
import numpy as np
import ml_dtypes
import concourse.bass as bass
import concourse.mybir as mybir
from concourse.bass import IndirectOffsetOnAxis
from concourse.bass_utils import run_bass_kernel_spmd

F32 = mybir.dt.float32
BF16 = mybir.dt.bfloat16
U32 = mybir.dt.uint32
AF = mybir.ActivationFunctionType
ALU = mybir.AluOpType
AX = mybir.AxisListType
BF16NP = ml_dtypes.bfloat16

CORES = 8
D = 1024
L = 8
NSEQ = 4096
NLOC = NSEQ // CORES
TOK = L * NLOC
TILES = TOK // 128
C4 = 4
HA = 8
HM = 4
E = 8
FF = 4096
CAP = 4096
CPC = 1024
EPS = 1e-5
OOB = float(1 << 24)


class Emit:
    def __init__(self):
        self.cnt = {}

    def mark(self, instr, sem, k=1):
        instr.then_inc(sem, k)
        self.cnt[sem.name] = self.cnt.get(sem.name, 0) + k
        return self.cnt[sem.name]

    def n(self, sem):
        return self.cnt.get(sem.name, 0)


class Scope:

    def __init__(self, nc):
        self.nc = nc
        self.cms = []

    def T(self, name, shape, dt):
        cm = self.nc.sbuf_tensor(name, shape, dt)
        t = cm.__enter__()
        self.cms.append(cm)
        return t

    def close(self):
        for cm in reversed(self.cms):
            cm.__exit__(None, None, None)
        self.cms = []


def barrier(block, em, sems_waits):
    def _b(e):
        for sem, val in sems_waits:
            e.wait_ge(sem, val)
    block.vector(_b)
    block.scalar(_b)
    block.tensor(_b)
    block.sync(_b)
    block.gpsimd(_b)


def _ln_chain(block, em, sDv, sSe, src, xhat_out, ln):
    def _v1(v):
        v.wait_ge(sDv, em.n(sDv))
        n0 = em.mark(v.reduce_sum(ln["sx"][:], src, axis=AX.X), sDv)
        v.wait_ge(sDv, n0)
        em.mark(v.tensor_scalar(ln["negmu"][:], ln["sx"][:], -1.0 / D, None, op0=ALU.mult), sDv)
    block.vector(_v1)

    def _s1(s):
        s.wait_ge(sDv, em.n(sDv))
        em.mark(s.activation(ln["dummy"][:], src, AF.Square, bias=ln["negmu"][:, 0:1],
                             scale=1.0, accum_out=ln["svar"][:]), sSe)
    block.scalar(_s1)

    def _v2(v):
        v.wait_ge(sSe, em.n(sSe))
        em.mark(v.tensor_scalar(ln["vep"][:], ln["svar"][:], 1.0 / D, EPS,
                                op0=ALU.mult, op1=ALU.add), sDv)
    block.vector(_v2)

    def _s2a(s):
        s.wait_ge(sDv, em.n(sDv))
        em.mark(s.activation(ln["rstd"][:], ln["vep"][:], AF.Ln, bias=0.0, scale=1.0), sSe)
    block.scalar(_s2a)

    def _v2b(v):
        v.wait_ge(sSe, em.n(sSe))
        em.mark(v.tensor_scalar(ln["rstd"][:], ln["rstd"][:], -0.5, None,
                                op0=ALU.mult), sDv)
    block.vector(_v2b)

    def _s2b(s):
        s.wait_ge(sDv, em.n(sDv))
        em.mark(s.activation(ln["rstd"][:], ln["rstd"][:], AF.Exp, bias=0.0, scale=1.0), sSe)
    block.scalar(_s2b)

    def _v3(v):
        v.wait_ge(sSe, em.n(sSe))
        em.mark(v.tensor_scalar(ln["nmr"][:], ln["negmu"][:], ln["rstd"][:, 0:1], None,
                                op0=ALU.mult), sDv)
    block.vector(_v3)

    def _s3(s):
        s.wait_ge(sDv, em.n(sDv))
        em.mark(s.activation(xhat_out, src, AF.Identity, bias=ln["nmr"][:, 0:1],
                             scale=ln["rstd"][:, 0:1]), sSe)
    block.scalar(_s3)
    return em.n(sSe)


def _ln_pipe(block, em, sDv, sSe, src, xhat_out, ln, pre_dv=None, pre_se=None):
    def _v1(v):
        if pre_dv is not None:
            v.wait_ge(sDv, pre_dv)
        if pre_se is not None:
            v.wait_ge(sSe, pre_se)
        em.mark(v.reduce_sum(ln["sx"][:], src, axis=AX.X), sDv)
    block.vector(_v1)
    n_red = em.n(sDv)

    def _s(s):
        s.wait_ge(sDv, n_red)
        n1 = em.mark(s.activation(ln["negmu"][:], ln["sx"][:], AF.Identity,
                                  bias=0.0, scale=-1.0 / D), sSe)
        s.wait_ge(sSe, n1)
        n2 = em.mark(s.activation(ln["dummy"][:], src, AF.Square,
                                  bias=ln["negmu"][:, 0:1], scale=1.0,
                                  accum_out=ln["svar"][:]), sSe)
        s.wait_ge(sSe, n2)
        n3 = em.mark(s.activation(ln["vep"][:], ln["svar"][:], AF.Ln,
                                  bias=EPS, scale=1.0 / D), sSe)
        s.wait_ge(sSe, n3)
        n4 = em.mark(s.activation(ln["rstd"][:], ln["vep"][:], AF.Exp,
                                  bias=0.0, scale=-0.5), sSe)
        s.wait_ge(sSe, n4)
        n5 = em.mark(s.activation(ln["nmr"][:], ln["negmu"][:], AF.Identity,
                                  bias=0.0, scale=ln["rstd"][:, 0:1]), sSe)
        s.wait_ge(sSe, n5)
        em.mark(s.activation(xhat_out, src, AF.Identity,
                             bias=ln["nmr"][:, 0:1], scale=ln["rstd"][:, 0:1]), sSe)
    block.scalar(_s)
    return em.n(sSe)



def build_launch1():
    nc = bass.Bass("TRN2", target_bir_lowering=False)

    x_in = nc.dram_tensor("x", [TOK, D], F32, kind="ExternalInput").ap()
    wrT = nc.dram_tensor("wrT", [D, D], F32, kind="ExternalInput").ap()
    attr = nc.dram_tensor("attr", [1, 128], F32, kind="ExternalInput").ap()
    ipT = nc.dram_tensor("ipT", [D, 3 * D], F32, kind="ExternalInput").ap()
    ipb = nc.dram_tensor("ipb", [1, 3 * D], F32, kind="ExternalInput").ap()
    opT = nc.dram_tensor("opT", [D, D], F32, kind="ExternalInput").ap()
    opb = nc.dram_tensor("opb", [1, D], F32, kind="ExternalInput").ap()
    g1 = nc.dram_tensor("g1", [1, D], F32, kind="ExternalInput").ap()
    b1r = nc.dram_tensor("b1r", [1, D], F32, kind="ExternalInput").ap()
    wg = nc.dram_tensor("wg", [D, E], F32, kind="ExternalInput").ap()
    w1T = nc.dram_tensor("w1T", [FF, D], F32, kind="ExternalInput").ap()
    w2e = nc.dram_tensor("w2e", [FF, D], F32, kind="ExternalInput").ap()
    b1e = nc.dram_tensor("b1e", [128, FF // 128], F32, kind="ExternalInput").ap()
    b2e = nc.dram_tensor("b2e", [1, D], F32, kind="ExternalInput").ap()
    idf = nc.dram_tensor("idf", [128, 128], F32, kind="ExternalInput").ap()
    idb = nc.dram_tensor("idb", [128, 128], BF16, kind="ExternalInput").ap()
    onesrow_b = nc.dram_tensor("onesrow_b", [1, 128], BF16, kind="ExternalInput").ap()
    onesrow_f = nc.dram_tensor("onesrow_f", [1, 128], F32, kind="ExternalInput").ap()
    onescol_f = nc.dram_tensor("onescol_f", [128, 1], F32, kind="ExternalInput").ap()
    iotaE_in = nc.dram_tensor("iotaE", [128, E], F32, kind="ExternalInput").ap()

    o_w12 = nc.dram_tensor("o_w12", [D, D], BF16, kind="ExternalOutput").ap()
    o_b12 = nc.dram_tensor("o_b12", [1, D], F32, kind="ExternalOutput").ap()
    o_m = nc.dram_tensor("o_m", [HA, L], F32, kind="ExternalOutput").ap()
    o_z = nc.dram_tensor("o_z", [HA, L], F32, kind="ExternalOutput").ap()
    o_s = nc.dram_tensor("o_s", [1, L * D], F32, kind="ExternalOutput").ap()
    o_cnt = nc.dram_tensor("o_cnt", [1, L * E], F32, kind="ExternalOutput").ap()
    o_y1 = nc.dram_tensor("o_y1", [TOK, D], BF16, kind="ExternalOutput").ap()
    o_xn = nc.dram_tensor("o_xn", [TOK, D], BF16, kind="ExternalOutput").ap()
    o_idx = nc.dram_tensor("o_idx", [128, TILES], F32, kind="ExternalOutput").ap()
    o_gate = nc.dram_tensor("o_gate", [128, TILES], F32, kind="ExternalOutput").ap()

    from contextlib import ExitStack
    es = ExitStack()
    P = lambda name, shape, dt: es.enter_context(nc.psum_tensor(name, shape, dt))

    G = Scope(nc)
    xn_sb = G.T("xn_sb", [128, TILES * D], BF16)
    idf_sb = G.T("idf_sb", [128, 128], F32)
    idb_sb = G.T("idb_sb", [128, 128], BF16)
    onesb_sb = G.T("onesb_sb", [1, 128], BF16)
    onesf_sb = G.T("onesf_sb", [1, 128], F32)
    onescol_sb = G.T("onescol_sb", [128, 1], F32)
    iotaE_sb = G.T("iotaE_sb", [128, E], F32)
    attrB_sb = G.T("attrB_sb", [128, 128], BF16)
    wgp_sb = G.T("wgp_sb", [128, 8 * E], F32)
    cvec_sb = G.T("cvec_sb", [1, E], F32)
    g1B_sb = G.T("g1B_sb", [128, D], BF16)
    b1B_sb = G.T("b1B_sb", [128, D], BF16)
    idx_sb = G.T("idx_sb", [128, TILES], F32)
    gate_sb = G.T("gate_sb", [128, TILES], F32)
    cnt_sb = G.T("cnt_sb", [1, L * E], F32)
    bf2k = G.T("bf2k", [128, D], BF16)
    ln = {k: G.T("ln_" + k, [128, 1], F32) for k in
          ("sx", "negmu", "svar", "vep", "rstd", "nmr")}
    ln["dummy"] = G.T("ln_dummy", [128, D], F32)

    psA = P("psA", [128, 512], F32)
    psB = P("psB", [128, 512], F32)
    psT = P("psT", [128, D], BF16)
    psF = P("psF", [128, D], F32)
    psS = P("psS", [1, D], F32)
    psC = P("psC", [1, 512], F32)

    sLd = es.enter_context(nc.semaphore("sLd"))
    sGw2 = es.enter_context(nc.semaphore("sGw2"))
    sGw1 = es.enter_context(nc.semaphore("sGw1"))
    sGrows = es.enter_context(nc.semaphore("sGrows"))
    sGwr = es.enter_context(nc.semaphore("sGwr"))
    sGip = es.enter_context(nc.semaphore("sGip"))
    sGop = es.enter_context(nc.semaphore("sGop"))
    sLdX = [es.enter_context(nc.semaphore(f"sLdX{i}")) for i in range(2)]
    sPe = es.enter_context(nc.semaphore("sPe"))
    sDv = es.enter_context(nc.semaphore("sDv"))
    sSe = es.enter_context(nc.semaphore("sSe"))
    sSt = es.enter_context(nc.semaphore("sSt"))
    block = es.enter_context(nc.Block())
    em = Emit()

    S0 = Scope(nc)
    w2_sb = S0.T("w2_sb", [128, 32 * D], BF16)
    w1m_sb = S0.T("w1m_sb", [128, FF], BF16)
    b1c_sb = S0.T("b1c_sb", [128, FF // 128], BF16)
    w12_sb = S0.T("w12_sb", [128, D], BF16)
    b12_sb = S0.T("b12_sb", [1, D], F32)
    b2_sb = S0.T("b2_sb", [1, D], F32)
    g1f_sb = S0.T("g1f_sb", [1, D], F32)
    b1f_sb = S0.T("b1f_sb", [1, D], F32)
    wg_sb = S0.T("wg_sb", [128, 8 * E], F32)
    col_sb = S0.T("col_sb", [128, 8], F32)

    def _ld(sync):
        for dst, src in ((idf_sb[:], idf), (idb_sb[:], idb), (onesb_sb[:], onesrow_b),
                         (onesf_sb[:], onesrow_f), (onescol_sb[:], onescol_f),
                         (iotaE_sb[:], iotaE_in), (b2_sb[:], b2e),
                         (g1f_sb[:], g1), (b1f_sb[:], b1r)):
            em.mark(sync.dma_start(out=dst, in_=src), sLd, 16)
        for k in range(8):
            em.mark(sync.dma_start(out=wg_sb[:, k * E:(k + 1) * E],
                                   in_=wg[k * 128:(k + 1) * 128, :]), sLd, 16)
    block.sync(_ld)
    LD_PRE = em.n(sLd)

    def _gw2(g):
        for k in range(32):
            em.mark(g.dma_start(out=w2_sb[:, k * D:(k + 1) * D],
                                in_=w2e[k * 128:(k + 1) * 128, :]), sGw2, 16)
        em.mark(g.dma_start(out=b1c_sb[:], in_=b1e), sGw2, 16)
    block.gpsimd(_gw2)
    G_W2 = em.n(sGw2)

    pe_after_M = []
    for M in range(8):
        def _gw1(g, M=M):
            if M >= 1:
                g.wait_ge(sPe, pe_after_M[M - 1])
            em.mark(g.dma_start(
                out=w1m_sb[:].rearrange("p (k i) -> p k i", k=32),
                in_=w1T[:, M * 128:(M + 1) * 128].rearrange("(k p) i -> p k i", p=128)),
                sGw1, 16)
        block.gpsimd(_gw1)
        g_w1m = em.n(sGw1)

        def _tmm(t, M=M, g_w1m=g_w1m):
            t.wait_ge(sGw2, G_W2)
            t.wait_ge(sGw1, g_w1m)
            if M >= 1:
                t.wait_ge(sDv, em.n(sDv))
            for K in range(32):
                t.matmul(psA[:, :], w1m_sb[:, K * 128:(K + 1) * 128],
                         w2_sb[:, K * D:K * D + 512], start=(K == 0), stop=(K == 31))
            for K in range(32):
                i = t.matmul(psB[:, :], w1m_sb[:, K * 128:(K + 1) * 128],
                             w2_sb[:, K * D + 512:(K + 1) * D],
                             start=(K == 0), stop=(K == 31))
                if K == 31 and M != 0:
                    em.mark(i, sPe)
            if M == 0:
                for K in range(32):
                    t.matmul(psS[0:1, 0:512], b1c_sb[:, K:K + 1],
                             w2_sb[:, K * D:K * D + 512],
                             start=(K == 0), stop=(K == 31), skip_group_check=True)
                for K in range(32):
                    i = t.matmul(psS[0:1, 512:1024], b1c_sb[:, K:K + 1],
                                 w2_sb[:, K * D + 512:(K + 1) * D],
                                 start=(K == 0), stop=(K == 31), skip_group_check=True)
                    if K == 31:
                        em.mark(i, sPe)
        block.tensor(_tmm)
        pe_after_M.append(em.n(sPe))

        def _vev(v, M=M):
            v.wait_ge(sPe, pe_after_M[M])
            if M >= 1:
                v.wait_ge(sSt, em.n(sSt))
            em.mark(v.tensor_copy(w12_sb[:, 0:512], psA[:, :]), sDv)
            em.mark(v.tensor_copy(w12_sb[:, 512:1024], psB[:, :]), sDv)
        block.vector(_vev)

        def _st(sync, M=M):
            sync.wait_ge(sDv, em.n(sDv))
            em.mark(sync.dma_start(out=o_w12[M * 128:(M + 1) * 128, :],
                                   in_=w12_sb[:]), sSt, 16)
        block.sync(_st)

    def _b12(v):
        v.wait_ge(sPe, pe_after_M[0])
        v.wait_ge(sLd, LD_PRE)
        em.mark(v.tensor_tensor(b12_sb[0:1, 0:512], psS[0:1, 0:512],
                                b2_sb[0:1, 0:512], op=ALU.add), sDv)
        em.mark(v.tensor_tensor(b12_sb[0:1, 512:1024], psS[0:1, 512:1024],
                                b2_sb[0:1, 512:1024], op=ALU.add), sDv)
    block.vector(_b12)

    def _b12st(sync):
        sync.wait_ge(sDv, em.n(sDv))
        em.mark(sync.dma_start(out=o_b12, in_=b12_sb[:]), sSt, 16)
    block.sync(_b12st)

    def _g1c(t):
        t.wait_ge(sLd, LD_PRE)
        t.wait_ge(sPe, pe_after_M[-1])
        for k in range(8):
            i = t.transpose(psF[:, k:k + 1], g1f_sb[0:1, k * 128:(k + 1) * 128],
                            idf_sb[0:1, 0:1])
        em.mark(i, sPe)
    block.tensor(_g1c)

    def _g1cv(v):
        v.wait_ge(sPe, em.n(sPe))
        n0 = em.mark(v.tensor_copy(col_sb[:], psF[:, 0:8]), sDv)
        v.wait_ge(sDv, n0)
        for k in range(8):
            em.mark(v.tensor_scalar(wgp_sb[:, k * E:(k + 1) * E],
                                    wg_sb[:, k * E:(k + 1) * E],
                                    col_sb[:, k:k + 1], None, op0=ALU.mult), sDv)
    block.vector(_g1cv)

    def _b1c2(t):
        t.wait_ge(sDv, em.n(sDv))
        for k in range(8):
            i = t.transpose(psF[:, 8 + k:9 + k], b1f_sb[0:1, k * 128:(k + 1) * 128],
                            idf_sb[0:1, 0:1])
        em.mark(i, sPe)
    block.tensor(_b1c2)

    def _b1cv(v):
        v.wait_ge(sPe, em.n(sPe))
        em.mark(v.tensor_copy(col_sb[:], psF[:, 8:16]), sDv)
    block.vector(_b1cv)

    def _cvec(t):
        t.wait_ge(sDv, em.n(sDv))
        for k in range(8):
            i = t.matmul(psC[0:1, 0:E], col_sb[:, k:k + 1], wg_sb[:, k * E:(k + 1) * E],
                         start=(k == 0), stop=(k == 7))
        em.mark(i, sPe)
    block.tensor(_cvec)

    def _cvecv(v):
        v.wait_ge(sPe, em.n(sPe))
        em.mark(v.tensor_copy(cvec_sb[:], psC[0:1, 0:E]), sDv)
    block.vector(_cvecv)

    def _gldg(g):
        em.mark(g.dma_start(out=g1B_sb[0:1, :], in_=g1), sGrows, 16)
        em.mark(g.dma_start(out=b1B_sb[0:1, :], in_=b1r), sGrows, 16)
        em.mark(g.dma_start(out=attrB_sb[0:1, :], in_=attr), sGrows, 16)
    block.gpsimd(_gldg)
    G_ROWS = em.n(sGrows)

    for dst in (g1B_sb, b1B_sb):
        def _bct(t, dst=dst):
            t.wait_ge(sGrows, G_ROWS)
            t.wait_ge(sDv, em.n(sDv))
            t.matmul(psF[:, 0:512], onesb_sb[0:1, :], dst[0:1, 0:512],
                     start=True, stop=True)
            em.mark(t.matmul(psF[:, 512:1024], onesb_sb[0:1, :], dst[0:1, 512:1024],
                             start=True, stop=True), sPe)
        block.tensor(_bct)

        def _bcv(v, dst=dst):
            v.wait_ge(sPe, em.n(sPe))
            em.mark(v.tensor_copy(dst[:], psF[:]), sDv)
        block.vector(_bcv)

    def _bca(t):
        t.wait_ge(sGrows, G_ROWS)
        t.wait_ge(sDv, em.n(sDv))
        em.mark(t.matmul(psF[:, 0:128], onesb_sb[0:1, :], attrB_sb[0:1, 0:128],
                         start=True, stop=True), sPe)
    block.tensor(_bca)

    def _bcav(v):
        v.wait_ge(sPe, em.n(sPe))
        em.mark(v.tensor_copy(attrB_sb[:], psF[:, 0:128]), sDv)
    block.vector(_bcav)

    ST_P0 = em.n(sSt)
    DV_P0 = em.n(sDv)
    S0.close()

    SA = Scope(nc)
    wrT_sb = SA.T("wrT_sb", [128, 8 * D], BF16)
    fr_sb = SA.T("fr_sb", [128, TILES * D], BF16)
    srT_sb = SA.T("srT_sb", [HA, TOK], F32)
    mloc_sb = SA.T("mloc_sb", [HA, L], F32)
    negm_sb = SA.T("negm_sb", [HA, L], F32)
    zloc_sb = SA.T("zloc_sb", [HA, L], F32)
    Sst_sb = SA.T("Sst_sb", [1, D], F32)
    xnF_sb = [SA.T(f"xnF{i}", [128, D], BF16) for i in range(2)]
    xbuf = [SA.T(f"xbuf{i}", [128, D], F32) for i in range(2)]
    sr_sb = SA.T("sr_sb", [128, HA], F32)
    e4_bf = SA.T("e4_bf", [128, C4 * HA], BF16)

    barrier(block, em, [(sSt, ST_P0), (sDv, DV_P0)])

    def _gwr(g):
        for k in range(8):
            em.mark(g.dma_start(out=wrT_sb[:, k * D:(k + 1) * D],
                                in_=wrT[k * 128:(k + 1) * 128, :]), sGwr, 16)
    block.gpsimd(_gwr)
    G_WR = em.n(sGwr)

    def _x0(sync):
        em.mark(sync.dma_start(out=xbuf[0][:], in_=x_in[0:128, :]), sLdX[0], 16)
    block.sync(_x0)

    se_xn = []
    x_ld = [em.n(sLdX[0])]
    for t_i in range(TILES):
        buf = t_i % 2

        def _pf(sync, t_i=t_i, buf=buf):
            if t_i + 1 < TILES:
                if t_i >= 1:
                    sync.wait_ge(sSe, se_xn[t_i - 1])
                em.mark(sync.dma_start(out=xbuf[1 - buf][:],
                                       in_=x_in[(t_i + 1) * 128:(t_i + 2) * 128, :]),
                        sLdX[1 - buf], 16)
                x_ld.append(em.n(sLdX[1 - buf]))
        block.sync(_pf)

        def _w(v, t_i=t_i, buf=buf):
            v.wait_ge(sLdX[buf], x_ld[t_i])
        block.vector(_w)

        _ln_chain(block, em, sDv, sSe, xbuf[buf][:], xn_sb[:, t_i * D:(t_i + 1) * D], ln)
        se_xn.append(em.n(sSe))

        def _xnst(sync, t_i=t_i):
            sync.wait_ge(sSe, se_xn[t_i])
            em.mark(sync.dma_start(out=o_xn[t_i * 128:(t_i + 1) * 128, :],
                                   in_=xn_sb[:, t_i * D:(t_i + 1) * D]), sSt, 16)
        block.sync(_xnst)

        def _tr(t, t_i=t_i):
            t.wait_ge(sSe, se_xn[t_i])
            t.wait_ge(sDv, em.n(sDv))
            for k in range(8):
                i = t.transpose(psT[:, k * 128:(k + 1) * 128],
                                xn_sb[:, t_i * D + k * 128:t_i * D + (k + 1) * 128],
                                idb_sb[:])
            em.mark(i, sPe)
        block.tensor(_tr)

        def _trv(v, buf=buf):
            v.wait_ge(sPe, em.n(sPe))
            em.mark(v.tensor_copy(xnF_sb[buf][:], psT[:]), sDv)
        block.vector(_trv)

        def _fr(t, buf=buf):
            t.wait_ge(sGwr, G_WR)
            t.wait_ge(sDv, em.n(sDv))
            for k in range(8):
                t.matmul(psA[:, :], xnF_sb[buf][:, k * 128:(k + 1) * 128],
                         wrT_sb[:, k * D:k * D + 512], start=(k == 0), stop=(k == 7))
            for k in range(8):
                i = t.matmul(psB[:, :], xnF_sb[buf][:, k * 128:(k + 1) * 128],
                             wrT_sb[:, k * D + 512:(k + 1) * D],
                             start=(k == 0), stop=(k == 7))
            em.mark(i, sPe)
        block.tensor(_fr)

        def _frv(v, t_i=t_i):
            v.wait_ge(sPe, em.n(sPe))
            em.mark(v.tensor_copy(fr_sb[:, t_i * D:t_i * D + 512], psA[:]), sDv)
            nn = em.mark(v.tensor_copy(fr_sb[:, t_i * D + 512:(t_i + 1) * D], psB[:]), sDv)
            v.wait_ge(sDv, nn)
            nn = em.mark(v.scalar_tensor_tensor(
                bf2k[:], fr_sb[:, t_i * D:(t_i + 1) * D], 0.01,
                fr_sb[:, t_i * D:(t_i + 1) * D], op0=ALU.mult, op1=ALU.max), sDv)
            v.wait_ge(sDv, nn)
            for h in range(HA):
                em.mark(v.scalar_tensor_tensor(
                    ln["dummy"][:, h * 128:(h + 1) * 128],
                    bf2k[:, h * 128:(h + 1) * 128], 1.0,
                    attrB_sb[:, 0:128], op0=ALU.mult, op1=ALU.mult,
                    accum_out=sr_sb[:, h:h + 1]), sDv)
        block.vector(_frv)

        def _srT(t, t_i=t_i):
            t.wait_ge(sDv, em.n(sDv))
            em.mark(t.transpose(psF[0:HA, 0:128], sr_sb[:], idf_sb[:]), sPe)
        block.tensor(_srT)

        def _srTv(v, t_i=t_i):
            v.wait_ge(sPe, em.n(sPe))
            em.mark(v.tensor_copy(srT_sb[:, t_i * 128:(t_i + 1) * 128],
                                  psF[0:HA, 0:128]), sDv)
        block.vector(_srTv)

    def _p2a(v):
        v.wait_ge(sDv, em.n(sDv))
        n0 = em.mark(v.reduce_max(
            mloc_sb[:], srT_sb[:].rearrange("h (l n) -> h l n", l=L), axis=AX.X), sDv)
        v.wait_ge(sDv, n0)
        em.mark(v.tensor_scalar(negm_sb[:], mloc_sb[:], -1.0, None, op0=ALU.mult), sDv)
    block.vector(_p2a)

    def _p2m(sync):
        sync.wait_ge(sDv, em.n(sDv))
        em.mark(sync.dma_start(out=o_m, in_=mloc_sb[:]), sSt, 16)
    block.sync(_p2m)

    def _p2e(s):
        s.wait_ge(sDv, em.n(sDv))
        for l in range(L):
            em.mark(s.activation(srT_sb[:, l * NLOC:(l + 1) * NLOC],
                                 srT_sb[:, l * NLOC:(l + 1) * NLOC], AF.Exp,
                                 bias=negm_sb[:, l:l + 1], scale=1.0), sSe)
    block.scalar(_p2e)
    SE_ET = em.n(sSe)

    def _p2z(v):
        v.wait_ge(sSe, SE_ET)
        em.mark(v.reduce_sum(
            zloc_sb[:], srT_sb[:].rearrange("h (l n) -> h l n", l=L), axis=AX.X), sDv)
    block.vector(_p2z)

    def _p2zs(sync):
        sync.wait_ge(sDv, em.n(sDv))
        em.mark(sync.dma_start(out=o_z, in_=zloc_sb[:]), sSt, 16)
    block.sync(_p2zs)

    for l in range(L):
        for c4 in range(C4):
            t_i = l * C4 + c4

            def _etr(t, t_i=t_i):
                t.wait_ge(sSe, SE_ET)
                t.wait_ge(sDv, em.n(sDv))
                em.mark(t.transpose(psF[:, 0:HA], srT_sb[0:HA, t_i * 128:(t_i + 1) * 128],
                                    idf_sb[0:HA, 0:HA]), sPe)
            block.tensor(_etr)

            def _etv(v, c4=c4):
                v.wait_ge(sPe, em.n(sPe))
                em.mark(v.tensor_copy(e4_bf[:, c4 * HA:(c4 + 1) * HA], psF[:, 0:HA]), sDv)
            block.vector(_etv)

        def _smm(t, l=l):
            t.wait_ge(sDv, em.n(sDv))
            for h in range(HA):
                for c4 in range(C4):
                    t_i = l * C4 + c4
                    i = t.matmul(psS[0:1, h * 128:(h + 1) * 128],
                                 e4_bf[:, c4 * HA + h:c4 * HA + h + 1],
                                 fr_sb[:, t_i * D + h * 128:t_i * D + (h + 1) * 128],
                                 start=(c4 == 0), stop=(c4 == C4 - 1),
                                 skip_group_check=True)
            em.mark(i, sPe)
        block.tensor(_smm)

        def _sev(v, l=l):
            v.wait_ge(sPe, em.n(sPe))
            if l >= 1:
                v.wait_ge(sSt, em.n(sSt))
            em.mark(v.tensor_copy(Sst_sb[0:1, :], psS[0:1, :]), sDv)
        block.vector(_sev)

        def _sst(sync, l=l):
            sync.wait_ge(sDv, em.n(sDv))
            em.mark(sync.dma_start(out=o_s[0:1, l * D:(l + 1) * D],
                                   in_=Sst_sb[:]), sSt, 16)
        block.sync(_sst)

    ST_ATTN = em.n(sSt)
    DV_ATTN = em.n(sDv)
    PE_ATTN = em.n(sPe)
    SA.close()

    SP = Scope(nc)
    ipT_sb = SP.T("ipT_sb", [128, 8 * 3 * 512], BF16)
    ipb_sb = SP.T("ipb_sb", [1, 3 * 512], BF16)
    opT_sb = SP.T("opT_sb", [128, 8 * D], BF16)
    opb_sb = SP.T("opb_sb", [1, D], BF16)
    qkv_sb = SP.T("qkv_sb", [128, L * 3 * D], BF16)
    xnFall = SP.T("xnFall", [128, L * D], BF16)
    oacc_sb = SP.T("oacc_sb", [128, D], F32)
    oF_sb = SP.T("oF_sb", [128, D], BF16)
    scr_sc = SP.T("scr_sc", [128, HM, 8], F32)
    scre_sc = SP.T("scre_sc", [128, HM, 8], F32)
    mxm_sc = SP.T("mxm_sc", [128, HM], F32)
    negmx_sc = SP.T("negmx_sc", [128, HM], F32)
    zm_sc = SP.T("zm_sc", [128, HM], F32)
    zinv_sc = SP.T("zinv_sc", [128, HM], F32)
    zsum_sb = SP.T("zsum_sb", [128, D], F32)
    xhat_sb = SP.T("xhat_sb", [128, D], F32)
    y1bf_sb = SP.T("y1bf_sb", [128, D], BF16)
    logits_sb = SP.T("logits_sb", [128, E], F32)
    mx8_sb = SP.T("mx8_sb", [128, E], F32)
    idxu_sb = SP.T("idxu_sb", [128, E], U32)
    e8_sb = SP.T("e8_sb", [128, E], F32)
    z8_sb = SP.T("z8_sb", [128, 1], F32)
    oh_sb = SP.T("oh_sb", [128, E], F32)

    barrier(block, em, [(sSt, ST_ATTN), (sDv, DV_ATTN), (sPe, PE_ATTN)])

    def _gop(g):
        for k in range(8):
            em.mark(g.dma_start(out=opT_sb[:, k * D:(k + 1) * D],
                                in_=opT[k * 128:(k + 1) * 128, :]), sGop, 16)
        em.mark(g.dma_start(out=opb_sb[:], in_=opb), sGop, 16)
    block.gpsimd(_gop)
    G_OP = em.n(sGop)

    for c4 in range(C4):
        for l in range(L):
            t_i = l * C4 + c4

            def _qtr(t, t_i=t_i, l=l):
                t.wait_ge(sDv, em.n(sDv))
                for k in range(8):
                    i = t.transpose(psT[:, k * 128:(k + 1) * 128],
                                    xn_sb[:, t_i * D + k * 128:t_i * D + (k + 1) * 128],
                                    idb_sb[:])
                em.mark(i, sPe)
            block.tensor(_qtr)

            def _qtrv(v, l=l):
                v.wait_ge(sPe, em.n(sPe))
                em.mark(v.tensor_copy(xnFall[:, l * D:(l + 1) * D], psT[:]), sDv)
            block.vector(_qtrv)

        for half in range(2):
            def _ghalf(g, half=half, c4=c4):
                if not (c4 == 0 and half == 0):
                    g.wait_ge(sPe, em.n(sPe))
                for k in range(8):
                    em.mark(g.dma_start(
                        out=ipT_sb[:, k * 1536:(k + 1) * 1536],
                        in_=ipT[k * 128:(k + 1) * 128,
                                half * 1536:(half + 1) * 1536]), sGip, 16)
                em.mark(g.dma_start(out=ipb_sb[:],
                                    in_=ipb[0:1, half * 1536:(half + 1) * 1536]), sGip, 16)
            block.gpsimd(_ghalf)
            g_half = em.n(sGip)

            for l in range(L):
                for ns in range(3):
                    nseg = half * 3 + ns
                    tgt = psA if ns % 2 == 0 else psB

                    def _qmm(t, l=l, ns=ns, nseg=nseg, tgt=tgt, g_half=g_half):
                        t.wait_ge(sGip, g_half)
                        t.wait_ge(sDv, em.n(sDv))
                        for k in range(8):
                            t.matmul(tgt[:, :], xnFall[:, l * D + k * 128:
                                                       l * D + (k + 1) * 128],
                                     ipT_sb[:, k * 1536 + ns * 512:
                                            k * 1536 + (ns + 1) * 512],
                                     start=(k == 0), stop=False)
                        em.mark(t.matmul(tgt[:, :], onesb_sb[0:1, 0:128],
                                         ipb_sb[0:1, ns * 512:(ns + 1) * 512],
                                         start=False, stop=True), sPe)
                    block.tensor(_qmm)

                    def _qev(v, l=l, nseg=nseg, tgt=tgt):
                        v.wait_ge(sPe, em.n(sPe))
                        em.mark(v.tensor_copy(
                            qkv_sb[:, l * 3 * D + nseg * 512:
                                   l * 3 * D + (nseg + 1) * 512], tgt[:, :]), sDv)
                    block.vector(_qev)

        for l in range(L):
            t_i = l * C4 + c4

            def _sc(v, l=l):
                v.wait_ge(sDv, em.n(sDv))
                scm = []
                for m in range(L):
                    for h in range(HM):
                        i = m * HM + h
                        if i >= 4:
                            v.wait_ge(sDv, scm[i - 4])
                        scm.append(em.mark(v.scalar_tensor_tensor(
                            ln["dummy"][:, (i % 4) * 256:(i % 4 + 1) * 256],
                            qkv_sb[:, l * 3 * D + h * 256:l * 3 * D + (h + 1) * 256],
                            0.0625,
                            qkv_sb[:, m * 3 * D + D + h * 256:
                                   m * 3 * D + D + (h + 1) * 256],
                            op0=ALU.mult, op1=ALU.mult,
                            accum_out=scr_sc[:, h, m:m + 1]), sDv))
                nn = em.n(sDv)
                v.wait_ge(sDv, nn)
                n0 = em.mark(v.reduce_max(mxm_sc[:], scr_sc[:], axis=AX.X), sDv)
                v.wait_ge(sDv, n0)
                em.mark(v.tensor_scalar(negmx_sc[:], mxm_sc[:], -1.0, None,
                                        op0=ALU.mult), sDv)
            block.vector(_sc)

            def _sce(s):
                s.wait_ge(sDv, em.n(sDv))
                for h in range(HM):
                    em.mark(s.activation(scre_sc[:, h, :], scr_sc[:, h, :], AF.Exp,
                                         bias=negmx_sc[:, h:h + 1], scale=1.0), sSe)
            block.scalar(_sce)

            def _so(v, l=l):
                v.wait_ge(sSe, em.n(sSe))
                n0 = em.mark(v.reduce_sum(zm_sc[:], scre_sc[:], axis=AX.X), sDv)
                v.wait_ge(sDv, n0)
                n1 = em.mark(v.reciprocal(zinv_sc[:], zm_sc[:]), sDv)
                v.wait_ge(sDv, n1)
                for h in range(HM):
                    em.mark(v.tensor_scalar(scre_sc[:, h, :], scre_sc[:, h, :],
                                            zinv_sc[:, h:h + 1], None, op0=ALU.mult), sDv)
                nn = em.n(sDv)
                v.wait_ge(sDv, nn)
                marks = {}
                for m in range(L):
                    for h in range(HM):
                        oseg = oacc_sb[:, h * 256:(h + 1) * 256]
                        vseg = qkv_sb[:, m * 3 * D + 2 * D + h * 256:
                                      m * 3 * D + 2 * D + (h + 1) * 256]
                        if m > 0:
                            v.wait_ge(sDv, marks[(m - 1, h)])
                        if m == 0:
                            marks[(m, h)] = em.mark(v.tensor_scalar(
                                oseg, vseg, scre_sc[:, h, m:m + 1], None,
                                op0=ALU.mult), sDv)
                        else:
                            marks[(m, h)] = em.mark(v.scalar_tensor_tensor(
                                oseg, vseg, scre_sc[:, h, m:m + 1], oseg,
                                op0=ALU.mult, op1=ALU.add), sDv)
                nn = em.n(sDv)
                v.wait_ge(sDv, nn)
                em.mark(v.tensor_copy(bf2k[:], oacc_sb[:]), sDv)
            block.vector(_so)

            def _otr(t):
                t.wait_ge(sDv, em.n(sDv))
                for k in range(8):
                    i = t.transpose(psT[:, k * 128:(k + 1) * 128],
                                    bf2k[:, k * 128:(k + 1) * 128], idb_sb[:])
                em.mark(i, sPe)
            block.tensor(_otr)

            def _otrv(v):
                v.wait_ge(sPe, em.n(sPe))
                em.mark(v.tensor_copy(oF_sb[:], psT[:]), sDv)
            block.vector(_otrv)

            def _omm(t):
                t.wait_ge(sGop, G_OP)
                t.wait_ge(sDv, em.n(sDv))
                for k in range(8):
                    t.matmul(psA[:, :], oF_sb[:, k * 128:(k + 1) * 128],
                             opT_sb[:, k * D:k * D + 512], start=(k == 0), stop=False)
                t.matmul(psA[:, :], onesb_sb[0:1, 0:128], opb_sb[0:1, 0:512],
                         start=False, stop=True)
                for k in range(8):
                    t.matmul(psB[:, :], oF_sb[:, k * 128:(k + 1) * 128],
                             opT_sb[:, k * D + 512:(k + 1) * D],
                             start=(k == 0), stop=False)
                em.mark(t.matmul(psB[:, :], onesb_sb[0:1, 0:128], opb_sb[0:1, 512:1024],
                                 start=False, stop=True), sPe)
            block.tensor(_omm)

            def _zs(v, t_i=t_i):
                v.wait_ge(sPe, em.n(sPe))
                em.mark(v.tensor_tensor(zsum_sb[:, 0:512],
                                        xn_sb[:, t_i * D:t_i * D + 512], psA[:, :],
                                        op=ALU.add), sDv)
                em.mark(v.tensor_tensor(zsum_sb[:, 512:1024],
                                        xn_sb[:, t_i * D + 512:(t_i + 1) * D], psB[:, :],
                                        op=ALU.add), sDv)
            block.vector(_zs)

            _ln_chain(block, em, sDv, sSe, zsum_sb[:], xhat_sb[:], ln)

            def _y1(v, t_i=t_i):
                v.wait_ge(sSe, em.n(sSe))
                v.wait_ge(sSt, em.n(sSt))
                n0 = em.mark(v.tensor_tensor(y1bf_sb[:], xhat_sb[:], g1B_sb[:],
                                             op=ALU.mult), sDv)
                v.wait_ge(sDv, n0)
                em.mark(v.tensor_tensor(y1bf_sb[:], y1bf_sb[:], b1B_sb[:],
                                        op=ALU.add), sDv)
            block.vector(_y1)

            def _y1st(sync, t_i=t_i):
                sync.wait_ge(sDv, em.n(sDv))
                em.mark(sync.dma_start(out=o_y1[t_i * 128:(t_i + 1) * 128, :],
                                       in_=y1bf_sb[:]), sSt, 16)
            block.sync(_y1st)

            def _xtr(t):
                t.wait_ge(sSe, em.n(sSe))
                t.wait_ge(sDv, em.n(sDv))
                for k in range(8):
                    i = t.transpose(psF[:, k * 128:(k + 1) * 128],
                                    xhat_sb[:, k * 128:(k + 1) * 128], idf_sb[:])
                em.mark(i, sPe)
            block.tensor(_xtr)

            def _xtrv(v):
                v.wait_ge(sPe, em.n(sPe))
                em.mark(v.tensor_copy(ln["dummy"][:], psF[:]), sDv)
            block.vector(_xtrv)

            def _lg(t):
                t.wait_ge(sDv, em.n(sDv))
                for k in range(8):
                    t.matmul(psF[0:128, 0:E], ln["dummy"][:, k * 128:(k + 1) * 128],
                             wgp_sb[:, k * E:(k + 1) * E],
                             start=(k == 0), stop=False, skip_group_check=True)
                em.mark(t.matmul(psF[0:128, 0:E], onesf_sb[0:1, 0:128], cvec_sb[0:1, :],
                                 start=False, stop=True, skip_group_check=True), sPe)
            block.tensor(_lg)

            def _am(v):
                v.wait_ge(sPe, em.n(sPe))
                n0 = em.mark(v.tensor_copy(logits_sb[:], psF[0:128, 0:E]), sDv)
                v.wait_ge(sDv, n0)
                n1 = em.mark(v.max(mx8_sb[:], logits_sb[:]), sDv)
                v.wait_ge(sDv, n1)
                n2 = em.mark(v.max_index(idxu_sb[:], mx8_sb[:], logits_sb[:]), sDv)
                v.wait_ge(sDv, n2)
                em.mark(v.tensor_scalar(ln["negmu"][:], mx8_sb[:, 0:1], -1.0, None,
                                        op0=ALU.mult), sDv)
            block.vector(_am)

            def _cv(s, t_i=t_i):
                s.wait_ge(sDv, em.n(sDv))
                em.mark(s.activation(idx_sb[:, t_i:t_i + 1], idxu_sb[:, 0:1],
                                     AF.Identity, bias=0.0, scale=1.0), sSe)
                em.mark(s.activation(e8_sb[:], logits_sb[:], AF.Exp,
                                     bias=ln["negmu"][:, 0:1], scale=1.0), sSe)
            block.scalar(_cv)

            def _gt(v, t_i=t_i):
                v.wait_ge(sSe, em.n(sSe))
                n0 = em.mark(v.reduce_sum(z8_sb[:], e8_sb[:], axis=AX.X), sDv)
                v.wait_ge(sDv, n0)
                n1 = em.mark(v.reciprocal(gate_sb[:, t_i:t_i + 1], z8_sb[:]), sDv)
                v.wait_ge(sDv, n1)
                em.mark(v.tensor_scalar(oh_sb[:], iotaE_sb[:], idx_sb[:, t_i:t_i + 1],
                                        0.0, op0=ALU.subtract, op1=ALU.is_equal), sDv)
            block.vector(_gt)

            def _cnt(t, t_i=t_i):
                t.wait_ge(sDv, em.n(sDv))
                em.mark(t.matmul(psC[0:1, t_i * E:(t_i + 1) * E], onescol_sb[:],
                                 oh_sb[:], start=True, stop=True,
                                 skip_group_check=True), sPe)
            block.tensor(_cnt)

    def _cfin(v):
        v.wait_ge(sPe, em.n(sPe))
        for l in range(L):
            v.wait_ge(sDv, em.n(sDv))
            em.mark(v.tensor_copy(cnt_sb[0:1, l * E:(l + 1) * E],
                                  psC[0:1, (l * C4) * E:(l * C4) * E + E]), sDv)
            for c4 in range(1, C4):
                v.wait_ge(sDv, em.n(sDv))
                em.mark(v.tensor_tensor(
                    cnt_sb[0:1, l * E:(l + 1) * E],
                    cnt_sb[0:1, l * E:(l + 1) * E],
                    psC[0:1, (l * C4 + c4) * E:(l * C4 + c4) * E + E],
                    op=ALU.add), sDv)
    block.vector(_cfin)

    def _fin(sync):
        sync.wait_ge(sDv, em.n(sDv))
        em.mark(sync.dma_start(out=o_cnt, in_=cnt_sb[:]), sSt, 16)
        em.mark(sync.dma_start(out=o_idx, in_=idx_sb[:]), sSt, 16)
        em.mark(sync.dma_start(out=o_gate, in_=gate_sb[:]), sSt, 16)
        sync.wait_ge(sSt, em.n(sSt))
    block.sync(_fin)

    SP.close()
    G.close()
    es.close()
    return nc



def build_launch2():
    nc = bass.Bass("TRN2", target_bir_lowering=False)

    x_in = nc.dram_tensor("x", [TOK, D], F32, kind="ExternalInput").ap()
    xn_in = nc.dram_tensor("xn", [TOK, D], BF16, kind="ExternalInput").ap()
    y1_in = nc.dram_tensor("y1", [TOK, D], BF16, kind="ExternalInput").ap()
    idx_in = nc.dram_tensor("idxv", [128, TILES], F32, kind="ExternalInput").ap()
    gate_in = nc.dram_tensor("gatev", [128, TILES], F32, kind="ExternalInput").ap()
    mz_in = nc.dram_tensor("mzall", [CORES, 2 * HA * L], F32, kind="ExternalInput").ap()
    s_in = nc.dram_tensor("sall", [CORES, L * D], F32, kind="ExternalInput").ap()
    cnt_in = nc.dram_tensor("cntall", [CORES * L, E], F32, kind="ExternalInput").ap()
    wcm_in = nc.dram_tensor("wcm", [CORES * L, L], F32, kind="ExternalInput").ap()
    w12_in = nc.dram_tensor("w12all", [E * D, D], BF16, kind="ExternalInput").ap()
    b12_in = nc.dram_tensor("b12all", [E, D], F32, kind="ExternalInput").ap()
    wfT_in = nc.dram_tensor("wfT", [D, D], F32, kind="ExternalInput").ap()
    lng_in = nc.dram_tensor("lng", [1, D], F32, kind="ExternalInput").ap()
    lnb_in = nc.dram_tensor("lnb", [1, D], F32, kind="ExternalInput").ap()
    g2_in = nc.dram_tensor("g2r", [1, D], F32, kind="ExternalInput").ap()
    b2r_in = nc.dram_tensor("b2r", [1, D], F32, kind="ExternalInput").ap()
    idf = nc.dram_tensor("idf", [128, 128], F32, kind="ExternalInput").ap()
    idb = nc.dram_tensor("idb", [128, 128], BF16, kind="ExternalInput").ap()
    onesrow_b = nc.dram_tensor("onesrow_b", [1, 128], BF16, kind="ExternalInput").ap()
    onesrow_f = nc.dram_tensor("onesrow_f", [1, 128], F32, kind="ExternalInput").ap()
    onescol_f = nc.dram_tensor("onescol_f", [128, 1], F32, kind="ExternalInput").ap()
    ltri_in = nc.dram_tensor("ltri", [128, 128], F32, kind="ExternalInput").ap()
    iotaE_in = nc.dram_tensor("iotaE", [128, E], F32, kind="ExternalInput").ap()
    lnc_in = nc.dram_tensor("lnconsts", [128, 4], F32, kind="ExternalInput").ap()

    o_out = nc.dram_tensor("o_out", [TOK, D], F32, kind="ExternalOutput").ap()

    staging = nc.dram_tensor("staging", [E * CPC, D], BF16)
    outstage = nc.dram_tensor("outstage", [E * CPC, D], BF16)

    from contextlib import ExitStack
    es = ExitStack()
    P = lambda name, shape, dt: es.enter_context(nc.psum_tensor(name, shape, dt))

    G = Scope(nc)
    xn_sb = G.T("xn_sb", [128, TILES * D], BF16)
    y1_sb = G.T("y1_sb", [128, TILES * D], BF16)
    fhB_sb = G.T("fhB_sb", [128, L * D], BF16)
    lngB = G.T("lngB", [128, D], BF16)
    lnbB = G.T("lnbB", [128, D], BF16)
    g2B = G.T("g2B", [128, D], BF16)
    b2B = G.T("b2B", [128, D], BF16)
    idf_sb = G.T("idf_sb", [128, 128], F32)
    idb_sb = G.T("idb_sb", [128, 128], BF16)
    onesb_sb = G.T("onesb_sb", [1, 128], BF16)
    onesf_sb = G.T("onesf_sb", [1, 128], F32)
    onescol_sb = G.T("onescol_sb", [128, 1], F32)
    ltri_sb = G.T("ltri_sb", [128, 128], F32)
    iotaE_sb = G.T("iotaE_sb", [128, E], F32)
    idx_sb = G.T("idx_sb", [128, TILES], F32)
    gate_sb = G.T("gate_sb", [128, TILES], F32)
    slotu_sb = G.T("slotu_sb", [128, TILES], U32)
    ln = {k: G.T("ln_" + k, [128, 1], F32) for k in
          ("sx", "negmu", "svar", "vep", "rstd", "nmr")}
    lnB = {k: G.T("lnB_" + k, [128, 1], F32) for k in
           ("sx", "negmu", "svar", "vep", "rstd", "nmr")}
    lnc_sb = G.T("lnc_sb", [128, 4], F32)
    nc.const_aps.register(F32, EPS, lnc_sb[:, 0:1])
    nc.const_aps.register(F32, -0.5, lnc_sb[:, 1:2])
    nc.const_aps.register(F32, -1.0 / D, lnc_sb[:, 2:3])
    nc.const_aps.register(F32, 1.0 / D, lnc_sb[:, 3:4])
    smalls = {k: G.T("sm_" + k, [128, 1], F32) for k in
              ("loc", "lpos", "keep", "gv", "slot", "ovf")}
    oh_sb = G.T("oh_sb", [128, E], F32)
    ohrun_sb = G.T("ohrun_sb", [128, E], F32)

    psA = P("psA", [128, 512], F32)
    psB = P("psB", [128, 512], F32)
    psT = P("psT", [128, D], BF16)
    psF = P("psF", [128, D], F32)
    psS = P("psS", [1, D], F32)
    psC = P("psC", [L, 512], F32)

    sLd = es.enter_context(nc.semaphore("sLd"))
    sG = es.enter_context(nc.semaphore("sG"))
    sGz = es.enter_context(nc.semaphore("sGz"))
    sGsc = es.enter_context(nc.semaphore("sGsc"))
    sGb = es.enter_context(nc.semaphore("sGb"))
    sGg = es.enter_context(nc.semaphore("sGg"))
    sLdsl = es.enter_context(nc.semaphore("sLdsl"))
    sLdfh = es.enter_context(nc.semaphore("sLdfh"))
    sLdb = es.enter_context(nc.semaphore("sLdb"))
    sLdw = es.enter_context(nc.semaphore("sLdw"))
    sLdst = es.enter_context(nc.semaphore("sLdst"))
    sLdX = [es.enter_context(nc.semaphore(f"sLdX{i}")) for i in range(2)]
    sPe = es.enter_context(nc.semaphore("sPe"))
    sDv = es.enter_context(nc.semaphore("sDv"))
    sSe = es.enter_context(nc.semaphore("sSe"))
    sPo = es.enter_context(nc.semaphore("sPo"))
    sSt = es.enter_context(nc.semaphore("sSt"))
    block = es.enter_context(nc.Block())
    em = Emit()

    AW = Scope(nc)
    wbig_sb = AW.T("wbig_sb", [128, 8 * D], BF16)
    SA = Scope(nc)
    zero_sb = SA.T("zero_sb", [128, D], BF16)
    cs_sb = SA.T("cs_sb", [1, TILES * E], F32)
    runl_sb = SA.T("runl_sb", [1, TILES * E], F32)
    runa_sb = SA.T("runa_sb", [1, TILES * E], F32)
    mz_sb = SA.T("mz_sb", [CORES, 2 * HA * L], F32)
    mzT_sb = SA.T("mzT_sb", [HA * L, CORES], F32)
    zT0_sb = SA.T("zT0_sb", [HA * L, CORES], F32)
    mg_sb = SA.T("mg_sb", [HA * L, 1], F32)
    negmg_sb = SA.T("negmg_sb", [HA * L, 1], F32)
    wT_sb = SA.T("wT_sb", [HA * L, CORES], F32)
    zt_sb = SA.T("zt_sb", [HA * L, CORES], F32)
    zg_sb = SA.T("zg_sb", [HA * L, 1], F32)
    zgi_sb = SA.T("zgi_sb", [HA * L, 1], F32)
    w8_sb = SA.T("w8_sb", [CORES, HA * L], F32)
    s_l = SA.T("s_l", [CORES, D], F32)
    ctx_l = SA.T("ctx_l", [1, D], F32)
    ctxF_sb = SA.T("ctxF_sb", [128, 8 * L], BF16)
    fh_sb = SA.T("fh_sb", [L, D], F32)
    fhrow_sb = ctx_l
    cnt2_sb = SA.T("cnt2_sb", [CORES * L, E], F32)
    wcm_sb = SA.T("wcm_sb", [CORES * L, L], F32)
    base_sb = SA.T("base_sb", [L, E], F32)
    baser_sb = SA.T("baser_sb", [1, L * E], F32)

    def _ld(sync):
        for t_i in range(TILES):
            em.mark(sync.dma_start(out=xn_sb[:, t_i * D:(t_i + 1) * D],
                                   in_=xn_in[t_i * 128:(t_i + 1) * 128, :]), sLd, 16)
            em.mark(sync.dma_start(out=y1_sb[:, t_i * D:(t_i + 1) * D],
                                   in_=y1_in[t_i * 128:(t_i + 1) * 128, :]), sLd, 16)
        for dst, src in ((idf_sb[:], idf), (idb_sb[:], idb), (onesb_sb[:], onesrow_b),
                         (onesf_sb[:], onesrow_f), (onescol_sb[:], onescol_f),
                         (ltri_sb[:], ltri_in), (iotaE_sb[:], iotaE_in),
                         (idx_sb[:], idx_in), (gate_sb[:], gate_in),
                         (mz_sb[:], mz_in), (wcm_sb[:], wcm_in), (cnt2_sb[:], cnt_in),
                         (lnc_sb[:], lnc_in)):
            em.mark(sync.dma_start(out=dst, in_=src), sLd, 16)
    block.sync(_ld)
    LD0 = em.n(sLd)

    def _gld(g):
        for k in range(8):
            em.mark(g.dma_start(out=wbig_sb[:, k * D:(k + 1) * D],
                                in_=wfT_in[k * 128:(k + 1) * 128, :]), sG, 16)
        for dst, src in ((lngB, lng_in), (lnbB, lnb_in), (g2B, g2_in), (b2B, b2r_in)):
            em.mark(g.dma_start(out=dst[0:1, :], in_=src), sG, 16)
    block.gpsimd(_gld)
    G0 = em.n(sG)

    def _z(v):
        em.mark(v.memset(zero_sb[:], 0.0), sDv)
    block.vector(_z)

    bc_cell = []

    def _zs(g):
        bc_cell.append(g.to_reg(E * CPC - 1))
        g.wait_ge(sDv, em.n(sDv))
        for r in range(E * CPC // 128):
            em.mark(g.dma_start(out=staging.ap()[r * 128:(r + 1) * 128, :],
                                in_=zero_sb[:]), sGz, 16)
    block.gpsimd(_zs)
    G_ZERO = em.n(sGz)

    for dst in (lngB, lnbB, g2B, b2B):
        def _bct(t, dst=dst):
            t.wait_ge(sG, G0)
            t.wait_ge(sLd, LD0)
            t.wait_ge(sDv, em.n(sDv))
            t.matmul(psF[:, 0:512], onesb_sb[0:1, :], dst[0:1, 0:512],
                     start=True, stop=True)
            em.mark(t.matmul(psF[:, 512:1024], onesb_sb[0:1, :], dst[0:1, 512:1024],
                             start=True, stop=True), sPe)
        block.tensor(_bct)

        def _bcv(v, dst=dst):
            v.wait_ge(sPe, em.n(sPe))
            em.mark(v.tensor_copy(dst[:], psF[:]), sDv)
        block.vector(_bcv)

    def _t1(t):
        t.wait_ge(sDv, em.n(sDv))
        em.mark(t.transpose(psF[0:128, 0:CORES], mz_sb[:], idf_sb[0:CORES, 0:CORES]), sPe)
    block.tensor(_t1)

    def _v1(v):
        v.wait_ge(sPe, em.n(sPe))
        em.mark(v.tensor_copy(mzT_sb[:], psF[0:HA * L, 0:CORES]), sDv)
        n0 = em.mark(v.tensor_copy(zT0_sb[:], psF[HA * L:2 * HA * L, 0:CORES]), sDv)
        v.wait_ge(sDv, n0)
        n1 = em.mark(v.reduce_max(mg_sb[:], mzT_sb[0:HA * L, :], axis=AX.X), sDv)
        v.wait_ge(sDv, n1)
        em.mark(v.tensor_scalar(negmg_sb[:], mg_sb[:], -1.0, None, op0=ALU.mult), sDv)
    block.vector(_v1)

    def _s1(s):
        s.wait_ge(sDv, em.n(sDv))
        em.mark(s.activation(wT_sb[:], mzT_sb[0:HA * L, :], AF.Exp,
                             bias=negmg_sb[:, 0:1], scale=1.0), sSe)
    block.scalar(_s1)

    def _v2(v):
        v.wait_ge(sSe, em.n(sSe))
        n0 = em.mark(v.tensor_tensor(zt_sb[:], wT_sb[:], zT0_sb[:], op=ALU.mult), sDv)
        v.wait_ge(sDv, n0)
        n1 = em.mark(v.reduce_sum(zg_sb[:], zt_sb[:], axis=AX.X), sDv)
        v.wait_ge(sDv, n1)
        n2 = em.mark(v.reciprocal(zgi_sb[:], zg_sb[:]), sDv)
        v.wait_ge(sDv, n2)
        em.mark(v.tensor_scalar(wT_sb[:], wT_sb[:], zgi_sb[:, 0:1], None,
                                op0=ALU.mult), sDv)
    block.vector(_v2)

    def _t2(t):
        t.wait_ge(sDv, em.n(sDv))
        em.mark(t.transpose(psF[0:CORES, 0:HA * L], wT_sb[:],
                            idf_sb[0:HA * L, 0:HA * L]), sPe)
    block.tensor(_t2)

    def _v3(v):
        v.wait_ge(sPe, em.n(sPe))
        em.mark(v.tensor_copy(w8_sb[:], psF[0:CORES, 0:HA * L]), sDv)
    block.vector(_v3)

    for l in range(L):
        def _sll(sync, l=l):
            if l >= 1:
                sync.wait_ge(sPe, em.n(sPe))
            em.mark(sync.dma_start(out=s_l[:], in_=s_in[:, l * D:(l + 1) * D]), sLdsl, 16)
        block.sync(_sll)
        ld_sl = em.n(sLdsl)

        def _ctx(t, l=l, ld_sl=ld_sl):
            t.wait_ge(sLdsl, ld_sl)
            t.wait_ge(sDv, em.n(sDv))
            for h in range(HA):
                i = t.matmul(psS[0:1, h * 128:(h + 1) * 128],
                             w8_sb[:, l * HA + h:l * HA + h + 1],
                             s_l[:, h * 128:(h + 1) * 128],
                             start=True, stop=True, skip_group_check=True)
            em.mark(i, sPe)
        block.tensor(_ctx)

        def _ctxv(v, l=l):
            v.wait_ge(sPe, em.n(sPe))
            em.mark(v.tensor_copy(ctx_l[0:1, :], psS[0:1, :]), sDv)
        block.vector(_ctxv)

        for k in range(8):
            def _cf(t, l=l, k=k):
                t.wait_ge(sDv, em.n(sDv))
                em.mark(t.transpose(psF[:, 0:1],
                                    ctx_l[0:1, k * 128:(k + 1) * 128],
                                    idf_sb[0:1, 0:1]), sPe)
            block.tensor(_cf)

            def _cfv(v, l=l, k=k):
                v.wait_ge(sPe, em.n(sPe))
                em.mark(v.tensor_copy(ctxF_sb[:, k * L + l:k * L + l + 1],
                                      psF[:, 0:1]), sDv)
            block.vector(_cfv)

    def _fh(t):
        t.wait_ge(sG, G0)
        t.wait_ge(sDv, em.n(sDv))
        for k in range(8):
            t.matmul(psA[0:L, :], ctxF_sb[:, k * L:(k + 1) * L],
                     wbig_sb[:, k * D:k * D + 512], start=(k == 0), stop=(k == 7))
        for k in range(8):
            i = t.matmul(psB[0:L, :], ctxF_sb[:, k * L:(k + 1) * L],
                         wbig_sb[:, k * D + 512:(k + 1) * D],
                         start=(k == 0), stop=(k == 7))
        em.mark(i, sPe)
    block.tensor(_fh)
    PE_FH = em.n(sPe)

    def _fhv(v):
        v.wait_ge(sPe, PE_FH)
        em.mark(v.tensor_copy(fh_sb[:, 0:512], psA[0:L, :]), sDv)
        em.mark(v.tensor_copy(fh_sb[:, 512:1024], psB[0:L, :]), sDv)
    block.vector(_fhv)
    DV_FH = em.n(sDv)

    for l in range(L):
        def _fhr(sync, l=l):
            sync.wait_ge(sDv, DV_FH)
            if l >= 1:
                sync.wait_ge(sPe, em.n(sPe))
            em.mark(sync.dma_start(out=fhrow_sb[0:1, :], in_=fh_sb[l:l + 1, :]), sLdfh, 16)
        block.sync(_fhr)
        ld_f = em.n(sLdfh)

        def _fb(t, l=l, ld_f=ld_f):
            t.wait_ge(sLdfh, ld_f)
            t.wait_ge(sDv, em.n(sDv))
            t.matmul(psA[:, :], onesf_sb[0:1, :], fhrow_sb[0:1, 0:512],
                     start=True, stop=True)
            em.mark(t.matmul(psB[:, :], onesf_sb[0:1, :], fhrow_sb[0:1, 512:1024],
                             start=True, stop=True), sPe)
        block.tensor(_fb)

        def _fbv(v, l=l):
            v.wait_ge(sPe, em.n(sPe))
            em.mark(v.tensor_copy(fhB_sb[:, l * D:l * D + 512], psA[:]), sDv)
            em.mark(v.tensor_copy(fhB_sb[:, l * D + 512:(l + 1) * D], psB[:]), sDv)
        block.vector(_fbv)

    def _base(t):
        t.wait_ge(sDv, em.n(sDv))
        em.mark(t.matmul(psC[0:L, 0:E], wcm_sb[:], cnt2_sb[:],
                         start=True, stop=True), sPe)
    block.tensor(_base)

    def _basev(v):
        v.wait_ge(sPe, em.n(sPe))
        em.mark(v.tensor_copy(base_sb[:], psC[0:L, 0:E]), sDv)
    block.vector(_basev)

    def _baser(sync):
        sync.wait_ge(sDv, em.n(sDv))
        em.mark(sync.dma_start(out=baser_sb[:], in_=base_sb[:]), sLdb, 16)
    block.sync(_baser)
    LD_BASE = em.n(sLdb)

    for t_i in range(TILES):
        def _oh(v, t_i=t_i):
            v.wait_ge(sPe, em.n(sPe))
            v.wait_ge(sDv, em.n(sDv))
            em.mark(v.tensor_scalar(oh_sb[:], iotaE_sb[:], idx_sb[:, t_i:t_i + 1],
                                    0.0, op0=ALU.subtract, op1=ALU.is_equal), sDv)
        block.vector(_oh)

        def _cs(t, t_i=t_i):
            t.wait_ge(sDv, em.n(sDv))
            em.mark(t.matmul(psC[0:1, t_i * E:(t_i + 1) * E], onescol_sb[:], oh_sb[:],
                             start=True, stop=True, skip_group_check=True), sPe)
        block.tensor(_cs)

    def _csv(v):
        v.wait_ge(sPe, em.n(sPe))
        em.mark(v.tensor_copy(cs_sb[0:1, :], psC[0:1, 0:TILES * E]), sDv)
    block.vector(_csv)

    def _pre(v):
        v.wait_ge(sDv, em.n(sDv))
        v.wait_ge(sLdb, LD_BASE)
        for t_i in range(TILES):
            l, c4 = divmod(t_i, C4)
            if c4 == 0:
                em.mark(v.tensor_copy(runl_sb[0:1, t_i * E:(t_i + 1) * E],
                                      baser_sb[0:1, l * E:(l + 1) * E]), sDv)
            else:
                v.wait_ge(sDv, em.n(sDv))
                em.mark(v.tensor_tensor(runl_sb[0:1, t_i * E:(t_i + 1) * E],
                                        runl_sb[0:1, (t_i - 1) * E:t_i * E],
                                        cs_sb[0:1, (t_i - 1) * E:t_i * E],
                                        op=ALU.add), sDv)
            if t_i == 0:
                em.mark(v.memset(runa_sb[0:1, 0:E], 0.0), sDv)
            else:
                v.wait_ge(sDv, em.n(sDv))
                em.mark(v.tensor_tensor(runa_sb[0:1, t_i * E:(t_i + 1) * E],
                                        runa_sb[0:1, (t_i - 1) * E:t_i * E],
                                        cs_sb[0:1, (t_i - 1) * E:t_i * E],
                                        op=ALU.add), sDv)
    block.vector(_pre)

    for t_i in range(TILES):
        def _oh2(v, t_i=t_i):
            v.wait_ge(sDv, em.n(sDv))
            v.wait_ge(sPe, em.n(sPe))
            em.mark(v.tensor_scalar(oh_sb[:], iotaE_sb[:], idx_sb[:, t_i:t_i + 1],
                                    0.0, op0=ALU.subtract, op1=ALU.is_equal), sDv)
        block.vector(_oh2)

        def _lt(t, t_i=t_i):
            t.wait_ge(sDv, em.n(sDv))
            t.matmul(psF[0:128, 0:E], ltri_sb[:], oh_sb[:],
                     start=True, stop=False, skip_group_check=True)
            t.matmul(psF[0:128, 0:E], onesf_sb[0:1, 0:128],
                     runl_sb[0:1, t_i * E:(t_i + 1) * E],
                     start=False, stop=True, skip_group_check=True)
            t.matmul(psF[0:128, E:2 * E], ltri_sb[:], oh_sb[:],
                     start=True, stop=False, skip_group_check=True)
            em.mark(t.matmul(psF[0:128, E:2 * E], onesf_sb[0:1, 0:128],
                             runa_sb[0:1, t_i * E:(t_i + 1) * E],
                             start=False, stop=True, skip_group_check=True), sPe)
        block.tensor(_lt)

        def _sel(v, t_i=t_i):
            v.wait_ge(sPe, em.n(sPe))
            v.wait_ge(sSe, em.n(sSe))
            n0 = em.mark(v.tensor_tensor(ohrun_sb[:], psF[0:128, 0:E], oh_sb[:],
                                         op=ALU.mult), sDv)
            v.wait_ge(sDv, n0)
            n1 = em.mark(v.reduce_sum(smalls["loc"][:], ohrun_sb[:], axis=AX.X), sDv)
            v.wait_ge(sDv, n1)
            n2 = em.mark(v.tensor_tensor(ohrun_sb[:], psF[0:128, E:2 * E], oh_sb[:],
                                         op=ALU.mult), sDv)
            v.wait_ge(sDv, n2)
            n3 = em.mark(v.reduce_sum(smalls["lpos"][:], ohrun_sb[:], axis=AX.X), sDv)
            v.wait_ge(sDv, n3)
            n4 = em.mark(v.tensor_scalar(smalls["keep"][:], smalls["loc"][:], float(CAP),
                                         None, op0=ALU.is_lt), sDv)
            v.wait_ge(sDv, n4)
            em.mark(v.tensor_tensor(smalls["gv"][:], smalls["keep"][:],
                                    gate_sb[:, t_i:t_i + 1], op=ALU.mult), sDv)
            n6 = em.mark(v.tensor_scalar(smalls["slot"][:], idx_sb[:, t_i:t_i + 1],
                                         float(CPC), smalls["lpos"][:, 0:1],
                                         op0=ALU.mult, op1=ALU.add), sDv)
            n7 = em.mark(v.tensor_scalar(smalls["ovf"][:], smalls["lpos"][:], float(CPC),
                                         None, op0=ALU.is_ge), sDv)
            v.wait_ge(sDv, n7)
            n8 = em.mark(v.scalar_tensor_tensor(smalls["slot"][:], smalls["ovf"][:], OOB,
                                                smalls["slot"][:], op0=ALU.mult,
                                                op1=ALU.add), sDv)
            v.wait_ge(sDv, n8)
            em.mark(v.tensor_copy(gate_sb[:, t_i:t_i + 1], smalls["gv"][:]), sDv)
        block.vector(_sel)

        def _cvt(s, t_i=t_i):
            s.wait_ge(sDv, em.n(sDv))
            em.mark(s.activation(slotu_sb[:, t_i:t_i + 1], smalls["slot"][:], AF.Identity,
                                 bias=0.0, scale=1.0), sSe)
        block.scalar(_cvt)

        def _scat(g, t_i=t_i):
            g.wait_ge(sSe, em.n(sSe))
            g.wait_ge(sGz, G_ZERO)
            em.mark(g.indirect_dma_start(
                out=staging.ap(),
                out_offset=IndirectOffsetOnAxis(slotu_sb[:, t_i:t_i + 1], axis=0),
                in_=y1_sb[:, t_i * D:(t_i + 1) * D],
                in_offset=None,
                bounds_check=bc_cell[0],
                oob_is_err=False), sGsc, 16)
        block.gpsimd(_scat)

    G_SCAT = em.n(sGsc)

    SA.close()

    A2 = Scope(nc)
    stage2 = [A2.T(f"stage{i}", [128, D], BF16) for i in range(2)]
    stF2 = [A2.T(f"stF{i}", [128, D], BF16) for i in range(2)]
    eo2 = [A2.T(f"eo{i}", [128, D], BF16) for i in range(2)]
    wbig2_sb = A2.T("wbig2_sb", [128, 8 * D], BF16)
    b12e_sb = A2.T("b12e_sb", [1, D], BF16)

    wpair = [wbig_sb, wbig2_sb]
    NI = E * (CPC // 128)
    m_T = {}
    m_stF = {}
    m_mm = {}
    m_eo = {}
    st_out = {}
    ld_stage = {}
    ld_w = {}
    g_b = {}

    for i in range(NI):
        e, r = divmod(i, CPC // 128)
        p = i % 2
        be = e % 2

        if r == 0:
            def _wld(sync, e=e, be=be):
                sync.wait_ge(sGsc, G_SCAT)
                if e == 0:
                    sync.wait_ge(sPe, PE_FH)
                elif e >= 2:
                    sync.wait_ge(sPe, m_mm[(e - 2) * 8 + 7])
                for k in range(8):
                    em.mark(sync.dma_start(
                        out=wpair[be][:, k * D:(k + 1) * D],
                        in_=w12_in[e * D + k * 128:e * D + (k + 1) * 128, :]), sLdw, 16)
            block.sync(_wld)
            ld_w[e] = em.n(sLdw)

            def _bld(g, e=e, be=be):
                g.wait_ge(sGsc, G_SCAT)
                if e >= 1:
                    g.wait_ge(sPe, m_mm[(e - 1) * 8 + 7])
                em.mark(g.dma_start(out=b12e_sb[0:1, :],
                                    in_=b12_in[e:e + 1, :]), sGb, 16)
            block.gpsimd(_bld)
            g_b[e] = em.n(sGb)

        def _sld(sync, i=i, e=e, r=r, p=p):
            sync.wait_ge(sGsc, G_SCAT)
            if i >= 2:
                sync.wait_ge(sPe, m_T[i - 2])
            em.mark(sync.dma_start(
                out=stage2[p][:],
                in_=staging.ap()[e * CPC + r * 128:e * CPC + (r + 1) * 128, :]),
                sLdst, 16)
        block.sync(_sld)
        ld_stage[i] = em.n(sLdst)

        def _T(t, i=i, p=p):
            t.wait_ge(sLdst, ld_stage[i])
            if i >= 1:
                t.wait_ge(sDv, m_stF[i - 1])
            for k in range(8):
                ins = t.transpose(psT[:, k * 128:(k + 1) * 128],
                                  stage2[p][:, k * 128:(k + 1) * 128], idb_sb[:])
            em.mark(ins, sPe)
        block.tensor(_T)
        m_T[i] = em.n(sPe)

        def _stv(v, i=i, p=p):
            v.wait_ge(sPe, m_T[i])
            em.mark(v.tensor_copy(stF2[p][:], psT[:]), sDv)
        block.vector(_stv)
        m_stF[i] = em.n(sDv)

        def _emm(t, i=i, e=e, p=p, be=be):
            t.wait_ge(sDv, m_stF[i])
            t.wait_ge(sLdw, ld_w[e])
            t.wait_ge(sGb, g_b[e])
            tA = psA[:, :] if p == 0 else psF[0:128, 0:512]
            tB = psB[:, :] if p == 0 else psF[0:128, 512:1024]
            for k in range(8):
                t.matmul(tA, stF2[p][:, k * 128:(k + 1) * 128],
                         wpair[be][:, k * D:k * D + 512], start=(k == 0), stop=False)
            t.matmul(tA, onesb_sb[0:1, 0:128], b12e_sb[0:1, 0:512],
                     start=False, stop=True)
            for k in range(8):
                t.matmul(tB, stF2[p][:, k * 128:(k + 1) * 128],
                         wpair[be][:, k * D + 512:(k + 1) * D],
                         start=(k == 0), stop=False)
            em.mark(t.matmul(tB, onesb_sb[0:1, 0:128], b12e_sb[0:1, 512:1024],
                             start=False, stop=True), sPe)
        block.tensor(_emm)
        m_mm[i] = em.n(sPe)

        def _eev(v, i=i, p=p):
            v.wait_ge(sPe, m_mm[i])
            if i >= 2:
                v.wait_ge(sSt, st_out[i - 2])
            tA = psA[:, :] if p == 0 else psF[0:128, 0:512]
            tB = psB[:, :] if p == 0 else psF[0:128, 512:1024]
            em.mark(v.tensor_copy(eo2[p][:, 0:512], tA), sDv)
            em.mark(v.tensor_copy(eo2[p][:, 512:1024], tB), sDv)
        block.vector(_eev)
        m_eo[i] = em.n(sDv)

        def _est(sync, i=i, e=e, r=r, p=p):
            sync.wait_ge(sDv, m_eo[i])
            em.mark(sync.dma_start(
                out=outstage.ap()[e * CPC + r * 128:e * CPC + (r + 1) * 128, :],
                in_=eo2[p][:]), sSt, 16)
        block.sync(_est)
        st_out[i] = em.n(sSt)

    ST_EXP = em.n(sSt)
    DV_EXP = em.n(sDv)
    A2.close()
    AW.close()

    SC = Scope(nc)
    xbuf = [SC.T(f"xbuf{i}", [128, D], F32) for i in range(2)]
    gz2 = [SC.T(f"gz{i}", [128, D], BF16) for i in range(2)]
    z2b = [SC.T(f"z2_{i}", [128, D], F32) for i in range(2)]
    xh2b = [SC.T(f"xh2_{i}", [128, D], BF16) for i in range(2)]
    xhab = [SC.T(f"xha_{i}", [128, D], BF16) for i in range(2)]
    y2b = [SC.T(f"y2_{i}", [128, D], F32) for i in range(2)]
    aob = [SC.T(f"ao_{i}", [128, D], BF16) for i in range(2)]
    outb = [SC.T(f"out_{i}", [128, D], F32) for i in range(2)]
    ln["dummy"] = SC.T("ln_dummy", [128, D], BF16)
    lnB["dummy"] = SC.T("lnB_dummy", [128, D], BF16)

    barrier(block, em, [(sSt, ST_EXP), (sDv, DV_EXP)])

    def _gzinit(v):
        em.mark(v.memset(gz2[0][:], 0.0), sDv)
        em.mark(v.memset(gz2[1][:], 0.0), sDv)
    block.vector(_gzinit)

    m_x = {}
    m_g = {}
    m_z2 = {}
    se2 = {-1: 0, -2: 0}
    seA = {-1: 0, -2: 0}
    m_y2 = {}
    m_ya = {}
    m_ao = {}
    m_sum = {-1: 0, -2: 0}
    st_o = {-1: 0, -2: 0}
    for t_i in range(TILES):
        l = t_i // C4
        q = t_i % 2

        def _xl(sync, t_i=t_i, q=q):
            if t_i >= 2:
                sync.wait_ge(sDv, m_sum[t_i - 2])
            em.mark(sync.dma_start(out=xbuf[q][:],
                                   in_=x_in[t_i * 128:(t_i + 1) * 128, :]), sLdX[q], 16)
        block.sync(_xl)
        m_x[t_i] = em.n(sLdX[q])

        def _gth(g, t_i=t_i, q=q):
            g.wait_ge(sSt, ST_EXP)
            if t_i >= 2:
                g.wait_ge(sDv, m_z2[t_i - 2])
            else:
                g.wait_ge(sDv, em.n(sDv))
            em.mark(g.indirect_dma_start(
                out=gz2[q][:],
                out_offset=None,
                in_=outstage.ap(),
                in_offset=IndirectOffsetOnAxis(slotu_sb[:, t_i:t_i + 1], axis=0),
                bounds_check=bc_cell[0],
                oob_is_err=False), sGg, 16)
        block.gpsimd(_gth)
        m_g[t_i] = em.n(sGg)

        def _s1(v, t_i=t_i, q=q):
            v.wait_ge(sGg, m_g[t_i])
            v.wait_ge(sSe, seA[t_i - 2])
            em.mark(v.scalar_tensor_tensor(
                z2b[q][:], gz2[q][:], gate_sb[:, t_i:t_i + 1],
                y1_sb[:, t_i * D:(t_i + 1) * D], op0=ALU.mult, op1=ALU.add), sDv)
        block.vector(_s1)
        m_z2[t_i] = em.n(sDv)

        se2[t_i] = _ln_pipe(block, em, sDv, sSe, z2b[q][:], xh2b[q][:], ln,
                            pre_dv=m_z2[t_i], pre_se=se2[t_i - 1])

        def _y2(v, t_i=t_i, q=q):
            v.wait_ge(sSe, se2[t_i])
            if t_i >= 2:
                v.wait_ge(sDv, m_sum[t_i - 2])
            n0 = em.mark(v.tensor_tensor(y2b[q][:], xh2b[q][:], g2B[:],
                                         op=ALU.mult), sDv)
            v.wait_ge(sDv, n0)
            em.mark(v.tensor_tensor(y2b[q][:], y2b[q][:], b2B[:], op=ALU.add), sDv)
        block.vector(_y2)
        m_y2[t_i] = em.n(sDv)

        def _ya(v, t_i=t_i, q=q, l=l):
            v.wait_ge(sSe, se2[t_i])
            em.mark(v.tensor_tensor(z2b[q][:], xn_sb[:, t_i * D:(t_i + 1) * D],
                                    fhB_sb[:, l * D:(l + 1) * D], op=ALU.add), sDv)
        block.vector(_ya)
        m_ya[t_i] = em.n(sDv)

        seA[t_i] = _ln_pipe(block, em, sDv, sSe, z2b[q][:], xhab[q][:], lnB,
                            pre_dv=m_ya[t_i], pre_se=seA[t_i - 1])

        def _ao(g, t_i=t_i, q=q):
            g.wait_ge(sSe, seA[t_i])
            if t_i >= 2:
                g.wait_ge(sDv, m_sum[t_i - 2])
            n0 = em.mark(g.tensor_tensor(aob[q][:], xhab[q][:], lngB[:],
                                         op=ALU.mult), sPo)
            g.wait_ge(sPo, n0)
            em.mark(g.tensor_tensor(aob[q][:], aob[q][:], lnbB[:], op=ALU.add), sPo)
        block.gpsimd(_ao)
        m_ao[t_i] = em.n(sPo)

        def _sum(v, t_i=t_i, q=q):
            v.wait_ge(sLdX[q], m_x[t_i])
            v.wait_ge(sPo, m_ao[t_i])
            v.wait_ge(sSt, st_o[t_i - 2])
            n0 = em.mark(v.tensor_tensor(outb[q][:], xbuf[q][:], y2b[q][:],
                                         op=ALU.add), sDv)
            v.wait_ge(sDv, n0)
            em.mark(v.tensor_tensor(outb[q][:], outb[q][:], aob[q][:],
                                    op=ALU.add), sDv)
        block.vector(_sum)
        m_sum[t_i] = em.n(sDv)

        def _ost(sync, t_i=t_i, q=q):
            sync.wait_ge(sDv, m_sum[t_i])
            em.mark(sync.dma_start(out=o_out[t_i * 128:(t_i + 1) * 128, :],
                                   in_=outb[q][:]), sSt, 16)
        block.sync(_ost)
        st_o[t_i] = em.n(sSt)

    def _end(sync):
        sync.wait_ge(sSt, em.n(sSt))
    block.sync(_end)

    SC.close()
    G.close()
    es.close()
    return nc



def _consts():
    idf = np.eye(128, dtype=np.float32)
    idb = np.eye(128, dtype=BF16NP)
    onesrow_b = np.ones((1, 128), dtype=BF16NP)
    onesrow_f = np.ones((1, 128), dtype=np.float32)
    onescol_f = np.ones((128, 1), dtype=np.float32)
    iotaE = np.tile(np.arange(E, dtype=np.float32), (128, 1))
    ltri = np.triu(np.ones((128, 128), dtype=np.float32), 1)
    return idf, idb, onesrow_b, onesrow_f, onescol_f, iotaE, ltri


def kernel(L0_attn_h, L0_attn_Wl, L0_attn_Wr, L0_attn_att_l, L0_attn_att_r,
           L0_attn_Wf, L0_attn_ln_gamma, L0_attn_ln_beta, L0_ffn_in_proj_w,
           L0_ffn_in_proj_b, L0_ffn_out_proj_w, L0_ffn_out_proj_b, L0_ffn_ln1_g,
           L0_ffn_ln1_b, L0_ffn_ln2_g, L0_ffn_ln2_b, L0_ffn_wg, L0_ffn_w1,
           L0_ffn_b1, L0_ffn_w2, L0_ffn_b2):
    f32 = np.float32
    x = np.asarray(L0_attn_h, f32)
    idf, idb, onesrow_b, onesrow_f, onescol_f, iotaE, ltri = _consts()

    wrT = np.ascontiguousarray(np.asarray(L0_attn_Wr, f32).T)
    attr = np.ascontiguousarray(np.asarray(L0_attn_att_r, f32).reshape(1, 128))
    ipT = np.ascontiguousarray(np.asarray(L0_ffn_in_proj_w, f32).T)
    ipb = np.asarray(L0_ffn_in_proj_b, f32).reshape(1, 3 * D)
    opT = np.ascontiguousarray(np.asarray(L0_ffn_out_proj_w, f32).T)
    opb = np.asarray(L0_ffn_out_proj_b, f32).reshape(1, D)
    g1 = np.asarray(L0_ffn_ln1_g, f32).reshape(1, D)
    b1r = np.asarray(L0_ffn_ln1_b, f32).reshape(1, D)
    wgA = np.ascontiguousarray(np.asarray(L0_ffn_wg, f32))
    w1 = np.asarray(L0_ffn_w1, f32)
    b1 = np.asarray(L0_ffn_b1, f32)
    w2 = np.asarray(L0_ffn_w2, f32)
    b2 = np.asarray(L0_ffn_b2, f32)
    wfT = np.ascontiguousarray(np.asarray(L0_attn_Wf, f32).T)
    lng = np.asarray(L0_attn_ln_gamma, f32).reshape(1, D)
    lnb = np.asarray(L0_attn_ln_beta, f32).reshape(1, D)
    g2r = np.asarray(L0_ffn_ln2_g, f32).reshape(1, D)
    b2r = np.asarray(L0_ffn_ln2_b, f32).reshape(1, D)

    xs = [np.ascontiguousarray(x[:, c * NLOC:(c + 1) * NLOC, :].reshape(TOK, D))
          for c in range(CORES)]

    nc1 = build_launch1()
    in_maps1 = []
    for c in range(CORES):
        in_maps1.append({
            "x": xs[c], "wrT": wrT, "attr": attr, "ipT": ipT, "ipb": ipb,
            "opT": opT, "opb": opb, "g1": g1, "b1r": b1r, "wg": wgA,
            "w1T": np.ascontiguousarray(w1[c].T),
            "w2e": np.ascontiguousarray(w2[c]),
            "b1e": np.ascontiguousarray(b1[c].reshape(FF // 128, 128).T),
            "b2e": b2[c].reshape(1, D),
            "idf": idf, "idb": idb, "onesrow_b": onesrow_b,
            "onesrow_f": onesrow_f, "onescol_f": onescol_f, "iotaE": iotaE,
        })
    res1 = run_bass_kernel_spmd(nc1, in_maps1, core_ids=list(range(CORES)))
    r1 = res1.results

    mz_all = np.stack([
        np.concatenate([np.asarray(r1[c]["o_m"], f32).T.reshape(-1),
                        np.asarray(r1[c]["o_z"], f32).T.reshape(-1)])
        for c in range(CORES)], axis=0).astype(f32)
    s_all = np.ascontiguousarray(
        np.stack([np.asarray(r1[c]["o_s"], f32).reshape(-1) for c in range(CORES)]))
    cnt_all = np.ascontiguousarray(
        np.stack([np.asarray(r1[c]["o_cnt"], f32).reshape(L, E)
                  for c in range(CORES)]).reshape(CORES * L, E))
    w12_all = np.ascontiguousarray(
        np.concatenate([np.asarray(r1[c]["o_w12"]) for c in range(CORES)], axis=0))
    b12_all = np.ascontiguousarray(
        np.stack([np.asarray(r1[c]["o_b12"], f32).reshape(-1) for c in range(CORES)]))

    nc2 = build_launch2()
    in_maps2 = []
    for c in range(CORES):
        wcm = np.zeros((CORES * L, L), dtype=f32)
        for cp in range(CORES):
            for lp in range(L):
                for l in range(L):
                    if lp < l or (lp == l and cp < c):
                        wcm[cp * L + lp, l] = 1.0
        in_maps2.append({
            "x": xs[c],
            "xn": np.asarray(r1[c]["o_xn"]),
            "y1": np.asarray(r1[c]["o_y1"]),
            "idxv": np.asarray(r1[c]["o_idx"], f32),
            "gatev": np.asarray(r1[c]["o_gate"], f32),
            "mzall": mz_all, "sall": s_all, "cntall": cnt_all, "wcm": wcm,
            "w12all": w12_all, "b12all": b12_all,
            "wfT": wfT, "lng": lng, "lnb": lnb, "g2r": g2r, "b2r": b2r,
            "idf": idf, "idb": idb, "onesrow_b": onesrow_b,
            "onesrow_f": onesrow_f, "onescol_f": onescol_f,
            "ltri": ltri, "iotaE": iotaE,
            "lnconsts": np.tile(np.array([[EPS, -0.5, -1.0 / D, 1.0 / D]],
                                         dtype=f32), (128, 1)),
        })
    res2 = run_bass_kernel_spmd(nc2, in_maps2, core_ids=list(range(CORES)))

    out = np.empty((L, NSEQ, D), dtype=f32)
    for c in range(CORES):
        oc = np.asarray(res2.results[c]["o_out"], f32).reshape(L, NLOC, D)
        out[:, c * NLOC:(c + 1) * NLOC, :] = oc
    return out


# version 07
# speedup vs baseline: 1.0898x; 1.0395x over previous
import numpy as np
import ml_dtypes
import concourse.bass as bass
import concourse.mybir as mybir
from concourse.bass import IndirectOffsetOnAxis
from concourse.bass_utils import run_bass_kernel_spmd

F32 = mybir.dt.float32
BF16 = mybir.dt.bfloat16
U32 = mybir.dt.uint32
AF = mybir.ActivationFunctionType
ALU = mybir.AluOpType
AX = mybir.AxisListType
BF16NP = ml_dtypes.bfloat16

CORES = 8
D = 1024
L = 8
NSEQ = 4096
NLOC = NSEQ // CORES
TOK = L * NLOC
TILES = TOK // 128
C4 = 4
HA = 8
HM = 4
E = 8
FF = 4096
CAP = 4096
CPC = 1024
EPS = 1e-5
OOB = float(1 << 24)


class Emit:
    def __init__(self):
        self.cnt = {}

    def mark(self, instr, sem, k=1):
        instr.then_inc(sem, k)
        self.cnt[sem.name] = self.cnt.get(sem.name, 0) + k
        return self.cnt[sem.name]

    def n(self, sem):
        return self.cnt.get(sem.name, 0)


class Scope:

    def __init__(self, nc):
        self.nc = nc
        self.cms = []

    def T(self, name, shape, dt):
        cm = self.nc.sbuf_tensor(name, shape, dt)
        t = cm.__enter__()
        self.cms.append(cm)
        return t

    def P(self, name, shape, dt):
        cm = self.nc.psum_tensor(name, shape, dt)
        t = cm.__enter__()
        self.cms.append(cm)
        return t

    def close(self):
        for cm in reversed(self.cms):
            cm.__exit__(None, None, None)
        self.cms = []


def barrier(block, em, sems_waits):
    def _b(e):
        for sem, val in sems_waits:
            e.wait_ge(sem, val)
    block.vector(_b)
    block.scalar(_b)
    block.tensor(_b)
    block.sync(_b)
    block.gpsimd(_b)


def _ln_chain(block, em, sDv, sSe, src, xhat_out, ln):
    def _v1(v):
        v.wait_ge(sDv, em.n(sDv))
        n0 = em.mark(v.reduce_sum(ln["sx"][:], src, axis=AX.X), sDv)
        v.wait_ge(sDv, n0)
        em.mark(v.tensor_scalar(ln["negmu"][:], ln["sx"][:], -1.0 / D, None, op0=ALU.mult), sDv)
    block.vector(_v1)

    def _s1(s):
        s.wait_ge(sDv, em.n(sDv))
        em.mark(s.activation(ln["dummy"][:], src, AF.Square, bias=ln["negmu"][:, 0:1],
                             scale=1.0, accum_out=ln["svar"][:]), sSe)
    block.scalar(_s1)

    def _v2(v):
        v.wait_ge(sSe, em.n(sSe))
        em.mark(v.tensor_scalar(ln["vep"][:], ln["svar"][:], 1.0 / D, EPS,
                                op0=ALU.mult, op1=ALU.add), sDv)
    block.vector(_v2)

    def _s2a(s):
        s.wait_ge(sDv, em.n(sDv))
        em.mark(s.activation(ln["rstd"][:], ln["vep"][:], AF.Ln, bias=0.0, scale=1.0), sSe)
    block.scalar(_s2a)

    def _v2b(v):
        v.wait_ge(sSe, em.n(sSe))
        em.mark(v.tensor_scalar(ln["rstd"][:], ln["rstd"][:], -0.5, None,
                                op0=ALU.mult), sDv)
    block.vector(_v2b)

    def _s2b(s):
        s.wait_ge(sDv, em.n(sDv))
        em.mark(s.activation(ln["rstd"][:], ln["rstd"][:], AF.Exp, bias=0.0, scale=1.0), sSe)
    block.scalar(_s2b)

    def _v3(v):
        v.wait_ge(sSe, em.n(sSe))
        em.mark(v.tensor_scalar(ln["nmr"][:], ln["negmu"][:], ln["rstd"][:, 0:1], None,
                                op0=ALU.mult), sDv)
    block.vector(_v3)

    def _s3(s):
        s.wait_ge(sDv, em.n(sDv))
        em.mark(s.activation(xhat_out, src, AF.Identity, bias=ln["nmr"][:, 0:1],
                             scale=ln["rstd"][:, 0:1]), sSe)
    block.scalar(_s3)
    return em.n(sSe)


def _ln_pipe(block, em, sDv, sSe, src, xhat_out, ln, pre_dv=None, pre_se=None,
             pre_dv2=None):
    def _v1(v):
        if pre_dv is not None:
            v.wait_ge(sDv, pre_dv)
        if pre_dv2 is not None:
            v.wait_ge(sDv, pre_dv2)
        if pre_se is not None:
            v.wait_ge(sSe, pre_se)
        n0 = em.mark(v.reduce_sum(ln["sx"][:], src, axis=AX.X), sDv)
        v.wait_ge(sDv, n0)
        em.mark(v.tensor_scalar(ln["negmu"][:], ln["sx"][:], -1.0 / D, None,
                                op0=ALU.mult), sDv)
    block.vector(_v1)
    n_mu = em.n(sDv)

    def _s(s):
        s.wait_ge(sDv, n_mu)
        n2 = em.mark(s.activation(ln["dummy"][:], src, AF.Square,
                                  bias=ln["negmu"][:, 0:1], scale=1.0,
                                  accum_out=ln["svar"][:]), sSe)
        s.wait_ge(sSe, n2)
        n3 = em.mark(s.activation(ln["vep"][:], ln["svar"][:], AF.Ln,
                                  bias=EPS, scale=1.0 / D), sSe)
        s.wait_ge(sSe, n3)
        em.mark(s.activation(ln["rstd"][:], ln["vep"][:], AF.Exp,
                             bias=0.0, scale=-0.5), sSe)
    block.scalar(_s)
    n_rstd = em.n(sSe)

    def _v2(v):
        v.wait_ge(sSe, n_rstd)
        em.mark(v.tensor_scalar(xhat_out, src, ln["negmu"][:, 0:1],
                                ln["rstd"][:, 0:1], op0=ALU.add, op1=ALU.mult), sDv)
    block.vector(_v2)
    return em.n(sDv), n_rstd



def build_launch1():
    nc = bass.Bass("TRN2", target_bir_lowering=False)

    x_in = nc.dram_tensor("x", [TOK, D], F32, kind="ExternalInput").ap()
    wrT = nc.dram_tensor("wrT", [D, D], F32, kind="ExternalInput").ap()
    attr = nc.dram_tensor("attr", [1, 128], F32, kind="ExternalInput").ap()
    ipT = nc.dram_tensor("ipT", [D, 3 * D], F32, kind="ExternalInput").ap()
    ipb = nc.dram_tensor("ipb", [1, 3 * D], F32, kind="ExternalInput").ap()
    opT = nc.dram_tensor("opT", [D, D], F32, kind="ExternalInput").ap()
    opb = nc.dram_tensor("opb", [1, D], F32, kind="ExternalInput").ap()
    g1 = nc.dram_tensor("g1", [1, D], F32, kind="ExternalInput").ap()
    b1r = nc.dram_tensor("b1r", [1, D], F32, kind="ExternalInput").ap()
    wg = nc.dram_tensor("wg", [D, E], F32, kind="ExternalInput").ap()
    w1T = nc.dram_tensor("w1T", [FF, D], F32, kind="ExternalInput").ap()
    w2e = nc.dram_tensor("w2e", [FF, D], F32, kind="ExternalInput").ap()
    b1e = nc.dram_tensor("b1e", [128, FF // 128], F32, kind="ExternalInput").ap()
    b2e = nc.dram_tensor("b2e", [1, D], F32, kind="ExternalInput").ap()
    idf = nc.dram_tensor("idf", [128, 128], F32, kind="ExternalInput").ap()
    idb = nc.dram_tensor("idb", [128, 128], BF16, kind="ExternalInput").ap()
    onesrow_b = nc.dram_tensor("onesrow_b", [1, 128], BF16, kind="ExternalInput").ap()
    onesrow_f = nc.dram_tensor("onesrow_f", [1, 128], F32, kind="ExternalInput").ap()
    onescol_f = nc.dram_tensor("onescol_f", [128, 1], F32, kind="ExternalInput").ap()
    iotaE_in = nc.dram_tensor("iotaE", [128, E], F32, kind="ExternalInput").ap()

    o_w12 = nc.dram_tensor("o_w12", [D, D], BF16, kind="ExternalOutput").ap()
    o_b12 = nc.dram_tensor("o_b12", [1, D], F32, kind="ExternalOutput").ap()
    o_m = nc.dram_tensor("o_m", [HA, L], F32, kind="ExternalOutput").ap()
    o_z = nc.dram_tensor("o_z", [HA, L], F32, kind="ExternalOutput").ap()
    o_s = nc.dram_tensor("o_s", [1, L * D], F32, kind="ExternalOutput").ap()
    o_cnt = nc.dram_tensor("o_cnt", [1, L * E], F32, kind="ExternalOutput").ap()
    o_y1 = nc.dram_tensor("o_y1", [TOK, D], BF16, kind="ExternalOutput").ap()
    o_xn = nc.dram_tensor("o_xn", [TOK, D], BF16, kind="ExternalOutput").ap()
    o_idx = nc.dram_tensor("o_idx", [128, TILES], F32, kind="ExternalOutput").ap()
    o_gate = nc.dram_tensor("o_gate", [128, TILES], F32, kind="ExternalOutput").ap()

    from contextlib import ExitStack
    es = ExitStack()
    P = lambda name, shape, dt: es.enter_context(nc.psum_tensor(name, shape, dt))

    G = Scope(nc)
    xn_sb = G.T("xn_sb", [128, TILES * D], BF16)
    idf_sb = G.T("idf_sb", [128, 128], F32)
    idb_sb = G.T("idb_sb", [128, 128], BF16)
    onesb_sb = G.T("onesb_sb", [1, 128], BF16)
    onesf_sb = G.T("onesf_sb", [1, 128], F32)
    onescol_sb = G.T("onescol_sb", [128, 1], F32)
    iotaE_sb = G.T("iotaE_sb", [128, E], F32)
    attrB_sb = G.T("attrB_sb", [128, 128], BF16)
    wgp_sb = G.T("wgp_sb", [128, 8 * E], F32)
    cvec_sb = G.T("cvec_sb", [1, E], F32)
    g1B_sb = G.T("g1B_sb", [128, D], BF16)
    b1B_sb = G.T("b1B_sb", [128, D], BF16)
    idx_sb = G.T("idx_sb", [128, TILES], F32)
    gate_sb = G.T("gate_sb", [128, TILES], F32)
    cnt_sb = G.T("cnt_sb", [1, L * E], F32)
    bf2k = G.T("bf2k", [128, D], BF16)
    ln = {k: G.T("ln_" + k, [128, 1], F32) for k in
          ("sx", "negmu", "svar", "vep", "rstd", "nmr")}
    ln["dummy"] = G.T("ln_dummy", [128, D], F32)

    psA = P("psA", [128, 512], F32)
    psB = P("psB", [128, 512], F32)
    psT = P("psT", [128, D], BF16)
    psF = P("psF", [128, D], F32)
    psS = P("psS", [1, D], F32)
    psC = P("psC", [1, 512], F32)

    sLd = es.enter_context(nc.semaphore("sLd"))
    sGw2 = es.enter_context(nc.semaphore("sGw2"))
    sGw1 = es.enter_context(nc.semaphore("sGw1"))
    sGrows = es.enter_context(nc.semaphore("sGrows"))
    sGwr = es.enter_context(nc.semaphore("sGwr"))
    sGip = es.enter_context(nc.semaphore("sGip"))
    sGop = es.enter_context(nc.semaphore("sGop"))
    sLdX = [es.enter_context(nc.semaphore(f"sLdX{i}")) for i in range(2)]
    sPe = es.enter_context(nc.semaphore("sPe"))
    sDv = es.enter_context(nc.semaphore("sDv"))
    sSe = es.enter_context(nc.semaphore("sSe"))
    sSt = es.enter_context(nc.semaphore("sSt"))
    block = es.enter_context(nc.Block())
    em = Emit()

    S0 = Scope(nc)
    w2_sb = S0.T("w2_sb", [128, 32 * D], BF16)
    w1m_sb = S0.T("w1m_sb", [128, FF], BF16)
    b1c_sb = S0.T("b1c_sb", [128, FF // 128], BF16)
    w12_sb = S0.T("w12_sb", [128, D], BF16)
    b12_sb = S0.T("b12_sb", [1, D], F32)
    b2_sb = S0.T("b2_sb", [1, D], F32)
    g1f_sb = S0.T("g1f_sb", [1, D], F32)
    b1f_sb = S0.T("b1f_sb", [1, D], F32)
    wg_sb = S0.T("wg_sb", [128, 8 * E], F32)
    col_sb = S0.T("col_sb", [128, 8], F32)

    def _ld(sync):
        for dst, src in ((idf_sb[:], idf), (idb_sb[:], idb), (onesb_sb[:], onesrow_b),
                         (onesf_sb[:], onesrow_f), (onescol_sb[:], onescol_f),
                         (iotaE_sb[:], iotaE_in), (b2_sb[:], b2e),
                         (g1f_sb[:], g1), (b1f_sb[:], b1r)):
            em.mark(sync.dma_start(out=dst, in_=src), sLd, 16)
        for k in range(8):
            em.mark(sync.dma_start(out=wg_sb[:, k * E:(k + 1) * E],
                                   in_=wg[k * 128:(k + 1) * 128, :]), sLd, 16)
    block.sync(_ld)
    LD_PRE = em.n(sLd)

    def _gw2(g):
        for k in range(32):
            em.mark(g.dma_start(out=w2_sb[:, k * D:(k + 1) * D],
                                in_=w2e[k * 128:(k + 1) * 128, :]), sGw2, 16)
        em.mark(g.dma_start(out=b1c_sb[:], in_=b1e), sGw2, 16)
    block.gpsimd(_gw2)
    G_W2 = em.n(sGw2)

    pe_after_M = []
    for M in range(8):
        def _gw1(g, M=M):
            if M >= 1:
                g.wait_ge(sPe, pe_after_M[M - 1])
            em.mark(g.dma_start(
                out=w1m_sb[:].rearrange("p (k i) -> p k i", k=32),
                in_=w1T[:, M * 128:(M + 1) * 128].rearrange("(k p) i -> p k i", p=128)),
                sGw1, 16)
        block.gpsimd(_gw1)
        g_w1m = em.n(sGw1)

        def _tmm(t, M=M, g_w1m=g_w1m):
            t.wait_ge(sGw2, G_W2)
            t.wait_ge(sGw1, g_w1m)
            if M >= 1:
                t.wait_ge(sDv, em.n(sDv))
            for K in range(32):
                t.matmul(psA[:, :], w1m_sb[:, K * 128:(K + 1) * 128],
                         w2_sb[:, K * D:K * D + 512], start=(K == 0), stop=(K == 31))
            for K in range(32):
                i = t.matmul(psB[:, :], w1m_sb[:, K * 128:(K + 1) * 128],
                             w2_sb[:, K * D + 512:(K + 1) * D],
                             start=(K == 0), stop=(K == 31))
                if K == 31 and M != 0:
                    em.mark(i, sPe)
            if M == 0:
                for K in range(32):
                    t.matmul(psS[0:1, 0:512], b1c_sb[:, K:K + 1],
                             w2_sb[:, K * D:K * D + 512],
                             start=(K == 0), stop=(K == 31), skip_group_check=True)
                for K in range(32):
                    i = t.matmul(psS[0:1, 512:1024], b1c_sb[:, K:K + 1],
                                 w2_sb[:, K * D + 512:(K + 1) * D],
                                 start=(K == 0), stop=(K == 31), skip_group_check=True)
                    if K == 31:
                        em.mark(i, sPe)
        block.tensor(_tmm)
        pe_after_M.append(em.n(sPe))

        def _vev(v, M=M):
            v.wait_ge(sPe, pe_after_M[M])
            if M >= 1:
                v.wait_ge(sSt, em.n(sSt))
            em.mark(v.tensor_copy(w12_sb[:, 0:512], psA[:, :]), sDv)
            em.mark(v.tensor_copy(w12_sb[:, 512:1024], psB[:, :]), sDv)
        block.vector(_vev)

        def _st(sync, M=M):
            sync.wait_ge(sDv, em.n(sDv))
            em.mark(sync.dma_start(out=o_w12[M * 128:(M + 1) * 128, :],
                                   in_=w12_sb[:]), sSt, 16)
        block.sync(_st)

    def _b12(v):
        v.wait_ge(sPe, pe_after_M[0])
        v.wait_ge(sLd, LD_PRE)
        em.mark(v.tensor_tensor(b12_sb[0:1, 0:512], psS[0:1, 0:512],
                                b2_sb[0:1, 0:512], op=ALU.add), sDv)
        em.mark(v.tensor_tensor(b12_sb[0:1, 512:1024], psS[0:1, 512:1024],
                                b2_sb[0:1, 512:1024], op=ALU.add), sDv)
    block.vector(_b12)

    def _b12st(sync):
        sync.wait_ge(sDv, em.n(sDv))
        em.mark(sync.dma_start(out=o_b12, in_=b12_sb[:]), sSt, 16)
    block.sync(_b12st)

    def _g1c(t):
        t.wait_ge(sLd, LD_PRE)
        t.wait_ge(sPe, pe_after_M[-1])
        for k in range(8):
            i = t.transpose(psF[:, k:k + 1], g1f_sb[0:1, k * 128:(k + 1) * 128],
                            idf_sb[0:1, 0:1])
        em.mark(i, sPe)
    block.tensor(_g1c)

    def _g1cv(v):
        v.wait_ge(sPe, em.n(sPe))
        n0 = em.mark(v.tensor_copy(col_sb[:], psF[:, 0:8]), sDv)
        v.wait_ge(sDv, n0)
        for k in range(8):
            em.mark(v.tensor_scalar(wgp_sb[:, k * E:(k + 1) * E],
                                    wg_sb[:, k * E:(k + 1) * E],
                                    col_sb[:, k:k + 1], None, op0=ALU.mult), sDv)
    block.vector(_g1cv)

    def _b1c2(t):
        t.wait_ge(sDv, em.n(sDv))
        for k in range(8):
            i = t.transpose(psF[:, 8 + k:9 + k], b1f_sb[0:1, k * 128:(k + 1) * 128],
                            idf_sb[0:1, 0:1])
        em.mark(i, sPe)
    block.tensor(_b1c2)

    def _b1cv(v):
        v.wait_ge(sPe, em.n(sPe))
        em.mark(v.tensor_copy(col_sb[:], psF[:, 8:16]), sDv)
    block.vector(_b1cv)

    def _cvec(t):
        t.wait_ge(sDv, em.n(sDv))
        for k in range(8):
            i = t.matmul(psC[0:1, 0:E], col_sb[:, k:k + 1], wg_sb[:, k * E:(k + 1) * E],
                         start=(k == 0), stop=(k == 7))
        em.mark(i, sPe)
    block.tensor(_cvec)

    def _cvecv(v):
        v.wait_ge(sPe, em.n(sPe))
        em.mark(v.tensor_copy(cvec_sb[:], psC[0:1, 0:E]), sDv)
    block.vector(_cvecv)

    def _gldg(g):
        em.mark(g.dma_start(out=g1B_sb[0:1, :], in_=g1), sGrows, 16)
        em.mark(g.dma_start(out=b1B_sb[0:1, :], in_=b1r), sGrows, 16)
        em.mark(g.dma_start(out=attrB_sb[0:1, :], in_=attr), sGrows, 16)
    block.gpsimd(_gldg)
    G_ROWS = em.n(sGrows)

    for dst in (g1B_sb, b1B_sb):
        def _bct(t, dst=dst):
            t.wait_ge(sGrows, G_ROWS)
            t.wait_ge(sDv, em.n(sDv))
            t.matmul(psF[:, 0:512], onesb_sb[0:1, :], dst[0:1, 0:512],
                     start=True, stop=True)
            em.mark(t.matmul(psF[:, 512:1024], onesb_sb[0:1, :], dst[0:1, 512:1024],
                             start=True, stop=True), sPe)
        block.tensor(_bct)

        def _bcv(v, dst=dst):
            v.wait_ge(sPe, em.n(sPe))
            em.mark(v.tensor_copy(dst[:], psF[:]), sDv)
        block.vector(_bcv)

    def _bca(t):
        t.wait_ge(sGrows, G_ROWS)
        t.wait_ge(sDv, em.n(sDv))
        em.mark(t.matmul(psF[:, 0:128], onesb_sb[0:1, :], attrB_sb[0:1, 0:128],
                         start=True, stop=True), sPe)
    block.tensor(_bca)

    def _bcav(v):
        v.wait_ge(sPe, em.n(sPe))
        em.mark(v.tensor_copy(attrB_sb[:], psF[:, 0:128]), sDv)
    block.vector(_bcav)

    ST_P0 = em.n(sSt)
    DV_P0 = em.n(sDv)
    S0.close()

    SA = Scope(nc)
    wrT_sb = SA.T("wrT_sb", [128, 8 * D], BF16)
    fr_sb = SA.T("fr_sb", [128, TILES * D], BF16)
    srT_sb = SA.T("srT_sb", [HA, TOK], F32)
    mloc_sb = SA.T("mloc_sb", [HA, L], F32)
    negm_sb = SA.T("negm_sb", [HA, L], F32)
    zloc_sb = SA.T("zloc_sb", [HA, L], F32)
    Sst_sb = SA.T("Sst_sb", [1, D], F32)
    xnF_sb = [SA.T(f"xnF{i}", [128, D], BF16) for i in range(2)]
    xbuf = [SA.T(f"xbuf{i}", [128, D], F32) for i in range(2)]
    sr_sb = SA.T("sr_sb", [128, HA], F32)
    e4_bf = SA.T("e4_bf", [128, C4 * HA], BF16)

    barrier(block, em, [(sSt, ST_P0), (sDv, DV_P0)])

    def _gwr(g):
        for k in range(8):
            em.mark(g.dma_start(out=wrT_sb[:, k * D:(k + 1) * D],
                                in_=wrT[k * 128:(k + 1) * 128, :]), sGwr, 16)
    block.gpsimd(_gwr)
    G_WR = em.n(sGwr)

    def _x0(sync):
        em.mark(sync.dma_start(out=xbuf[0][:], in_=x_in[0:128, :]), sLdX[0], 16)
    block.sync(_x0)

    se_xn = []
    x_ld = [em.n(sLdX[0])]
    for t_i in range(TILES):
        buf = t_i % 2

        def _pf(sync, t_i=t_i, buf=buf):
            if t_i + 1 < TILES:
                if t_i >= 1:
                    sync.wait_ge(sSe, se_xn[t_i - 1])
                em.mark(sync.dma_start(out=xbuf[1 - buf][:],
                                       in_=x_in[(t_i + 1) * 128:(t_i + 2) * 128, :]),
                        sLdX[1 - buf], 16)
                x_ld.append(em.n(sLdX[1 - buf]))
        block.sync(_pf)

        def _w(v, t_i=t_i, buf=buf):
            v.wait_ge(sLdX[buf], x_ld[t_i])
        block.vector(_w)

        _ln_chain(block, em, sDv, sSe, xbuf[buf][:], xn_sb[:, t_i * D:(t_i + 1) * D], ln)
        se_xn.append(em.n(sSe))

        def _xnst(sync, t_i=t_i):
            sync.wait_ge(sSe, se_xn[t_i])
            em.mark(sync.dma_start(out=o_xn[t_i * 128:(t_i + 1) * 128, :],
                                   in_=xn_sb[:, t_i * D:(t_i + 1) * D]), sSt, 16)
        block.sync(_xnst)

        def _tr(t, t_i=t_i):
            t.wait_ge(sSe, se_xn[t_i])
            t.wait_ge(sDv, em.n(sDv))
            for k in range(8):
                i = t.transpose(psT[:, k * 128:(k + 1) * 128],
                                xn_sb[:, t_i * D + k * 128:t_i * D + (k + 1) * 128],
                                idb_sb[:])
            em.mark(i, sPe)
        block.tensor(_tr)

        def _trv(v, buf=buf):
            v.wait_ge(sPe, em.n(sPe))
            em.mark(v.tensor_copy(xnF_sb[buf][:], psT[:]), sDv)
        block.vector(_trv)

        def _fr(t, buf=buf):
            t.wait_ge(sGwr, G_WR)
            t.wait_ge(sDv, em.n(sDv))
            for k in range(8):
                t.matmul(psA[:, :], xnF_sb[buf][:, k * 128:(k + 1) * 128],
                         wrT_sb[:, k * D:k * D + 512], start=(k == 0), stop=(k == 7))
            for k in range(8):
                i = t.matmul(psB[:, :], xnF_sb[buf][:, k * 128:(k + 1) * 128],
                             wrT_sb[:, k * D + 512:(k + 1) * D],
                             start=(k == 0), stop=(k == 7))
            em.mark(i, sPe)
        block.tensor(_fr)

        def _frv(v, t_i=t_i):
            v.wait_ge(sPe, em.n(sPe))
            em.mark(v.tensor_copy(fr_sb[:, t_i * D:t_i * D + 512], psA[:]), sDv)
            nn = em.mark(v.tensor_copy(fr_sb[:, t_i * D + 512:(t_i + 1) * D], psB[:]), sDv)
            v.wait_ge(sDv, nn)
            nn = em.mark(v.scalar_tensor_tensor(
                bf2k[:], fr_sb[:, t_i * D:(t_i + 1) * D], 0.01,
                fr_sb[:, t_i * D:(t_i + 1) * D], op0=ALU.mult, op1=ALU.max), sDv)
            v.wait_ge(sDv, nn)
            for h in range(HA):
                em.mark(v.scalar_tensor_tensor(
                    ln["dummy"][:, h * 128:(h + 1) * 128],
                    bf2k[:, h * 128:(h + 1) * 128], 1.0,
                    attrB_sb[:, 0:128], op0=ALU.mult, op1=ALU.mult,
                    accum_out=sr_sb[:, h:h + 1]), sDv)
        block.vector(_frv)

        def _srT(t, t_i=t_i):
            t.wait_ge(sDv, em.n(sDv))
            em.mark(t.transpose(psF[0:HA, 0:128], sr_sb[:], idf_sb[:]), sPe)
        block.tensor(_srT)

        def _srTv(v, t_i=t_i):
            v.wait_ge(sPe, em.n(sPe))
            em.mark(v.tensor_copy(srT_sb[:, t_i * 128:(t_i + 1) * 128],
                                  psF[0:HA, 0:128]), sDv)
        block.vector(_srTv)

    def _p2a(v):
        v.wait_ge(sDv, em.n(sDv))
        n0 = em.mark(v.reduce_max(
            mloc_sb[:], srT_sb[:].rearrange("h (l n) -> h l n", l=L), axis=AX.X), sDv)
        v.wait_ge(sDv, n0)
        em.mark(v.tensor_scalar(negm_sb[:], mloc_sb[:], -1.0, None, op0=ALU.mult), sDv)
    block.vector(_p2a)

    def _p2m(sync):
        sync.wait_ge(sDv, em.n(sDv))
        em.mark(sync.dma_start(out=o_m, in_=mloc_sb[:]), sSt, 16)
    block.sync(_p2m)

    def _p2e(s):
        s.wait_ge(sDv, em.n(sDv))
        for l in range(L):
            em.mark(s.activation(srT_sb[:, l * NLOC:(l + 1) * NLOC],
                                 srT_sb[:, l * NLOC:(l + 1) * NLOC], AF.Exp,
                                 bias=negm_sb[:, l:l + 1], scale=1.0), sSe)
    block.scalar(_p2e)
    SE_ET = em.n(sSe)

    def _p2z(v):
        v.wait_ge(sSe, SE_ET)
        em.mark(v.reduce_sum(
            zloc_sb[:], srT_sb[:].rearrange("h (l n) -> h l n", l=L), axis=AX.X), sDv)
    block.vector(_p2z)

    def _p2zs(sync):
        sync.wait_ge(sDv, em.n(sDv))
        em.mark(sync.dma_start(out=o_z, in_=zloc_sb[:]), sSt, 16)
    block.sync(_p2zs)

    for l in range(L):
        for c4 in range(C4):
            t_i = l * C4 + c4

            def _etr(t, t_i=t_i):
                t.wait_ge(sSe, SE_ET)
                t.wait_ge(sDv, em.n(sDv))
                em.mark(t.transpose(psF[:, 0:HA], srT_sb[0:HA, t_i * 128:(t_i + 1) * 128],
                                    idf_sb[0:HA, 0:HA]), sPe)
            block.tensor(_etr)

            def _etv(v, c4=c4):
                v.wait_ge(sPe, em.n(sPe))
                em.mark(v.tensor_copy(e4_bf[:, c4 * HA:(c4 + 1) * HA], psF[:, 0:HA]), sDv)
            block.vector(_etv)

        def _smm(t, l=l):
            t.wait_ge(sDv, em.n(sDv))
            for h in range(HA):
                for c4 in range(C4):
                    t_i = l * C4 + c4
                    i = t.matmul(psS[0:1, h * 128:(h + 1) * 128],
                                 e4_bf[:, c4 * HA + h:c4 * HA + h + 1],
                                 fr_sb[:, t_i * D + h * 128:t_i * D + (h + 1) * 128],
                                 start=(c4 == 0), stop=(c4 == C4 - 1),
                                 skip_group_check=True)
            em.mark(i, sPe)
        block.tensor(_smm)

        def _sev(v, l=l):
            v.wait_ge(sPe, em.n(sPe))
            if l >= 1:
                v.wait_ge(sSt, em.n(sSt))
            em.mark(v.tensor_copy(Sst_sb[0:1, :], psS[0:1, :]), sDv)
        block.vector(_sev)

        def _sst(sync, l=l):
            sync.wait_ge(sDv, em.n(sDv))
            em.mark(sync.dma_start(out=o_s[0:1, l * D:(l + 1) * D],
                                   in_=Sst_sb[:]), sSt, 16)
        block.sync(_sst)

    ST_ATTN = em.n(sSt)
    DV_ATTN = em.n(sDv)
    PE_ATTN = em.n(sPe)
    SA.close()

    SP = Scope(nc)
    ipT_sb = SP.T("ipT_sb", [128, 8 * 3 * 512], BF16)
    ipb_sb = SP.T("ipb_sb", [1, 3 * 512], BF16)
    opT_sb = SP.T("opT_sb", [128, 8 * D], BF16)
    opb_sb = SP.T("opb_sb", [1, D], BF16)
    qkv_sb = SP.T("qkv_sb", [128, L * 3 * D], BF16)
    xnFall = SP.T("xnFall", [128, L * D], BF16)
    oacc_sb = SP.T("oacc_sb", [128, D], F32)
    oF_sb = SP.T("oF_sb", [128, D], BF16)
    scr_sc = SP.T("scr_sc", [128, HM, 8], F32)
    scre_sc = SP.T("scre_sc", [128, HM, 8], F32)
    mxm_sc = SP.T("mxm_sc", [128, HM], F32)
    negmx_sc = SP.T("negmx_sc", [128, HM], F32)
    zm_sc = SP.T("zm_sc", [128, HM], F32)
    zinv_sc = SP.T("zinv_sc", [128, HM], F32)
    zsum_sb = SP.T("zsum_sb", [128, D], F32)
    xhat_sb = SP.T("xhat_sb", [128, D], F32)
    y1bf_sb = SP.T("y1bf_sb", [128, D], BF16)
    logits_sb = SP.T("logits_sb", [128, E], F32)
    mx8_sb = SP.T("mx8_sb", [128, E], F32)
    idxu_sb = SP.T("idxu_sb", [128, E], U32)
    e8_sb = SP.T("e8_sb", [128, E], F32)
    z8_sb = SP.T("z8_sb", [128, 1], F32)
    oh_sb = SP.T("oh_sb", [128, E], F32)

    barrier(block, em, [(sSt, ST_ATTN), (sDv, DV_ATTN), (sPe, PE_ATTN)])

    def _gop(g):
        for k in range(8):
            em.mark(g.dma_start(out=opT_sb[:, k * D:(k + 1) * D],
                                in_=opT[k * 128:(k + 1) * 128, :]), sGop, 16)
        em.mark(g.dma_start(out=opb_sb[:], in_=opb), sGop, 16)
    block.gpsimd(_gop)
    G_OP = em.n(sGop)

    for c4 in range(C4):
        for l in range(L):
            t_i = l * C4 + c4

            def _qtr(t, t_i=t_i, l=l):
                t.wait_ge(sDv, em.n(sDv))
                for k in range(8):
                    i = t.transpose(psT[:, k * 128:(k + 1) * 128],
                                    xn_sb[:, t_i * D + k * 128:t_i * D + (k + 1) * 128],
                                    idb_sb[:])
                em.mark(i, sPe)
            block.tensor(_qtr)

            def _qtrv(v, l=l):
                v.wait_ge(sPe, em.n(sPe))
                em.mark(v.tensor_copy(xnFall[:, l * D:(l + 1) * D], psT[:]), sDv)
            block.vector(_qtrv)

        for half in range(2):
            def _ghalf(g, half=half, c4=c4):
                if not (c4 == 0 and half == 0):
                    g.wait_ge(sPe, em.n(sPe))
                for k in range(8):
                    em.mark(g.dma_start(
                        out=ipT_sb[:, k * 1536:(k + 1) * 1536],
                        in_=ipT[k * 128:(k + 1) * 128,
                                half * 1536:(half + 1) * 1536]), sGip, 16)
                em.mark(g.dma_start(out=ipb_sb[:],
                                    in_=ipb[0:1, half * 1536:(half + 1) * 1536]), sGip, 16)
            block.gpsimd(_ghalf)
            g_half = em.n(sGip)

            for l in range(L):
                for ns in range(3):
                    nseg = half * 3 + ns
                    tgt = psA if ns % 2 == 0 else psB

                    def _qmm(t, l=l, ns=ns, nseg=nseg, tgt=tgt, g_half=g_half):
                        t.wait_ge(sGip, g_half)
                        t.wait_ge(sDv, em.n(sDv))
                        for k in range(8):
                            t.matmul(tgt[:, :], xnFall[:, l * D + k * 128:
                                                       l * D + (k + 1) * 128],
                                     ipT_sb[:, k * 1536 + ns * 512:
                                            k * 1536 + (ns + 1) * 512],
                                     start=(k == 0), stop=False)
                        em.mark(t.matmul(tgt[:, :], onesb_sb[0:1, 0:128],
                                         ipb_sb[0:1, ns * 512:(ns + 1) * 512],
                                         start=False, stop=True), sPe)
                    block.tensor(_qmm)

                    def _qev(v, l=l, nseg=nseg, tgt=tgt):
                        v.wait_ge(sPe, em.n(sPe))
                        em.mark(v.tensor_copy(
                            qkv_sb[:, l * 3 * D + nseg * 512:
                                   l * 3 * D + (nseg + 1) * 512], tgt[:, :]), sDv)
                    block.vector(_qev)

        for l in range(L):
            t_i = l * C4 + c4

            def _sc(v, l=l):
                v.wait_ge(sDv, em.n(sDv))
                scm = []
                for m in range(L):
                    for h in range(HM):
                        i = m * HM + h
                        if i >= 4:
                            v.wait_ge(sDv, scm[i - 4])
                        scm.append(em.mark(v.scalar_tensor_tensor(
                            ln["dummy"][:, (i % 4) * 256:(i % 4 + 1) * 256],
                            qkv_sb[:, l * 3 * D + h * 256:l * 3 * D + (h + 1) * 256],
                            0.0625,
                            qkv_sb[:, m * 3 * D + D + h * 256:
                                   m * 3 * D + D + (h + 1) * 256],
                            op0=ALU.mult, op1=ALU.mult,
                            accum_out=scr_sc[:, h, m:m + 1]), sDv))
                nn = em.n(sDv)
                v.wait_ge(sDv, nn)
                n0 = em.mark(v.reduce_max(mxm_sc[:], scr_sc[:], axis=AX.X), sDv)
                v.wait_ge(sDv, n0)
                em.mark(v.tensor_scalar(negmx_sc[:], mxm_sc[:], -1.0, None,
                                        op0=ALU.mult), sDv)
            block.vector(_sc)

            def _sce(s):
                s.wait_ge(sDv, em.n(sDv))
                for h in range(HM):
                    em.mark(s.activation(scre_sc[:, h, :], scr_sc[:, h, :], AF.Exp,
                                         bias=negmx_sc[:, h:h + 1], scale=1.0), sSe)
            block.scalar(_sce)

            def _so(v, l=l):
                v.wait_ge(sSe, em.n(sSe))
                n0 = em.mark(v.reduce_sum(zm_sc[:], scre_sc[:], axis=AX.X), sDv)
                v.wait_ge(sDv, n0)
                n1 = em.mark(v.reciprocal(zinv_sc[:], zm_sc[:]), sDv)
                v.wait_ge(sDv, n1)
                for h in range(HM):
                    em.mark(v.tensor_scalar(scre_sc[:, h, :], scre_sc[:, h, :],
                                            zinv_sc[:, h:h + 1], None, op0=ALU.mult), sDv)
                nn = em.n(sDv)
                v.wait_ge(sDv, nn)
                marks = {}
                for m in range(L):
                    for h in range(HM):
                        oseg = oacc_sb[:, h * 256:(h + 1) * 256]
                        vseg = qkv_sb[:, m * 3 * D + 2 * D + h * 256:
                                      m * 3 * D + 2 * D + (h + 1) * 256]
                        if m > 0:
                            v.wait_ge(sDv, marks[(m - 1, h)])
                        if m == 0:
                            marks[(m, h)] = em.mark(v.tensor_scalar(
                                oseg, vseg, scre_sc[:, h, m:m + 1], None,
                                op0=ALU.mult), sDv)
                        else:
                            marks[(m, h)] = em.mark(v.scalar_tensor_tensor(
                                oseg, vseg, scre_sc[:, h, m:m + 1], oseg,
                                op0=ALU.mult, op1=ALU.add), sDv)
                nn = em.n(sDv)
                v.wait_ge(sDv, nn)
                em.mark(v.tensor_copy(bf2k[:], oacc_sb[:]), sDv)
            block.vector(_so)

            def _otr(t):
                t.wait_ge(sDv, em.n(sDv))
                for k in range(8):
                    i = t.transpose(psT[:, k * 128:(k + 1) * 128],
                                    bf2k[:, k * 128:(k + 1) * 128], idb_sb[:])
                em.mark(i, sPe)
            block.tensor(_otr)

            def _otrv(v):
                v.wait_ge(sPe, em.n(sPe))
                em.mark(v.tensor_copy(oF_sb[:], psT[:]), sDv)
            block.vector(_otrv)

            def _omm(t):
                t.wait_ge(sGop, G_OP)
                t.wait_ge(sDv, em.n(sDv))
                for k in range(8):
                    t.matmul(psA[:, :], oF_sb[:, k * 128:(k + 1) * 128],
                             opT_sb[:, k * D:k * D + 512], start=(k == 0), stop=False)
                t.matmul(psA[:, :], onesb_sb[0:1, 0:128], opb_sb[0:1, 0:512],
                         start=False, stop=True)
                for k in range(8):
                    t.matmul(psB[:, :], oF_sb[:, k * 128:(k + 1) * 128],
                             opT_sb[:, k * D + 512:(k + 1) * D],
                             start=(k == 0), stop=False)
                em.mark(t.matmul(psB[:, :], onesb_sb[0:1, 0:128], opb_sb[0:1, 512:1024],
                                 start=False, stop=True), sPe)
            block.tensor(_omm)

            def _zs(v, t_i=t_i):
                v.wait_ge(sPe, em.n(sPe))
                em.mark(v.tensor_tensor(zsum_sb[:, 0:512],
                                        xn_sb[:, t_i * D:t_i * D + 512], psA[:, :],
                                        op=ALU.add), sDv)
                em.mark(v.tensor_tensor(zsum_sb[:, 512:1024],
                                        xn_sb[:, t_i * D + 512:(t_i + 1) * D], psB[:, :],
                                        op=ALU.add), sDv)
            block.vector(_zs)

            _ln_chain(block, em, sDv, sSe, zsum_sb[:], xhat_sb[:], ln)

            def _y1(v, t_i=t_i):
                v.wait_ge(sSe, em.n(sSe))
                v.wait_ge(sSt, em.n(sSt))
                n0 = em.mark(v.tensor_tensor(y1bf_sb[:], xhat_sb[:], g1B_sb[:],
                                             op=ALU.mult), sDv)
                v.wait_ge(sDv, n0)
                em.mark(v.tensor_tensor(y1bf_sb[:], y1bf_sb[:], b1B_sb[:],
                                        op=ALU.add), sDv)
            block.vector(_y1)

            def _y1st(sync, t_i=t_i):
                sync.wait_ge(sDv, em.n(sDv))
                em.mark(sync.dma_start(out=o_y1[t_i * 128:(t_i + 1) * 128, :],
                                       in_=y1bf_sb[:]), sSt, 16)
            block.sync(_y1st)

            def _xtr(t):
                t.wait_ge(sSe, em.n(sSe))
                t.wait_ge(sDv, em.n(sDv))
                for k in range(8):
                    i = t.transpose(psF[:, k * 128:(k + 1) * 128],
                                    xhat_sb[:, k * 128:(k + 1) * 128], idf_sb[:])
                em.mark(i, sPe)
            block.tensor(_xtr)

            def _xtrv(v):
                v.wait_ge(sPe, em.n(sPe))
                em.mark(v.tensor_copy(ln["dummy"][:], psF[:]), sDv)
            block.vector(_xtrv)

            def _lg(t):
                t.wait_ge(sDv, em.n(sDv))
                for k in range(8):
                    t.matmul(psF[0:128, 0:E], ln["dummy"][:, k * 128:(k + 1) * 128],
                             wgp_sb[:, k * E:(k + 1) * E],
                             start=(k == 0), stop=False, skip_group_check=True)
                em.mark(t.matmul(psF[0:128, 0:E], onesf_sb[0:1, 0:128], cvec_sb[0:1, :],
                                 start=False, stop=True, skip_group_check=True), sPe)
            block.tensor(_lg)

            def _am(v):
                v.wait_ge(sPe, em.n(sPe))
                n0 = em.mark(v.tensor_copy(logits_sb[:], psF[0:128, 0:E]), sDv)
                v.wait_ge(sDv, n0)
                n1 = em.mark(v.max(mx8_sb[:], logits_sb[:]), sDv)
                v.wait_ge(sDv, n1)
                n2 = em.mark(v.max_index(idxu_sb[:], mx8_sb[:], logits_sb[:]), sDv)
                v.wait_ge(sDv, n2)
                em.mark(v.tensor_scalar(ln["negmu"][:], mx8_sb[:, 0:1], -1.0, None,
                                        op0=ALU.mult), sDv)
            block.vector(_am)

            def _cv(s, t_i=t_i):
                s.wait_ge(sDv, em.n(sDv))
                em.mark(s.activation(idx_sb[:, t_i:t_i + 1], idxu_sb[:, 0:1],
                                     AF.Identity, bias=0.0, scale=1.0), sSe)
                em.mark(s.activation(e8_sb[:], logits_sb[:], AF.Exp,
                                     bias=ln["negmu"][:, 0:1], scale=1.0), sSe)
            block.scalar(_cv)

            def _gt(v, t_i=t_i):
                v.wait_ge(sSe, em.n(sSe))
                n0 = em.mark(v.reduce_sum(z8_sb[:], e8_sb[:], axis=AX.X), sDv)
                v.wait_ge(sDv, n0)
                n1 = em.mark(v.reciprocal(gate_sb[:, t_i:t_i + 1], z8_sb[:]), sDv)
                v.wait_ge(sDv, n1)
                em.mark(v.tensor_scalar(oh_sb[:], iotaE_sb[:], idx_sb[:, t_i:t_i + 1],
                                        0.0, op0=ALU.subtract, op1=ALU.is_equal), sDv)
            block.vector(_gt)

            def _cnt(t, t_i=t_i):
                t.wait_ge(sDv, em.n(sDv))
                em.mark(t.matmul(psC[0:1, t_i * E:(t_i + 1) * E], onescol_sb[:],
                                 oh_sb[:], start=True, stop=True,
                                 skip_group_check=True), sPe)
            block.tensor(_cnt)

    def _cfin(v):
        v.wait_ge(sPe, em.n(sPe))
        for l in range(L):
            v.wait_ge(sDv, em.n(sDv))
            em.mark(v.tensor_copy(cnt_sb[0:1, l * E:(l + 1) * E],
                                  psC[0:1, (l * C4) * E:(l * C4) * E + E]), sDv)
            for c4 in range(1, C4):
                v.wait_ge(sDv, em.n(sDv))
                em.mark(v.tensor_tensor(
                    cnt_sb[0:1, l * E:(l + 1) * E],
                    cnt_sb[0:1, l * E:(l + 1) * E],
                    psC[0:1, (l * C4 + c4) * E:(l * C4 + c4) * E + E],
                    op=ALU.add), sDv)
    block.vector(_cfin)

    def _fin(sync):
        sync.wait_ge(sDv, em.n(sDv))
        em.mark(sync.dma_start(out=o_cnt, in_=cnt_sb[:]), sSt, 16)
        em.mark(sync.dma_start(out=o_idx, in_=idx_sb[:]), sSt, 16)
        em.mark(sync.dma_start(out=o_gate, in_=gate_sb[:]), sSt, 16)
        sync.wait_ge(sSt, em.n(sSt))
    block.sync(_fin)

    SP.close()
    G.close()
    es.close()
    return nc



def build_launch2():
    nc = bass.Bass("TRN2", target_bir_lowering=False)

    x_in = nc.dram_tensor("x", [TOK, D], F32, kind="ExternalInput").ap()
    xn_in = nc.dram_tensor("xn", [TOK, D], BF16, kind="ExternalInput").ap()
    y1_in = nc.dram_tensor("y1", [TOK, D], BF16, kind="ExternalInput").ap()
    idx_in = nc.dram_tensor("idxv", [128, TILES], F32, kind="ExternalInput").ap()
    gate_in = nc.dram_tensor("gatev", [128, TILES], F32, kind="ExternalInput").ap()
    mz_in = nc.dram_tensor("mzall", [CORES, 2 * HA * L], F32, kind="ExternalInput").ap()
    s_in = nc.dram_tensor("sall", [CORES, L * D], F32, kind="ExternalInput").ap()
    cnt_in = nc.dram_tensor("cntall", [CORES * L, E], F32, kind="ExternalInput").ap()
    wcm_in = nc.dram_tensor("wcm", [CORES * L, L], F32, kind="ExternalInput").ap()
    w12_in = nc.dram_tensor("w12all", [E * D, D], BF16, kind="ExternalInput").ap()
    b12_in = nc.dram_tensor("b12all", [E, D], F32, kind="ExternalInput").ap()
    wfT_in = nc.dram_tensor("wfT", [D, D], F32, kind="ExternalInput").ap()
    lng_in = nc.dram_tensor("lng", [1, D], F32, kind="ExternalInput").ap()
    lnb_in = nc.dram_tensor("lnb", [1, D], F32, kind="ExternalInput").ap()
    g2_in = nc.dram_tensor("g2r", [1, D], F32, kind="ExternalInput").ap()
    b2r_in = nc.dram_tensor("b2r", [1, D], F32, kind="ExternalInput").ap()
    idf = nc.dram_tensor("idf", [128, 128], F32, kind="ExternalInput").ap()
    idb = nc.dram_tensor("idb", [128, 128], BF16, kind="ExternalInput").ap()
    onesrow_b = nc.dram_tensor("onesrow_b", [1, 128], BF16, kind="ExternalInput").ap()
    onesrow_f = nc.dram_tensor("onesrow_f", [1, 128], F32, kind="ExternalInput").ap()
    onescol_f = nc.dram_tensor("onescol_f", [128, 1], F32, kind="ExternalInput").ap()
    ltri_in = nc.dram_tensor("ltri", [128, 128], F32, kind="ExternalInput").ap()
    iotaE_in = nc.dram_tensor("iotaE", [128, E], F32, kind="ExternalInput").ap()
    lnc_in = nc.dram_tensor("lnconsts", [128, 4], F32, kind="ExternalInput").ap()

    o_out = nc.dram_tensor("o_out", [TOK, D], F32, kind="ExternalOutput").ap()

    staging = nc.dram_tensor("staging", [E * CPC, D], BF16)
    outstage = nc.dram_tensor("outstage", [E * CPC, D], BF16)

    from contextlib import ExitStack
    es = ExitStack()
    P = lambda name, shape, dt: es.enter_context(nc.psum_tensor(name, shape, dt))

    G = Scope(nc)
    xn_sb = G.T("xn_sb", [128, TILES * D], BF16)
    y1_sb = G.T("y1_sb", [128, TILES * D], BF16)
    fhB_sb = G.T("fhB_sb", [128, L * D], BF16)
    lngB = G.T("lngB", [128, D], BF16)
    lnbB = G.T("lnbB", [128, D], BF16)
    g2B = G.T("g2B", [128, D], BF16)
    b2B = G.T("b2B", [128, D], BF16)
    idf_sb = G.T("idf_sb", [128, 128], F32)
    idb_sb = G.T("idb_sb", [128, 128], BF16)
    onesb_sb = G.T("onesb_sb", [1, 128], BF16)
    onesf_sb = G.T("onesf_sb", [1, 128], F32)
    onescol_sb = G.T("onescol_sb", [128, 1], F32)
    ltri_sb = G.T("ltri_sb", [128, 128], F32)
    iotaE_sb = G.T("iotaE_sb", [128, E], F32)
    idx_sb = G.T("idx_sb", [128, TILES], F32)
    gate_sb = G.T("gate_sb", [128, TILES], F32)
    slotu_sb = G.T("slotu_sb", [128, TILES], U32)
    ln = {k: G.T("ln_" + k, [128, 1], F32) for k in
          ("sx", "negmu", "svar", "vep", "rstd", "nmr")}
    lnB = {k: G.T("lnB_" + k, [128, 1], F32) for k in
           ("sx", "negmu", "svar", "vep", "rstd", "nmr")}
    lnc_sb = G.T("lnc_sb", [128, 4], F32)
    nc.const_aps.register(F32, EPS, lnc_sb[:, 0:1])
    nc.const_aps.register(F32, -0.5, lnc_sb[:, 1:2])
    nc.const_aps.register(F32, -1.0 / D, lnc_sb[:, 2:3])
    nc.const_aps.register(F32, 1.0 / D, lnc_sb[:, 3:4])
    smalls = {k: G.T("sm_" + k, [128, 1], F32) for k in
              ("loc", "lpos", "keep", "gv", "slot", "ovf")}
    oh_sb = G.T("oh_sb", [128, E], F32)
    ohrun_sb = G.T("ohrun_sb", [128, E], F32)

    psA = P("psA", [128, 512], F32)
    psB = P("psB", [128, 512], F32)
    psT = P("psT", [128, D], BF16)
    psF = P("psF", [128, D], F32)

    sLd = es.enter_context(nc.semaphore("sLd"))
    sG = es.enter_context(nc.semaphore("sG"))
    sGz = es.enter_context(nc.semaphore("sGz"))
    sGsc = es.enter_context(nc.semaphore("sGsc"))
    sGb = es.enter_context(nc.semaphore("sGb"))
    sGg = es.enter_context(nc.semaphore("sGg"))
    sLdsl = es.enter_context(nc.semaphore("sLdsl"))
    sLdfh = es.enter_context(nc.semaphore("sLdfh"))
    sLdb = es.enter_context(nc.semaphore("sLdb"))
    sLdw = es.enter_context(nc.semaphore("sLdw"))
    sLdst = es.enter_context(nc.semaphore("sLdst"))
    sLdS = es.enter_context(nc.semaphore("sLdS"))
    sLdX = [es.enter_context(nc.semaphore(f"sLdX{i}")) for i in range(2)]
    sPe = es.enter_context(nc.semaphore("sPe"))
    sDv = es.enter_context(nc.semaphore("sDv"))
    sSe = es.enter_context(nc.semaphore("sSe"))
    sPo = es.enter_context(nc.semaphore("sPo"))
    sSt = es.enter_context(nc.semaphore("sSt"))
    block = es.enter_context(nc.Block())
    em = Emit()

    AW = Scope(nc)
    wbig_sb = AW.T("wbig_sb", [128, 8 * D], BF16)
    SA = Scope(nc)
    psS = SA.P("psS", [1, D], F32)
    psC = SA.P("psC", [L, 512], F32)
    zero_sb = SA.T("zero_sb", [128, D], BF16)
    cs_sb = SA.T("cs_sb", [1, TILES * E], F32)
    runl_sb = SA.T("runl_sb", [1, TILES * E], F32)
    runa_sb = SA.T("runa_sb", [1, TILES * E], F32)
    mz_sb = SA.T("mz_sb", [CORES, 2 * HA * L], F32)
    mzT_sb = SA.T("mzT_sb", [HA * L, CORES], F32)
    zT0_sb = SA.T("zT0_sb", [HA * L, CORES], F32)
    mg_sb = SA.T("mg_sb", [HA * L, 1], F32)
    negmg_sb = SA.T("negmg_sb", [HA * L, 1], F32)
    wT_sb = SA.T("wT_sb", [HA * L, CORES], F32)
    zt_sb = SA.T("zt_sb", [HA * L, CORES], F32)
    zg_sb = SA.T("zg_sb", [HA * L, 1], F32)
    zgi_sb = SA.T("zgi_sb", [HA * L, 1], F32)
    w8_sb = SA.T("w8_sb", [CORES, HA * L], F32)
    s_l = SA.T("s_l", [CORES, D], F32)
    ctx_l = SA.T("ctx_l", [1, D], F32)
    ctxF_sb = SA.T("ctxF_sb", [128, 8 * L], BF16)
    fh_sb = SA.T("fh_sb", [L, D], F32)
    fhrow_sb = ctx_l
    cnt2_sb = SA.T("cnt2_sb", [CORES * L, E], F32)
    wcm_sb = SA.T("wcm_sb", [CORES * L, L], F32)
    base_sb = SA.T("base_sb", [L, E], F32)
    baser_sb = SA.T("baser_sb", [1, L * E], F32)

    def _ld(sync):
        for dst, src in ((idf_sb[:], idf), (idb_sb[:], idb), (onesb_sb[:], onesrow_b),
                         (onesf_sb[:], onesrow_f), (onescol_sb[:], onescol_f),
                         (ltri_sb[:], ltri_in), (iotaE_sb[:], iotaE_in),
                         (idx_sb[:], idx_in), (gate_sb[:], gate_in),
                         (mz_sb[:], mz_in), (wcm_sb[:], wcm_in), (cnt2_sb[:], cnt_in),
                         (lnc_sb[:], lnc_in)):
            em.mark(sync.dma_start(out=dst, in_=src), sLdS, 16)
    block.sync(_ld)
    LD0 = em.n(sLdS)

    def _ldbig(sync):
        for t_i in range(TILES):
            em.mark(sync.dma_start(out=xn_sb[:, t_i * D:(t_i + 1) * D],
                                   in_=xn_in[t_i * 128:(t_i + 1) * 128, :]), sLd, 16)
            em.mark(sync.dma_start(out=y1_sb[:, t_i * D:(t_i + 1) * D],
                                   in_=y1_in[t_i * 128:(t_i + 1) * 128, :]), sLd, 16)
    block.sync(_ldbig)
    LD_BIG = em.n(sLd)

    def _gld(g):
        for k in range(8):
            em.mark(g.dma_start(out=wbig_sb[:, k * D:(k + 1) * D],
                                in_=wfT_in[k * 128:(k + 1) * 128, :]), sG, 16)
        for dst, src in ((lngB, lng_in), (lnbB, lnb_in), (g2B, g2_in), (b2B, b2r_in)):
            em.mark(g.dma_start(out=dst[0:1, :], in_=src), sG, 16)
    block.gpsimd(_gld)
    G0 = em.n(sG)

    def _z(v):
        em.mark(v.memset(zero_sb[:], 0.0), sDv)
    block.vector(_z)

    bc_cell = []

    def _zs(g):
        bc_cell.append(g.to_reg(E * CPC - 1))
        g.wait_ge(sDv, em.n(sDv))
        for r in range(E * CPC // 128):
            em.mark(g.dma_start(out=staging.ap()[r * 128:(r + 1) * 128, :],
                                in_=zero_sb[:]), sGz, 16)
    block.gpsimd(_zs)
    G_ZERO = em.n(sGz)

    for dst in (lngB, lnbB, g2B, b2B):
        def _bct(t, dst=dst):
            t.wait_ge(sG, G0)
            t.wait_ge(sLdS, LD0)
            t.wait_ge(sDv, em.n(sDv))
            t.matmul(psF[:, 0:512], onesb_sb[0:1, :], dst[0:1, 0:512],
                     start=True, stop=True)
            em.mark(t.matmul(psF[:, 512:1024], onesb_sb[0:1, :], dst[0:1, 512:1024],
                             start=True, stop=True), sPe)
        block.tensor(_bct)

        def _bcv(v, dst=dst):
            v.wait_ge(sPe, em.n(sPe))
            em.mark(v.tensor_copy(dst[:], psF[:]), sDv)
        block.vector(_bcv)

    def _t1(t):
        t.wait_ge(sDv, em.n(sDv))
        em.mark(t.transpose(psF[0:128, 0:CORES], mz_sb[:], idf_sb[0:CORES, 0:CORES]), sPe)
    block.tensor(_t1)

    def _v1(v):
        v.wait_ge(sPe, em.n(sPe))
        em.mark(v.tensor_copy(mzT_sb[:], psF[0:HA * L, 0:CORES]), sDv)
        n0 = em.mark(v.tensor_copy(zT0_sb[:], psF[HA * L:2 * HA * L, 0:CORES]), sDv)
        v.wait_ge(sDv, n0)
        n1 = em.mark(v.reduce_max(mg_sb[:], mzT_sb[0:HA * L, :], axis=AX.X), sDv)
        v.wait_ge(sDv, n1)
        em.mark(v.tensor_scalar(negmg_sb[:], mg_sb[:], -1.0, None, op0=ALU.mult), sDv)
    block.vector(_v1)

    def _s1(s):
        s.wait_ge(sDv, em.n(sDv))
        em.mark(s.activation(wT_sb[:], mzT_sb[0:HA * L, :], AF.Exp,
                             bias=negmg_sb[:, 0:1], scale=1.0), sSe)
    block.scalar(_s1)

    def _v2(v):
        v.wait_ge(sSe, em.n(sSe))
        n0 = em.mark(v.tensor_tensor(zt_sb[:], wT_sb[:], zT0_sb[:], op=ALU.mult), sDv)
        v.wait_ge(sDv, n0)
        n1 = em.mark(v.reduce_sum(zg_sb[:], zt_sb[:], axis=AX.X), sDv)
        v.wait_ge(sDv, n1)
        n2 = em.mark(v.reciprocal(zgi_sb[:], zg_sb[:]), sDv)
        v.wait_ge(sDv, n2)
        em.mark(v.tensor_scalar(wT_sb[:], wT_sb[:], zgi_sb[:, 0:1], None,
                                op0=ALU.mult), sDv)
    block.vector(_v2)

    def _t2(t):
        t.wait_ge(sDv, em.n(sDv))
        em.mark(t.transpose(psF[0:CORES, 0:HA * L], wT_sb[:],
                            idf_sb[0:HA * L, 0:HA * L]), sPe)
    block.tensor(_t2)

    def _v3(v):
        v.wait_ge(sPe, em.n(sPe))
        em.mark(v.tensor_copy(w8_sb[:], psF[0:CORES, 0:HA * L]), sDv)
    block.vector(_v3)

    for l in range(L):
        def _sll(sync, l=l):
            if l >= 1:
                sync.wait_ge(sPe, em.n(sPe))
            em.mark(sync.dma_start(out=s_l[:], in_=s_in[:, l * D:(l + 1) * D]), sLdsl, 16)
        block.sync(_sll)
        ld_sl = em.n(sLdsl)

        def _ctx(t, l=l, ld_sl=ld_sl):
            t.wait_ge(sLdsl, ld_sl)
            t.wait_ge(sDv, em.n(sDv))
            for h in range(HA):
                i = t.matmul(psS[0:1, h * 128:(h + 1) * 128],
                             w8_sb[:, l * HA + h:l * HA + h + 1],
                             s_l[:, h * 128:(h + 1) * 128],
                             start=True, stop=True, skip_group_check=True)
            em.mark(i, sPe)
        block.tensor(_ctx)

        def _ctxv(v, l=l):
            v.wait_ge(sPe, em.n(sPe))
            em.mark(v.tensor_copy(ctx_l[0:1, :], psS[0:1, :]), sDv)
        block.vector(_ctxv)

        def _cf(t, l=l):
            t.wait_ge(sDv, em.n(sDv))
            for k in range(8):
                i = t.transpose(psF[:, k:k + 1],
                                ctx_l[0:1, k * 128:(k + 1) * 128],
                                idf_sb[0:1, 0:1])
            em.mark(i, sPe)
        block.tensor(_cf)

        def _cfv(v, l=l):
            v.wait_ge(sPe, em.n(sPe))
            em.mark(v.tensor_copy(
                ctxF_sb[:].rearrange("p (k j) -> p k j", j=L)[:, :, l],
                psF[:, 0:8]), sDv)
        block.vector(_cfv)

    def _fh(t):
        t.wait_ge(sG, G0)
        t.wait_ge(sDv, em.n(sDv))
        for k in range(8):
            t.matmul(psA[0:L, :], ctxF_sb[:, k * L:(k + 1) * L],
                     wbig_sb[:, k * D:k * D + 512], start=(k == 0), stop=(k == 7))
        for k in range(8):
            i = t.matmul(psB[0:L, :], ctxF_sb[:, k * L:(k + 1) * L],
                         wbig_sb[:, k * D + 512:(k + 1) * D],
                         start=(k == 0), stop=(k == 7))
        em.mark(i, sPe)
    block.tensor(_fh)
    PE_FH = em.n(sPe)

    def _fhv(v):
        v.wait_ge(sPe, PE_FH)
        em.mark(v.tensor_copy(fh_sb[:, 0:512], psA[0:L, :]), sDv)
        em.mark(v.tensor_copy(fh_sb[:, 512:1024], psB[0:L, :]), sDv)
    block.vector(_fhv)
    DV_FH = em.n(sDv)

    for l in range(L):
        def _fhr(sync, l=l):
            sync.wait_ge(sDv, DV_FH)
            if l >= 1:
                sync.wait_ge(sPe, em.n(sPe))
            em.mark(sync.dma_start(out=fhrow_sb[0:1, :], in_=fh_sb[l:l + 1, :]), sLdfh, 16)
        block.sync(_fhr)
        ld_f = em.n(sLdfh)

        def _fb(t, l=l, ld_f=ld_f):
            t.wait_ge(sLdfh, ld_f)
            t.wait_ge(sDv, em.n(sDv))
            t.matmul(psA[:, :], onesf_sb[0:1, :], fhrow_sb[0:1, 0:512],
                     start=True, stop=True)
            em.mark(t.matmul(psB[:, :], onesf_sb[0:1, :], fhrow_sb[0:1, 512:1024],
                             start=True, stop=True), sPe)
        block.tensor(_fb)

        def _fbv(v, l=l):
            v.wait_ge(sPe, em.n(sPe))
            em.mark(v.tensor_copy(fhB_sb[:, l * D:l * D + 512], psA[:]), sDv)
            em.mark(v.tensor_copy(fhB_sb[:, l * D + 512:(l + 1) * D], psB[:]), sDv)
        block.vector(_fbv)

    def _base(t):
        t.wait_ge(sDv, em.n(sDv))
        em.mark(t.matmul(psC[0:L, 0:E], wcm_sb[:], cnt2_sb[:],
                         start=True, stop=True), sPe)
    block.tensor(_base)

    def _basev(v):
        v.wait_ge(sPe, em.n(sPe))
        em.mark(v.tensor_copy(base_sb[:], psC[0:L, 0:E]), sDv)
    block.vector(_basev)

    def _baser(sync):
        sync.wait_ge(sDv, em.n(sDv))
        em.mark(sync.dma_start(out=baser_sb[:], in_=base_sb[:]), sLdb, 16)
    block.sync(_baser)
    LD_BASE = em.n(sLdb)

    for t_i in range(TILES):
        def _oh(v, t_i=t_i):
            v.wait_ge(sPe, em.n(sPe))
            v.wait_ge(sDv, em.n(sDv))
            em.mark(v.tensor_scalar(oh_sb[:], iotaE_sb[:], idx_sb[:, t_i:t_i + 1],
                                    0.0, op0=ALU.subtract, op1=ALU.is_equal), sDv)
        block.vector(_oh)

        def _cs(t, t_i=t_i):
            t.wait_ge(sDv, em.n(sDv))
            em.mark(t.matmul(psC[0:1, t_i * E:(t_i + 1) * E], onescol_sb[:], oh_sb[:],
                             start=True, stop=True, skip_group_check=True), sPe)
        block.tensor(_cs)

    def _csv(v):
        v.wait_ge(sPe, em.n(sPe))
        em.mark(v.tensor_copy(cs_sb[0:1, :], psC[0:1, 0:TILES * E]), sDv)
    block.vector(_csv)

    def _pre(v):
        v.wait_ge(sDv, em.n(sDv))
        v.wait_ge(sLdb, LD_BASE)
        for t_i in range(TILES):
            l, c4 = divmod(t_i, C4)
            if c4 == 0:
                em.mark(v.tensor_copy(runl_sb[0:1, t_i * E:(t_i + 1) * E],
                                      baser_sb[0:1, l * E:(l + 1) * E]), sDv)
            else:
                v.wait_ge(sDv, em.n(sDv))
                em.mark(v.tensor_tensor(runl_sb[0:1, t_i * E:(t_i + 1) * E],
                                        runl_sb[0:1, (t_i - 1) * E:t_i * E],
                                        cs_sb[0:1, (t_i - 1) * E:t_i * E],
                                        op=ALU.add), sDv)
            if t_i == 0:
                em.mark(v.memset(runa_sb[0:1, 0:E], 0.0), sDv)
            else:
                v.wait_ge(sDv, em.n(sDv))
                em.mark(v.tensor_tensor(runa_sb[0:1, t_i * E:(t_i + 1) * E],
                                        runa_sb[0:1, (t_i - 1) * E:t_i * E],
                                        cs_sb[0:1, (t_i - 1) * E:t_i * E],
                                        op=ALU.add), sDv)
    block.vector(_pre)

    for t_i in range(TILES):
        def _oh2(v, t_i=t_i):
            v.wait_ge(sDv, em.n(sDv))
            v.wait_ge(sPe, em.n(sPe))
            em.mark(v.tensor_scalar(oh_sb[:], iotaE_sb[:], idx_sb[:, t_i:t_i + 1],
                                    0.0, op0=ALU.subtract, op1=ALU.is_equal), sDv)
        block.vector(_oh2)

        def _lt(t, t_i=t_i):
            t.wait_ge(sDv, em.n(sDv))
            t.matmul(psF[0:128, 0:E], ltri_sb[:], oh_sb[:],
                     start=True, stop=False, skip_group_check=True)
            t.matmul(psF[0:128, 0:E], onesf_sb[0:1, 0:128],
                     runl_sb[0:1, t_i * E:(t_i + 1) * E],
                     start=False, stop=True, skip_group_check=True)
            t.matmul(psF[0:128, E:2 * E], ltri_sb[:], oh_sb[:],
                     start=True, stop=False, skip_group_check=True)
            em.mark(t.matmul(psF[0:128, E:2 * E], onesf_sb[0:1, 0:128],
                             runa_sb[0:1, t_i * E:(t_i + 1) * E],
                             start=False, stop=True, skip_group_check=True), sPe)
        block.tensor(_lt)

        def _sel(v, t_i=t_i):
            v.wait_ge(sPe, em.n(sPe))
            v.wait_ge(sSe, em.n(sSe))
            n0 = em.mark(v.tensor_tensor(ohrun_sb[:], psF[0:128, 0:E], oh_sb[:],
                                         op=ALU.mult), sDv)
            v.wait_ge(sDv, n0)
            n1 = em.mark(v.reduce_sum(smalls["loc"][:], ohrun_sb[:], axis=AX.X), sDv)
            v.wait_ge(sDv, n1)
            n2 = em.mark(v.tensor_tensor(ohrun_sb[:], psF[0:128, E:2 * E], oh_sb[:],
                                         op=ALU.mult), sDv)
            v.wait_ge(sDv, n2)
            n3 = em.mark(v.reduce_sum(smalls["lpos"][:], ohrun_sb[:], axis=AX.X), sDv)
            v.wait_ge(sDv, n3)
            n4 = em.mark(v.tensor_scalar(smalls["keep"][:], smalls["loc"][:], float(CAP),
                                         None, op0=ALU.is_lt), sDv)
            v.wait_ge(sDv, n4)
            em.mark(v.tensor_tensor(smalls["gv"][:], smalls["keep"][:],
                                    gate_sb[:, t_i:t_i + 1], op=ALU.mult), sDv)
            n6 = em.mark(v.tensor_scalar(smalls["slot"][:], idx_sb[:, t_i:t_i + 1],
                                         float(CPC), smalls["lpos"][:, 0:1],
                                         op0=ALU.mult, op1=ALU.add), sDv)
            n7 = em.mark(v.tensor_scalar(smalls["ovf"][:], smalls["lpos"][:], float(CPC),
                                         None, op0=ALU.is_ge), sDv)
            v.wait_ge(sDv, n7)
            n8 = em.mark(v.scalar_tensor_tensor(smalls["slot"][:], smalls["ovf"][:], OOB,
                                                smalls["slot"][:], op0=ALU.mult,
                                                op1=ALU.add), sDv)
            v.wait_ge(sDv, n8)
            em.mark(v.tensor_copy(gate_sb[:, t_i:t_i + 1], smalls["gv"][:]), sDv)
        block.vector(_sel)

        def _cvt(s, t_i=t_i):
            s.wait_ge(sDv, em.n(sDv))
            em.mark(s.activation(slotu_sb[:, t_i:t_i + 1], smalls["slot"][:], AF.Identity,
                                 bias=0.0, scale=1.0), sSe)
        block.scalar(_cvt)

        def _scat(g, t_i=t_i):
            g.wait_ge(sSe, em.n(sSe))
            g.wait_ge(sGz, G_ZERO)
            g.wait_ge(sLd, LD_BIG)
            em.mark(g.indirect_dma_start(
                out=staging.ap(),
                out_offset=IndirectOffsetOnAxis(slotu_sb[:, t_i:t_i + 1], axis=0),
                in_=y1_sb[:, t_i * D:(t_i + 1) * D],
                in_offset=None,
                bounds_check=bc_cell[0],
                oob_is_err=False), sGsc, 16)
        block.gpsimd(_scat)

    G_SCAT = em.n(sGsc)

    SA.close()

    A2 = Scope(nc)
    psT2 = A2.P("psT2", [128, D], BF16)
    stage2 = [A2.T(f"stage{i}", [128, D], BF16) for i in range(2)]
    stF2 = [A2.T(f"stF{i}", [128, D], BF16) for i in range(2)]
    eo2 = [A2.T(f"eo{i}", [128, D], BF16) for i in range(2)]
    wbig2_sb = A2.T("wbig2_sb", [128, 8 * D], BF16)
    b12e_sb = A2.T("b12e_sb", [1, D], BF16)

    wpair = [wbig_sb, wbig2_sb]
    tpair = [psT, psT2]
    NI = E * (CPC // 128)
    m_T = {}
    m_stF = {}
    m_mm = {-1: 0, -2: 0}
    m_eo = {-1: 0, -2: 0}
    st_out = {-1: 0, -2: 0}
    ld_stage = {}
    ld_w = {}
    g_b = {}

    def emit_wld(e):
        be = e % 2

        def _wld(sync, e=e, be=be):
            sync.wait_ge(sGsc, G_SCAT)
            if e == 0:
                sync.wait_ge(sPe, PE_FH)
            elif e >= 2:
                sync.wait_ge(sPe, m_mm[(e - 2) * 8 + 7])
            for k in range(8):
                em.mark(sync.dma_start(
                    out=wpair[be][:, k * D:(k + 1) * D],
                    in_=w12_in[e * D + k * 128:e * D + (k + 1) * 128, :]), sLdw, 16)
        block.sync(_wld)
        ld_w[e] = em.n(sLdw)

    def emit_bld(e):
        def _bld(g, e=e):
            g.wait_ge(sGsc, G_SCAT)
            if e >= 1:
                g.wait_ge(sPe, m_mm[(e - 1) * 8 + 7])
            em.mark(g.dma_start(out=b12e_sb[0:1, :],
                                in_=b12_in[e:e + 1, :]), sGb, 16)
        block.gpsimd(_bld)
        g_b[e] = em.n(sGb)

    def emit_sld(i):
        e, r = divmod(i, CPC // 128)
        p = i % 2

        def _sld(sync, i=i, e=e, r=r, p=p):
            sync.wait_ge(sGsc, G_SCAT)
            if i >= 2:
                sync.wait_ge(sPe, m_T[i - 2])
            em.mark(sync.dma_start(
                out=stage2[p][:],
                in_=staging.ap()[e * CPC + r * 128:e * CPC + (r + 1) * 128, :]),
                sLdst, 16)
        block.sync(_sld)
        ld_stage[i] = em.n(sLdst)

    def emit_T(i):
        p = i % 2

        def _T(t, i=i, p=p):
            t.wait_ge(sLdst, ld_stage[i])
            if i >= 2:
                t.wait_ge(sDv, m_stF[i - 2])
            for k in range(8):
                ins = t.transpose(tpair[p][:, k * 128:(k + 1) * 128],
                                  stage2[p][:, k * 128:(k + 1) * 128], idb_sb[:])
            em.mark(ins, sPe)
        block.tensor(_T)
        m_T[i] = em.n(sPe)

        def _stv(v, i=i, p=p):
            v.wait_ge(sPe, m_T[i])
            v.wait_ge(sPe, m_mm[i - 2])
            em.mark(v.tensor_copy(stF2[p][:], tpair[p][:]), sDv)
        block.vector(_stv)
        m_stF[i] = em.n(sDv)

    def emit_mm(i):
        e = i // 8
        p = i % 2
        be = e % 2

        def _emm(t, i=i, e=e, p=p, be=be):
            t.wait_ge(sDv, m_stF[i])
            t.wait_ge(sLdw, ld_w[e])
            t.wait_ge(sGb, g_b[e])
            t.wait_ge(sDv, m_eo[i - 2])
            tA = psA[:, :] if p == 0 else psF[0:128, 0:512]
            tB = psB[:, :] if p == 0 else psF[0:128, 512:1024]
            for k in range(8):
                t.matmul(tA, stF2[p][:, k * 128:(k + 1) * 128],
                         wpair[be][:, k * D:k * D + 512], start=(k == 0), stop=False)
            t.matmul(tA, onesb_sb[0:1, 0:128], b12e_sb[0:1, 0:512],
                     start=False, stop=True)
            for k in range(8):
                t.matmul(tB, stF2[p][:, k * 128:(k + 1) * 128],
                         wpair[be][:, k * D + 512:(k + 1) * D],
                         start=(k == 0), stop=False)
            em.mark(t.matmul(tB, onesb_sb[0:1, 0:128], b12e_sb[0:1, 512:1024],
                             start=False, stop=True), sPe)
        block.tensor(_emm)
        m_mm[i] = em.n(sPe)

        def _eev(v, i=i, p=p):
            v.wait_ge(sPe, m_mm[i])
            v.wait_ge(sSt, st_out[i - 2])
            tA = psA[:, :] if p == 0 else psF[0:128, 0:512]
            tB = psB[:, :] if p == 0 else psF[0:128, 512:1024]
            em.mark(v.tensor_copy(eo2[p][:, 0:512], tA), sDv)
            em.mark(v.tensor_copy(eo2[p][:, 512:1024], tB), sDv)
        block.vector(_eev)
        m_eo[i] = em.n(sDv)

        def _est(sync, i=i, p=p):
            e, r = divmod(i, CPC // 128)
            sync.wait_ge(sDv, m_eo[i])
            em.mark(sync.dma_start(
                out=outstage.ap()[e * CPC + r * 128:e * CPC + (r + 1) * 128, :],
                in_=eo2[p][:]), sSt, 16)
        block.sync(_est)
        st_out[i] = em.n(sSt)

    emit_wld(0)
    emit_bld(0)
    emit_sld(0)
    emit_T(0)
    for i in range(NI):
        e, r = divmod(i, CPC // 128)
        if r == 0 and e > 0:
            emit_wld(e)
            emit_bld(e)
        if i + 1 < NI:
            emit_sld(i + 1)
            emit_T(i + 1)
        emit_mm(i)

    ST_EXP = em.n(sSt)
    DV_EXP = em.n(sDv)
    A2.close()
    AW.close()

    SC = Scope(nc)
    xbuf = [SC.T(f"xbuf{i}", [128, D], F32) for i in range(2)]
    gz2 = [SC.T(f"gz{i}", [128, D], BF16) for i in range(2)]
    z2b = [SC.T(f"z2_{i}", [128, D], F32) for i in range(2)]
    xh2b = [SC.T(f"xh2_{i}", [128, D], BF16) for i in range(2)]
    xhab = [SC.T(f"xha_{i}", [128, D], BF16) for i in range(2)]
    y2b = [SC.T(f"y2_{i}", [128, D], F32) for i in range(2)]
    aob = [SC.T(f"ao_{i}", [128, D], BF16) for i in range(2)]
    outb = [SC.T(f"out_{i}", [128, D], F32) for i in range(2)]
    ln["dummy"] = SC.T("ln_dummy", [128, D], BF16)
    lnB["dummy"] = SC.T("lnB_dummy", [128, D], BF16)

    barrier(block, em, [(sSt, ST_EXP), (sDv, DV_EXP)])

    def _gzinit(v):
        em.mark(v.memset(gz2[0][:], 0.0), sDv)
        em.mark(v.memset(gz2[1][:], 0.0), sDv)
    block.vector(_gzinit)

    m_x = {}
    m_g = {}
    m_z2 = {}
    se2 = {-1: (0, 0), -2: (0, 0)}
    seA = {-1: (0, 0), -2: (0, 0)}
    m_y2 = {}
    m_ya = {}
    m_ao = {}
    m_sum = {-1: 0, -2: 0}
    st_o = {-1: 0, -2: 0}
    for t_i in range(TILES):
        l = t_i // C4
        q = t_i % 2

        def _xl(sync, t_i=t_i, q=q):
            if t_i >= 2:
                sync.wait_ge(sDv, m_sum[t_i - 2])
            em.mark(sync.dma_start(out=xbuf[q][:],
                                   in_=x_in[t_i * 128:(t_i + 1) * 128, :]), sLdX[q], 16)
        block.sync(_xl)
        m_x[t_i] = em.n(sLdX[q])

        def _gth(g, t_i=t_i, q=q):
            g.wait_ge(sSt, ST_EXP)
            if t_i >= 2:
                g.wait_ge(sDv, m_z2[t_i - 2])
            else:
                g.wait_ge(sDv, em.n(sDv))
            em.mark(g.indirect_dma_start(
                out=gz2[q][:],
                out_offset=None,
                in_=outstage.ap(),
                in_offset=IndirectOffsetOnAxis(slotu_sb[:, t_i:t_i + 1], axis=0),
                bounds_check=bc_cell[0],
                oob_is_err=False), sGg, 16)
        block.gpsimd(_gth)
        m_g[t_i] = em.n(sGg)

        def _s1(v, t_i=t_i, q=q):
            v.wait_ge(sGg, m_g[t_i])
            v.wait_ge(sSe, seA[t_i - 2][1])
            v.wait_ge(sDv, seA[t_i - 2][0])
            em.mark(v.scalar_tensor_tensor(
                z2b[q][:], gz2[q][:], gate_sb[:, t_i:t_i + 1],
                y1_sb[:, t_i * D:(t_i + 1) * D], op0=ALU.mult, op1=ALU.add), sDv)
        block.vector(_s1)
        m_z2[t_i] = em.n(sDv)

        se2[t_i] = _ln_pipe(block, em, sDv, sSe, z2b[q][:], xh2b[q][:], ln,
                            pre_dv=m_z2[t_i], pre_se=se2[t_i - 1][1],
                            pre_dv2=se2[t_i - 1][0])

        def _y2(v, t_i=t_i, q=q):
            v.wait_ge(sDv, se2[t_i][0])
            if t_i >= 2:
                v.wait_ge(sDv, m_sum[t_i - 2])
            n0 = em.mark(v.tensor_tensor(y2b[q][:], xh2b[q][:], g2B[:],
                                         op=ALU.mult), sDv)
            v.wait_ge(sDv, n0)
            em.mark(v.tensor_tensor(y2b[q][:], y2b[q][:], b2B[:], op=ALU.add), sDv)
        block.vector(_y2)
        m_y2[t_i] = em.n(sDv)

        def _ya(v, t_i=t_i, q=q, l=l):
            v.wait_ge(sDv, se2[t_i][0])
            v.wait_ge(sSe, se2[t_i][1])
            em.mark(v.tensor_tensor(z2b[q][:], xn_sb[:, t_i * D:(t_i + 1) * D],
                                    fhB_sb[:, l * D:(l + 1) * D], op=ALU.add), sDv)
        block.vector(_ya)
        m_ya[t_i] = em.n(sDv)

        seA[t_i] = _ln_pipe(block, em, sDv, sSe, z2b[q][:], xhab[q][:], lnB,
                            pre_dv=m_ya[t_i], pre_se=seA[t_i - 1][1],
                            pre_dv2=seA[t_i - 1][0])

        def _ao(g, t_i=t_i, q=q):
            g.wait_ge(sDv, seA[t_i][0])
            if t_i >= 2:
                g.wait_ge(sDv, m_sum[t_i - 2])
            n0 = em.mark(g.tensor_tensor(aob[q][:], xhab[q][:], lngB[:],
                                         op=ALU.mult), sPo)
            g.wait_ge(sPo, n0)
            em.mark(g.tensor_tensor(aob[q][:], aob[q][:], lnbB[:], op=ALU.add), sPo)
        block.gpsimd(_ao)
        m_ao[t_i] = em.n(sPo)

        def _sum(v, t_i=t_i, q=q):
            v.wait_ge(sLdX[q], m_x[t_i])
            v.wait_ge(sPo, m_ao[t_i])
            v.wait_ge(sSt, st_o[t_i - 2])
            n0 = em.mark(v.tensor_tensor(outb[q][:], xbuf[q][:], y2b[q][:],
                                         op=ALU.add), sDv)
            v.wait_ge(sDv, n0)
            em.mark(v.tensor_tensor(outb[q][:], outb[q][:], aob[q][:],
                                    op=ALU.add), sDv)
        block.vector(_sum)
        m_sum[t_i] = em.n(sDv)

        def _ost(sync, t_i=t_i, q=q):
            sync.wait_ge(sDv, m_sum[t_i])
            em.mark(sync.dma_start(out=o_out[t_i * 128:(t_i + 1) * 128, :],
                                   in_=outb[q][:]), sSt, 16)
        block.sync(_ost)
        st_o[t_i] = em.n(sSt)

    def _end(sync):
        sync.wait_ge(sSt, em.n(sSt))
    block.sync(_end)

    SC.close()
    G.close()
    es.close()
    return nc



def _consts():
    idf = np.eye(128, dtype=np.float32)
    idb = np.eye(128, dtype=BF16NP)
    onesrow_b = np.ones((1, 128), dtype=BF16NP)
    onesrow_f = np.ones((1, 128), dtype=np.float32)
    onescol_f = np.ones((128, 1), dtype=np.float32)
    iotaE = np.tile(np.arange(E, dtype=np.float32), (128, 1))
    ltri = np.triu(np.ones((128, 128), dtype=np.float32), 1)
    return idf, idb, onesrow_b, onesrow_f, onescol_f, iotaE, ltri


def kernel(L0_attn_h, L0_attn_Wl, L0_attn_Wr, L0_attn_att_l, L0_attn_att_r,
           L0_attn_Wf, L0_attn_ln_gamma, L0_attn_ln_beta, L0_ffn_in_proj_w,
           L0_ffn_in_proj_b, L0_ffn_out_proj_w, L0_ffn_out_proj_b, L0_ffn_ln1_g,
           L0_ffn_ln1_b, L0_ffn_ln2_g, L0_ffn_ln2_b, L0_ffn_wg, L0_ffn_w1,
           L0_ffn_b1, L0_ffn_w2, L0_ffn_b2):
    f32 = np.float32
    x = np.asarray(L0_attn_h, f32)
    idf, idb, onesrow_b, onesrow_f, onescol_f, iotaE, ltri = _consts()

    wrT = np.ascontiguousarray(np.asarray(L0_attn_Wr, f32).T)
    attr = np.ascontiguousarray(np.asarray(L0_attn_att_r, f32).reshape(1, 128))
    ipT = np.ascontiguousarray(np.asarray(L0_ffn_in_proj_w, f32).T)
    ipb = np.asarray(L0_ffn_in_proj_b, f32).reshape(1, 3 * D)
    opT = np.ascontiguousarray(np.asarray(L0_ffn_out_proj_w, f32).T)
    opb = np.asarray(L0_ffn_out_proj_b, f32).reshape(1, D)
    g1 = np.asarray(L0_ffn_ln1_g, f32).reshape(1, D)
    b1r = np.asarray(L0_ffn_ln1_b, f32).reshape(1, D)
    wgA = np.ascontiguousarray(np.asarray(L0_ffn_wg, f32))
    w1 = np.asarray(L0_ffn_w1, f32)
    b1 = np.asarray(L0_ffn_b1, f32)
    w2 = np.asarray(L0_ffn_w2, f32)
    b2 = np.asarray(L0_ffn_b2, f32)
    wfT = np.ascontiguousarray(np.asarray(L0_attn_Wf, f32).T)
    lng = np.asarray(L0_attn_ln_gamma, f32).reshape(1, D)
    lnb = np.asarray(L0_attn_ln_beta, f32).reshape(1, D)
    g2r = np.asarray(L0_ffn_ln2_g, f32).reshape(1, D)
    b2r = np.asarray(L0_ffn_ln2_b, f32).reshape(1, D)

    xs = [np.ascontiguousarray(x[:, c * NLOC:(c + 1) * NLOC, :].reshape(TOK, D))
          for c in range(CORES)]

    nc1 = build_launch1()
    in_maps1 = []
    for c in range(CORES):
        in_maps1.append({
            "x": xs[c], "wrT": wrT, "attr": attr, "ipT": ipT, "ipb": ipb,
            "opT": opT, "opb": opb, "g1": g1, "b1r": b1r, "wg": wgA,
            "w1T": np.ascontiguousarray(w1[c].T),
            "w2e": np.ascontiguousarray(w2[c]),
            "b1e": np.ascontiguousarray(b1[c].reshape(FF // 128, 128).T),
            "b2e": b2[c].reshape(1, D),
            "idf": idf, "idb": idb, "onesrow_b": onesrow_b,
            "onesrow_f": onesrow_f, "onescol_f": onescol_f, "iotaE": iotaE,
        })
    res1 = run_bass_kernel_spmd(nc1, in_maps1, core_ids=list(range(CORES)))
    r1 = res1.results

    mz_all = np.stack([
        np.concatenate([np.asarray(r1[c]["o_m"], f32).T.reshape(-1),
                        np.asarray(r1[c]["o_z"], f32).T.reshape(-1)])
        for c in range(CORES)], axis=0).astype(f32)
    s_all = np.ascontiguousarray(
        np.stack([np.asarray(r1[c]["o_s"], f32).reshape(-1) for c in range(CORES)]))
    cnt_all = np.ascontiguousarray(
        np.stack([np.asarray(r1[c]["o_cnt"], f32).reshape(L, E)
                  for c in range(CORES)]).reshape(CORES * L, E))
    w12_all = np.ascontiguousarray(
        np.concatenate([np.asarray(r1[c]["o_w12"]) for c in range(CORES)], axis=0))
    b12_all = np.ascontiguousarray(
        np.stack([np.asarray(r1[c]["o_b12"], f32).reshape(-1) for c in range(CORES)]))

    nc2 = build_launch2()
    in_maps2 = []
    for c in range(CORES):
        wcm = np.zeros((CORES * L, L), dtype=f32)
        for cp in range(CORES):
            for lp in range(L):
                for l in range(L):
                    if lp < l or (lp == l and cp < c):
                        wcm[cp * L + lp, l] = 1.0
        in_maps2.append({
            "x": xs[c],
            "xn": np.asarray(r1[c]["o_xn"]),
            "y1": np.asarray(r1[c]["o_y1"]),
            "idxv": np.asarray(r1[c]["o_idx"], f32),
            "gatev": np.asarray(r1[c]["o_gate"], f32),
            "mzall": mz_all, "sall": s_all, "cntall": cnt_all, "wcm": wcm,
            "w12all": w12_all, "b12all": b12_all,
            "wfT": wfT, "lng": lng, "lnb": lnb, "g2r": g2r, "b2r": b2r,
            "idf": idf, "idb": idb, "onesrow_b": onesrow_b,
            "onesrow_f": onesrow_f, "onescol_f": onescol_f,
            "ltri": ltri, "iotaE": iotaE,
            "lnconsts": np.tile(np.array([[EPS, -0.5, -1.0 / D, 1.0 / D]],
                                         dtype=f32), (128, 1)),
        })
    res2 = run_bass_kernel_spmd(nc2, in_maps2, core_ids=list(range(CORES)))

    out = np.empty((L, NSEQ, D), dtype=f32)
    for c in range(CORES):
        oc = np.asarray(res2.results[c]["o_out"], f32).reshape(L, NLOC, D)
        out[:, c * NLOC:(c + 1) * NLOC, :] = oc
    return out


# version 08
# speedup vs baseline: 1.2282x; 1.1270x over previous
import numpy as np
import ml_dtypes
import concourse.bass as bass
import concourse.mybir as mybir
from concourse.bass import IndirectOffsetOnAxis
from concourse.bass_utils import run_bass_kernel_spmd

F32 = mybir.dt.float32
BF16 = mybir.dt.bfloat16
U32 = mybir.dt.uint32
AF = mybir.ActivationFunctionType
ALU = mybir.AluOpType
AX = mybir.AxisListType
BF16NP = ml_dtypes.bfloat16

CORES = 8
D = 1024
L = 8
NSEQ = 4096
NLOC = NSEQ // CORES
TOK = L * NLOC
TILES = TOK // 128
C4 = 4
HA = 8
HM = 4
E = 8
FF = 4096
CAP = 4096
CPC = 1024
EPS = 1e-5
OOB = float(1 << 24)


class Emit:
    def __init__(self):
        self.cnt = {}

    def mark(self, instr, sem, k=1):
        instr.then_inc(sem, k)
        self.cnt[sem.name] = self.cnt.get(sem.name, 0) + k
        return self.cnt[sem.name]

    def n(self, sem):
        return self.cnt.get(sem.name, 0)


class Scope:

    def __init__(self, nc):
        self.nc = nc
        self.cms = []

    def T(self, name, shape, dt):
        cm = self.nc.sbuf_tensor(name, shape, dt)
        t = cm.__enter__()
        self.cms.append(cm)
        return t

    def P(self, name, shape, dt):
        cm = self.nc.psum_tensor(name, shape, dt)
        t = cm.__enter__()
        self.cms.append(cm)
        return t

    def close(self):
        for cm in reversed(self.cms):
            cm.__exit__(None, None, None)
        self.cms = []


def barrier(block, em, sems_waits):
    def _b(e):
        for sem, val in sems_waits:
            e.wait_ge(sem, val)
    block.vector(_b)
    block.scalar(_b)
    block.tensor(_b)
    block.sync(_b)
    block.gpsimd(_b)


def _ln_chain(block, em, sDv, sSe, src, xhat_out, ln):
    def _v1(v):
        v.wait_ge(sDv, em.n(sDv))
        n0 = em.mark(v.reduce_sum(ln["sx"][:], src, axis=AX.X), sDv)
        v.wait_ge(sDv, n0)
        em.mark(v.tensor_scalar(ln["negmu"][:], ln["sx"][:], -1.0 / D, None, op0=ALU.mult), sDv)
    block.vector(_v1)

    def _s1(s):
        s.wait_ge(sDv, em.n(sDv))
        em.mark(s.activation(ln["dummy"][:], src, AF.Square, bias=ln["negmu"][:, 0:1],
                             scale=1.0, accum_out=ln["svar"][:]), sSe)
    block.scalar(_s1)

    def _v2(v):
        v.wait_ge(sSe, em.n(sSe))
        em.mark(v.tensor_scalar(ln["vep"][:], ln["svar"][:], 1.0 / D, EPS,
                                op0=ALU.mult, op1=ALU.add), sDv)
    block.vector(_v2)

    def _s2a(s):
        s.wait_ge(sDv, em.n(sDv))
        em.mark(s.activation(ln["rstd"][:], ln["vep"][:], AF.Ln, bias=0.0, scale=1.0), sSe)
    block.scalar(_s2a)

    def _v2b(v):
        v.wait_ge(sSe, em.n(sSe))
        em.mark(v.tensor_scalar(ln["rstd"][:], ln["rstd"][:], -0.5, None,
                                op0=ALU.mult), sDv)
    block.vector(_v2b)

    def _s2b(s):
        s.wait_ge(sDv, em.n(sDv))
        em.mark(s.activation(ln["rstd"][:], ln["rstd"][:], AF.Exp, bias=0.0, scale=1.0), sSe)
    block.scalar(_s2b)

    def _v3(v):
        v.wait_ge(sSe, em.n(sSe))
        em.mark(v.tensor_scalar(ln["nmr"][:], ln["negmu"][:], ln["rstd"][:, 0:1], None,
                                op0=ALU.mult), sDv)
    block.vector(_v3)

    def _s3(s):
        s.wait_ge(sDv, em.n(sDv))
        em.mark(s.activation(xhat_out, src, AF.Identity, bias=ln["nmr"][:, 0:1],
                             scale=ln["rstd"][:, 0:1]), sSe)
    block.scalar(_s3)
    return em.n(sSe)


def _ln_pipe(block, em, sDv, sSe, src, xhat_out, ln, pre_dv=None, pre_se=None,
             pre_dv2=None):
    def _v1(v):
        if pre_dv is not None:
            v.wait_ge(sDv, pre_dv)
        if pre_dv2 is not None:
            v.wait_ge(sDv, pre_dv2)
        if pre_se is not None:
            v.wait_ge(sSe, pre_se)
        n0 = em.mark(v.reduce_sum(ln["sx"][:], src, axis=AX.X), sDv)
        v.wait_ge(sDv, n0)
        em.mark(v.tensor_scalar(ln["negmu"][:], ln["sx"][:], -1.0 / D, None,
                                op0=ALU.mult), sDv)
    block.vector(_v1)
    n_mu = em.n(sDv)

    def _s(s):
        s.wait_ge(sDv, n_mu)
        n2 = em.mark(s.activation(ln["dummy"][:], src, AF.Square,
                                  bias=ln["negmu"][:, 0:1], scale=1.0,
                                  accum_out=ln["svar"][:]), sSe)
        s.wait_ge(sSe, n2)
        n3 = em.mark(s.activation(ln["vep"][:], ln["svar"][:], AF.Ln,
                                  bias=EPS, scale=1.0 / D), sSe)
        s.wait_ge(sSe, n3)
        em.mark(s.activation(ln["rstd"][:], ln["vep"][:], AF.Exp,
                             bias=0.0, scale=-0.5), sSe)
    block.scalar(_s)
    n_rstd = em.n(sSe)

    def _v2(v):
        v.wait_ge(sSe, n_rstd)
        em.mark(v.tensor_scalar(xhat_out, src, ln["negmu"][:, 0:1],
                                ln["rstd"][:, 0:1], op0=ALU.add, op1=ALU.mult), sDv)
    block.vector(_v2)
    return em.n(sDv), n_rstd



def build_launch1(phases=4):
    nc = bass.Bass("TRN2", target_bir_lowering=False)

    x_in = nc.dram_tensor("x", [TOK, D], F32, kind="ExternalInput").ap()
    wrT = nc.dram_tensor("wrT", [D, D], F32, kind="ExternalInput").ap()
    attr = nc.dram_tensor("attr", [1, 128], F32, kind="ExternalInput").ap()
    ipT = nc.dram_tensor("ipT", [D, 3 * D], F32, kind="ExternalInput").ap()
    ipb = nc.dram_tensor("ipb", [1, 3 * D], F32, kind="ExternalInput").ap()
    opT = nc.dram_tensor("opT", [D, D], F32, kind="ExternalInput").ap()
    opb = nc.dram_tensor("opb", [1, D], F32, kind="ExternalInput").ap()
    g1 = nc.dram_tensor("g1", [1, D], F32, kind="ExternalInput").ap()
    b1r = nc.dram_tensor("b1r", [1, D], F32, kind="ExternalInput").ap()
    wg = nc.dram_tensor("wg", [D, E], F32, kind="ExternalInput").ap()
    w1T = nc.dram_tensor("w1T", [FF, D], F32, kind="ExternalInput").ap()
    w2e = nc.dram_tensor("w2e", [FF, D], F32, kind="ExternalInput").ap()
    b1e = nc.dram_tensor("b1e", [128, FF // 128], F32, kind="ExternalInput").ap()
    b2e = nc.dram_tensor("b2e", [1, D], F32, kind="ExternalInput").ap()
    idf = nc.dram_tensor("idf", [128, 128], F32, kind="ExternalInput").ap()
    idb = nc.dram_tensor("idb", [128, 128], BF16, kind="ExternalInput").ap()
    onesrow_b = nc.dram_tensor("onesrow_b", [1, 128], BF16, kind="ExternalInput").ap()
    onesrow_f = nc.dram_tensor("onesrow_f", [1, 128], F32, kind="ExternalInput").ap()
    onescol_f = nc.dram_tensor("onescol_f", [128, 1], F32, kind="ExternalInput").ap()
    iotaE_in = nc.dram_tensor("iotaE", [128, E], F32, kind="ExternalInput").ap()

    o_w12 = nc.dram_tensor("o_w12", [D, D], BF16, kind="ExternalOutput").ap()
    o_b12 = nc.dram_tensor("o_b12", [1, D], F32, kind="ExternalOutput").ap()
    o_m = nc.dram_tensor("o_m", [HA, L], F32, kind="ExternalOutput").ap()
    o_z = nc.dram_tensor("o_z", [HA, L], F32, kind="ExternalOutput").ap()
    o_s = nc.dram_tensor("o_s", [1, L * D], F32, kind="ExternalOutput").ap()
    o_cnt = nc.dram_tensor("o_cnt", [1, L * E], F32, kind="ExternalOutput").ap()
    o_y1 = nc.dram_tensor("o_y1", [TOK, D], BF16, kind="ExternalOutput").ap()
    o_xn = nc.dram_tensor("o_xn", [TOK, D], BF16, kind="ExternalOutput").ap()
    o_idx = nc.dram_tensor("o_idx", [128, TILES], F32, kind="ExternalOutput").ap()
    o_gate = nc.dram_tensor("o_gate", [128, TILES], F32, kind="ExternalOutput").ap()
    xnT_d = nc.dram_tensor("xnT_d", [TOK, D], BF16, kind="Internal").ap()

    from contextlib import ExitStack
    es = ExitStack()

    G = Scope(nc)
    idf_sb = G.T("idf_sb", [128, 128], F32)
    idb_sb = G.T("idb_sb", [128, 128], BF16)
    onesb_sb = G.T("onesb_sb", [1, 128], BF16)
    onesf_sb = G.T("onesf_sb", [1, 128], F32)
    onescol_sb = G.T("onescol_sb", [128, 1], F32)
    iotaE_sb = G.T("iotaE_sb", [128, E], F32)
    attrB_sb = G.T("attrB_sb", [128, 128], BF16)
    wgp_sb = G.T("wgp_sb", [128, 8 * E], F32)
    cvec_sb = G.T("cvec_sb", [1, E], F32)
    g1B_sb = G.T("g1B_sb", [128, D], BF16)
    b1B_sb = G.T("b1B_sb", [128, D], BF16)
    idx_sb = G.T("idx_sb", [128, TILES], F32)
    gate_sb = G.T("gate_sb", [128, TILES], F32)
    cnt_sb = G.T("cnt_sb", [1, L * E], F32)

    psA = G.P("psA", [128, 512], F32)
    psB = G.P("psB", [128, 512], F32)
    psF = G.P("psF", [128, D], F32)

    sLd = es.enter_context(nc.semaphore("sLd"))
    sGw2 = es.enter_context(nc.semaphore("sGw2"))
    sGw2b = es.enter_context(nc.semaphore("sGw2b"))
    sGw1 = [es.enter_context(nc.semaphore(f"sGw1{i}")) for i in range(2)]
    sGrows = es.enter_context(nc.semaphore("sGrows"))
    sGwr = es.enter_context(nc.semaphore("sGwr"))
    sGq = es.enter_context(nc.semaphore("sGq"))
    sLdX = [es.enter_context(nc.semaphore(f"sLdX{i}")) for i in range(3)]
    sSxn = [es.enter_context(nc.semaphore(f"sSxn{i}")) for i in range(2)]
    sSxt = es.enter_context(nc.semaphore("sSxt"))
    sLxt = [es.enter_context(nc.semaphore(f"sLxt{i}")) for i in range(2)]
    sLxn = [es.enter_context(nc.semaphore(f"sLxn{i}")) for i in range(2)]
    sSy1 = es.enter_context(nc.semaphore("sSy1"))
    sSt = es.enter_context(nc.semaphore("sSt"))
    sPe = es.enter_context(nc.semaphore("sPe"))
    sDv = es.enter_context(nc.semaphore("sDv"))
    sSe = es.enter_context(nc.semaphore("sSe"))
    sGp = es.enter_context(nc.semaphore("sGp"))
    block = es.enter_context(nc.Block())
    em = Emit()

    S0 = Scope(nc)
    w2_sb = S0.T("w2_sb", [128, 32 * D], BF16)
    w1m_sb = [S0.T(f"w1m{i}", [128, FF], BF16) for i in range(2)]
    b1c_sb = S0.T("b1c_sb", [128, FF // 128], BF16)
    w12_sb = S0.T("w12_sb", [128, D], BF16)
    b12_sb = S0.T("b12_sb", [1, D], F32)
    b2_sb = S0.T("b2_sb", [1, D], F32)
    g1f_sb = S0.T("g1f_sb", [1, D], F32)
    b1f_sb = S0.T("b1f_sb", [1, D], F32)
    wg_sb = S0.T("wg_sb", [128, 8 * E], F32)
    col_sb = S0.T("col_sb", [128, 8], F32)
    g1row_sb = S0.T("g1row_sb", [1, D], BF16)
    b1row_sb = S0.T("b1row_sb", [1, D], BF16)
    attrow_sb = S0.T("attrow_sb", [1, 128], BF16)

    def _ld(sync):
        for dst, src in ((idf_sb[:], idf), (idb_sb[:], idb), (onesb_sb[:], onesrow_b),
                         (onesf_sb[:], onesrow_f), (onescol_sb[:], onescol_f),
                         (iotaE_sb[:], iotaE_in), (b2_sb[:], b2e),
                         (g1f_sb[:], g1), (b1f_sb[:], b1r)):
            em.mark(sync.dma_start(out=dst, in_=src), sLd, 16)
        for k in range(8):
            em.mark(sync.dma_start(out=wg_sb[:, k * E:(k + 1) * E],
                                   in_=wg[k * 128:(k + 1) * 128, :]), sLd, 16)
    block.sync(_ld)
    LD_PRE = em.n(sLd)

    def _grows(g):
        em.mark(g.dma_start(out=g1row_sb[:], in_=g1), sGrows, 16)
        em.mark(g.dma_start(out=b1row_sb[:], in_=b1r), sGrows, 16)
        em.mark(g.dma_start(out=attrow_sb[:], in_=attr), sGrows, 16)
    block.gpsimd(_grows)
    G_ROWS = em.n(sGrows)

    def _bc(t):
        t.wait_ge(sGrows, G_ROWS)
        t.wait_ge(sLd, LD_PRE)
        em.mark(t.matmul(psF[:, 0:128], onesb_sb[0:1, :], attrow_sb[0:1, :],
                         start=True, stop=True), sPe)
    block.tensor(_bc)

    def _bcv(v):
        v.wait_ge(sPe, em.n(sPe))
        em.mark(v.tensor_copy(attrB_sb[:], psF[:, 0:128]), sDv)
    block.vector(_bcv)

    for dst, srcrow in ((g1B_sb, g1row_sb), (b1B_sb, b1row_sb)):
        def _bct(t, dst=dst, srcrow=srcrow):
            t.wait_ge(sDv, em.n(sDv))
            t.matmul(psF[:, 0:512], onesb_sb[0:1, :], srcrow[0:1, 0:512],
                     start=True, stop=True)
            em.mark(t.matmul(psF[:, 512:1024], onesb_sb[0:1, :], srcrow[0:1, 512:1024],
                             start=True, stop=True), sPe)
        block.tensor(_bct)

        def _bcv2(v, dst=dst):
            v.wait_ge(sPe, em.n(sPe))
            em.mark(v.tensor_copy(dst[:], psF[:]), sDv)
        block.vector(_bcv2)

    def _g1c(t):
        t.wait_ge(sDv, em.n(sDv))
        for k in range(8):
            t.transpose(psF[:, k:k + 1], g1f_sb[0:1, k * 128:(k + 1) * 128],
                        idf_sb[0:1, 0:1])
        for k in range(8):
            i = t.transpose(psF[:, 8 + k:9 + k], b1f_sb[0:1, k * 128:(k + 1) * 128],
                            idf_sb[0:1, 0:1])
        em.mark(i, sPe)
    block.tensor(_g1c)

    def _g1cv(v):
        v.wait_ge(sPe, em.n(sPe))
        n0 = em.mark(v.tensor_copy(col_sb[:], psF[:, 0:8]), sDv)
        v.wait_ge(sDv, n0)
        for k in range(8):
            em.mark(v.tensor_scalar(wgp_sb[:, k * E:(k + 1) * E],
                                    wg_sb[:, k * E:(k + 1) * E],
                                    col_sb[:, k:k + 1], None, op0=ALU.mult), sDv)
    block.vector(_g1cv)

    def _b1cv(v):
        v.wait_ge(sDv, em.n(sDv))
        em.mark(v.tensor_copy(col_sb[:], psF[:, 8:16]), sDv)
    block.vector(_b1cv)

    def _cvec(t):
        t.wait_ge(sDv, em.n(sDv))
        for k in range(8):
            i = t.matmul(psB[0:1, 0:E], col_sb[:, k:k + 1], wg_sb[:, k * E:(k + 1) * E],
                         start=(k == 0), stop=(k == 7))
        em.mark(i, sPe)
    block.tensor(_cvec)

    def _cvecv(v):
        v.wait_ge(sPe, em.n(sPe))
        em.mark(v.tensor_copy(cvec_sb[:], psB[0:1, 0:E]), sDv)
    block.vector(_cvecv)

    DV_BC = em.n(sDv)

    m_M = {}
    m_w12ev = {}
    m_w12st = {}
    g_w1 = {}
    DV_B12 = [0]

    def _gw2a(g):
        for k in range(16):
            em.mark(g.dma_start(out=w2_sb[:, k * D:(k + 1) * D],
                                in_=w2e[k * 128:(k + 1) * 128, :]), sGw2, 16)
    block.gpsimd(_gw2a)
    G_W2A = em.n(sGw2)

    def _gw10(g):
        em.mark(g.dma_start(
            out=w1m_sb[0][:].rearrange("p (k i) -> p k i", k=32),
            in_=w1T[:, 0:128].rearrange("(k p) i -> p k i", p=128)), sGw1[0], 16)
    block.gpsimd(_gw10)
    g_w1[0] = em.n(sGw1[0])

    def _gw2b(g):
        for k in range(16, 32):
            em.mark(g.dma_start(out=w2_sb[:, k * D:(k + 1) * D],
                                in_=w2e[k * 128:(k + 1) * 128, :]), sGw2b, 16)
        em.mark(g.dma_start(out=b1c_sb[:], in_=b1e), sGw2b, 16)
    block.gpsimd(_gw2b)
    G_W2B = em.n(sGw2b)

    def _gw11(g):
        em.mark(g.dma_start(
            out=w1m_sb[1][:].rearrange("p (k i) -> p k i", k=32),
            in_=w1T[:, 128:256].rearrange("(k p) i -> p k i", p=128)), sGw1[1], 16)
    block.gpsimd(_gw11)
    g_w1[1] = em.n(sGw1[1])

    for mI in range(8):
        def _mm(tt, mI=mI):
            tt.wait_ge(sGw2, G_W2A)
            tt.wait_ge(sGw1[mI % 2], g_w1[mI])
            if mI - 1 >= 0:
                tt.wait_ge(sSe, m_w12ev[mI - 1])
            for K in range(32):
                if K == 16:
                    tt.wait_ge(sGw2b, G_W2B)
                tt.matmul(psF[:, 0:512], w1m_sb[mI % 2][:, K * 128:(K + 1) * 128],
                          w2_sb[:, K * D:K * D + 512], start=(K == 0), stop=(K == 31))
            for K in range(32):
                i = tt.matmul(psF[:, 512:1024],
                              w1m_sb[mI % 2][:, K * 128:(K + 1) * 128],
                              w2_sb[:, K * D + 512:(K + 1) * D],
                              start=(K == 0), stop=(K == 31))
            em.mark(i, sPe)
            m_M[mI] = em.n(sPe)
        block.tensor(_mm)

        if mI + 2 < 8:
            def _gw1n(gq, mI=mI):
                gq.wait_ge(sPe, m_M[mI])
                em.mark(gq.dma_start(
                    out=w1m_sb[mI % 2][:].rearrange("p (k i) -> p k i", k=32),
                    in_=w1T[:, (mI + 2) * 128:(mI + 3) * 128].rearrange(
                        "(k p) i -> p k i", p=128)), sGw1[mI % 2], 16)
                g_w1[mI + 2] = em.n(sGw1[mI % 2])
            block.gpsimd(_gw1n)

        ST_NOW = em.n(sSt)

        def _w12e(s, mI=mI, ST_NOW=ST_NOW):
            s.wait_ge(sPe, m_M[mI])
            if mI - 1 >= 0:
                s.wait_ge(sSt, ST_NOW)
            em.mark(s.activation(w12_sb[:], psF[:], AF.Identity,
                                 bias=0.0, scale=1.0), sSe)
            m_w12ev[mI] = em.n(sSe)
        block.scalar(_w12e)

        def _w12s(sync, mI=mI):
            sync.wait_ge(sSe, m_w12ev[mI])
            em.mark(sync.dma_start(out=o_w12[mI * 128:(mI + 1) * 128, :],
                                   in_=w12_sb[:]), sSt, 16)
            m_w12st[mI] = em.n(sSt)
        block.sync(_w12s)

        if mI == 0:
            def _b12(tt):
                tt.wait_ge(sPe, m_M[0])
                tt.wait_ge(sDv, DV_BC)
                for K in range(32):
                    tt.matmul(psA[0:1, 0:512], b1c_sb[:, K:K + 1],
                              w2_sb[:, K * D:K * D + 512],
                              start=(K == 0), stop=(K == 31), skip_group_check=True)
                for K in range(32):
                    i = tt.matmul(psB[0:1, 0:512], b1c_sb[:, K:K + 1],
                                  w2_sb[:, K * D + 512:(K + 1) * D],
                                  start=(K == 0), stop=(K == 31),
                                  skip_group_check=True)
                em.mark(i, sPe)
            block.tensor(_b12)
            PE_B12 = em.n(sPe)

            def _b12v(v, PE_B12=PE_B12):
                v.wait_ge(sPe, PE_B12)
                v.tensor_tensor(b12_sb[0:1, 0:512], psA[0:1, 0:512],
                                b2_sb[0:1, 0:512], op=ALU.add)
                em.mark(v.tensor_tensor(b12_sb[0:1, 512:1024], psB[0:1, 0:512],
                                        b2_sb[0:1, 512:1024], op=ALU.add), sDv)
            block.vector(_b12v)
            DV_B12[0] = em.n(sDv)

            def _b12s(sync):
                sync.wait_ge(sDv, DV_B12[0])
                em.mark(sync.dma_start(out=o_b12, in_=b12_sb[:]), sSt, 16)
            block.sync(_b12s)

    barrier(block, em, [(sPe, em.n(sPe)), (sSe, em.n(sSe)), (sDv, em.n(sDv)),
                        (sSt, em.n(sSt)), (sGw1[0], em.n(sGw1[0])),
                        (sGw1[1], em.n(sGw1[1])), (sGw2, em.n(sGw2)),
                        (sGw2b, em.n(sGw2b)), (sLd, LD_PRE), (sGrows, G_ROWS)])
    S0.close()

    if phases <= 1:
        def _finP0(sync):
            sync.wait_ge(sSt, em.n(sSt))
        block.sync(_finP0)
        G.close()
        es.close()
        return nc

    SA = Scope(nc)
    wrT_sb = SA.T("wrT_sb", [128, 8 * D], BF16)
    fr_sb = SA.T("fr_sb", [128, TILES * D], BF16)
    srT_sb = SA.T("srT_sb", [HA, TOK], F32)
    xb = [SA.T(f"xb{i}", [128, D], F32) for i in range(6)]
    xno = [SA.T(f"xno{i}", [128, D], BF16) for i in range(2)]
    xnF_sb = SA.T("xnF_sb", [128, D], BF16)
    bf2k = SA.T("bf2k", [128, D], BF16)
    bnsc = SA.T("bnsc", [128, 12], F32)
    mv_sb = SA.T("mv_sb", [128, 2], F32)
    negmuC = SA.T("negmuC", [128, TILES], F32)
    vepC = SA.T("vepC", [128, TILES], F32)
    rstdC = SA.T("rstdC", [128, TILES], F32)
    sr4 = SA.T("sr4", [128, 32], F32)
    e4_sb = SA.T("e4_sb", [128, TILES * HA], BF16)
    mloc_sb = SA.T("mloc_sb", [HA, L], F32)
    negm_sb = SA.T("negm_sb", [HA, L], F32)
    zloc_sb = SA.T("zloc_sb", [HA, L], F32)
    Sst = [SA.T(f"Sst{i}", [1, D], F32) for i in range(2)]
    psTp = [SA.P("psTa", [128, D], BF16), SA.P("psTb", [128, D], BF16)]
    psS = SA.P("psS", [1, D], F32)

    def _gwr(g):
        for k in range(8):
            em.mark(g.dma_start(out=wrT_sb[:, k * D:(k + 1) * D],
                                in_=wrT[k * 128:(k + 1) * 128, :]), sGwr, 16)
    block.gpsimd(_gwr)
    G_WR = em.n(sGwr)

    m_xld = {}
    m_xhat = {}
    m_sxn = {}
    m_sums = {}
    m_trF = {}
    m_fr = {}
    m_evF = {}
    m_evAB = {}
    m_sxt = {}
    m_sr = {}
    m_rstd = {}
    m_srTev = {}

    def _x0(sync):
        for t in range(2):
            em.mark(sync.dma_start(out=xb[t][:], in_=x_in[t * 128:(t + 1) * 128, :]),
                    sLdX[t % 3], 16)
            m_xld[t] = em.n(sLdX[t % 3])
    block.sync(_x0)

    for g in range(8):
        for t in range(4 * g, 4 * g + 4):
            def _pft(sync, t=t):
                tp = t + 2
                if tp < TILES:
                    if tp - 6 >= 0:
                        sync.wait_ge(sDv, m_xhat[tp - 6])
                    if tp - 3 >= 0:
                        sync.wait_ge(sLdX[tp % 3], m_xld[tp - 3])
                    em.mark(sync.dma_start(out=xb[tp % 6][:],
                                           in_=x_in[tp * 128:(tp + 1) * 128, :]),
                            sLdX[tp % 3], 16)
                    m_xld[tp] = em.n(sLdX[tp % 3])
            block.sync(_pft)

            def _sums(v, t=t):
                if t == 0:
                    v.wait_ge(sDv, DV_BC)
                v.wait_ge(sLdX[t % 3], m_xld[t])
                if t - 1 >= 0:
                    v.wait_ge(sDv, m_sums[t - 1])
                v.bn_stats(bnsc[:, 0:6], xb[t % 6][:, 0:512])
                n0 = em.mark(v.bn_stats(bnsc[:, 6:12], xb[t % 6][:, 512:1024]), sDv)
                v.wait_ge(sDv, n0)
                n1 = em.mark(v.bn_aggr(mv_sb[:], bnsc[:]), sDv)
                v.wait_ge(sDv, n1)
                v.tensor_scalar(negmuC[:, t:t + 1], mv_sb[:, 0:1], -1.0, None,
                                op0=ALU.mult)
                em.mark(v.tensor_scalar(vepC[:, t:t + 1], mv_sb[:, 1:2], EPS, None,
                                        op0=ALU.add), sDv)
                m_sums[t] = em.n(sDv)
            block.vector(_sums)

        def _ln(s, g=g):
            s.wait_ge(sDv, m_sums[4 * g + 3])
            em.mark(s.activation(rstdC[:, 4 * g:4 * g + 4], vepC[:, 4 * g:4 * g + 4],
                                 AF.Ln, bias=0.0, scale=1.0), sSe)
        block.scalar(_ln)

        def _lnv(v, g=g):
            v.wait_ge(sSe, em.n(sSe))
            em.mark(v.tensor_scalar(rstdC[:, 4 * g:4 * g + 4],
                                    rstdC[:, 4 * g:4 * g + 4], -0.5, None,
                                    op0=ALU.mult), sDv)
        block.vector(_lnv)

        def _exp(s, g=g):
            s.wait_ge(sDv, em.n(sDv))
            em.mark(s.activation(rstdC[:, 4 * g:4 * g + 4], rstdC[:, 4 * g:4 * g + 4],
                                 AF.Exp, bias=0.0, scale=1.0), sSe)
            m_rstd[g] = em.n(sSe)
        block.scalar(_exp)

        for t in range(4 * g, 4 * g + 4):
            def _xh(v, t=t, g=g):
                v.wait_ge(sSe, m_rstd[g])
                if t - 2 >= 0:
                    v.wait_ge(sPe, m_trF[t - 2])
                    v.wait_ge(sSxn[t % 2], m_sxn[t - 2])
                em.mark(v.tensor_scalar(xno[t % 2][:], xb[t % 6][:],
                                        negmuC[:, t:t + 1], rstdC[:, t:t + 1],
                                        op0=ALU.add, op1=ALU.mult), sDv)
                m_xhat[t] = em.n(sDv)
            block.vector(_xh)

            def _st_xn(sync, t=t):
                sync.wait_ge(sDv, m_xhat[t])
                em.mark(sync.dma_start(out=o_xn[t * 128:(t + 1) * 128, :],
                                       in_=xno[t % 2][:]), sSxn[t % 2], 16)
                m_sxn[t] = em.n(sSxn[t % 2])
            block.sync(_st_xn)

            def _tr(tt, t=t):
                tt.wait_ge(sDv, m_xhat[t])
                if t - 2 >= 0:
                    tt.wait_ge(sSe, m_evF[t - 2])
                for k in range(8):
                    i = tt.transpose(psTp[t % 2][:, k * 128:(k + 1) * 128],
                                     xno[t % 2][:, k * 128:(k + 1) * 128], idb_sb[:])
                em.mark(i, sPe)
                m_trF[t] = em.n(sPe)
            block.tensor(_tr)

            def _evF(s, t=t):
                s.wait_ge(sPe, m_trF[t])
                if t - 1 >= 0:
                    s.wait_ge(sPe, m_fr[t - 1])
                    s.wait_ge(sSxt, m_sxt[t - 1])
                em.mark(s.activation(xnF_sb[:], psTp[t % 2][:], AF.Identity,
                                     bias=0.0, scale=1.0), sSe)
                m_evF[t] = em.n(sSe)
            block.scalar(_evF)

            def _st_xt(sync, t=t):
                sync.wait_ge(sSe, m_evF[t])
                em.mark(sync.dma_start(out=xnT_d[t * 128:(t + 1) * 128, :],
                                       in_=xnF_sb[:]), sSxt, 16)
                m_sxt[t] = em.n(sSxt)
            block.sync(_st_xt)

            def _fr(tt, t=t):
                tt.wait_ge(sGwr, G_WR)
                tt.wait_ge(sSe, m_evF[t])
                if t - 1 >= 0:
                    tt.wait_ge(sSe, m_evAB[t - 1])
                else:
                    tt.wait_ge(sDv, DV_B12[0])
                for k in range(8):
                    tt.matmul(psA[:, :], xnF_sb[:, k * 128:(k + 1) * 128],
                              wrT_sb[:, k * D:k * D + 512], start=(k == 0), stop=(k == 7))
                for k in range(8):
                    i = tt.matmul(psB[:, :], xnF_sb[:, k * 128:(k + 1) * 128],
                                  wrT_sb[:, k * D + 512:(k + 1) * D],
                                  start=(k == 0), stop=(k == 7))
                em.mark(i, sPe)
                m_fr[t] = em.n(sPe)
            block.tensor(_fr)

            def _evAB(s, t=t):
                s.wait_ge(sPe, m_fr[t])
                s.activation(fr_sb[:, t * D:t * D + 512], psA[:], AF.Identity,
                             bias=0.0, scale=1.0)
                em.mark(s.activation(fr_sb[:, t * D + 512:(t + 1) * D], psB[:],
                                     AF.Identity, bias=0.0, scale=1.0), sSe)
                m_evAB[t] = em.n(sSe)
            block.scalar(_evAB)

            def _lk(v, t=t):
                v.wait_ge(sSe, m_evAB[t])
                if t - 1 >= 0:
                    v.wait_ge(sDv, m_sr[t - 1])
                n0 = em.mark(v.scalar_tensor_tensor(
                    bf2k[:], fr_sb[:, t * D:(t + 1) * D], 0.01,
                    fr_sb[:, t * D:(t + 1) * D], op0=ALU.mult, op1=ALU.max), sDv)
                v.wait_ge(sDv, n0)
                for h in range(HA):
                    em.mark(v.scalar_tensor_tensor(
                        bf2k[:, h * 128:(h + 1) * 128],
                        bf2k[:, h * 128:(h + 1) * 128], 1.0,
                        attrB_sb[:, 0:128], op0=ALU.mult, op1=ALU.mult,
                        accum_out=sr4[:, (t % 4) * 8 + h:(t % 4) * 8 + h + 1]), sDv)
                m_sr[t] = em.n(sDv)
            block.vector(_lk)

        def _srT(tt, g=g):
            tt.wait_ge(sDv, m_sr[4 * g + 3])
            if g - 1 >= 0:
                tt.wait_ge(sDv, m_srTev[g - 1])
            else:
                tt.wait_ge(sSe, m_w12ev[7])
            for ti in range(4):
                i = tt.transpose(psF[0:8, ti * 128:(ti + 1) * 128],
                                 sr4[:, ti * 8:(ti + 1) * 8], idf_sb[:])
            em.mark(i, sPe)
        block.tensor(_srT)

        def _srTv(v, g=g):
            v.wait_ge(sPe, em.n(sPe))
            em.mark(v.tensor_copy(srT_sb[:, g * 512:(g + 1) * 512], psF[0:8, 0:512]),
                    sDv)
            m_srTev[g] = em.n(sDv)
        block.vector(_srTv)

    if phases <= 2:
        def _finP1(sync):
            sync.wait_ge(sSxt, em.n(sSxt))
            sync.wait_ge(sSxn[0], em.n(sSxn[0]))
            sync.wait_ge(sSxn[1], em.n(sSxn[1]))
            sync.wait_ge(sSt, em.n(sSt))
            sync.wait_ge(sDv, em.n(sDv))
        block.sync(_finP1)
        SA.close()
        G.close()
        es.close()
        return nc

    def _p2a(v):
        v.wait_ge(sDv, em.n(sDv))
        n0 = em.mark(v.reduce_max(
            mloc_sb[:], srT_sb[:].rearrange("h (l n) -> h l n", l=L), axis=AX.X), sDv)
        v.wait_ge(sDv, n0)
        em.mark(v.tensor_scalar(negm_sb[:], mloc_sb[:], -1.0, None, op0=ALU.mult), sDv)
    block.vector(_p2a)
    DV_P2A = em.n(sDv)

    def _p2m(sync):
        sync.wait_ge(sDv, DV_P2A)
        em.mark(sync.dma_start(out=o_m, in_=mloc_sb[:]), sSt, 16)
    block.sync(_p2m)

    def _p2e(s):
        s.wait_ge(sDv, DV_P2A)
        for l in range(L):
            em.mark(s.activation(srT_sb[:, l * NLOC:(l + 1) * NLOC],
                                 srT_sb[:, l * NLOC:(l + 1) * NLOC], AF.Exp,
                                 bias=negm_sb[:, l:l + 1], scale=1.0), sSe)
    block.scalar(_p2e)
    SE_ET = em.n(sSe)

    def _p2z(v):
        v.wait_ge(sSe, SE_ET)
        em.mark(v.reduce_sum(
            zloc_sb[:], srT_sb[:].rearrange("h (l n) -> h l n", l=L), axis=AX.X), sDv)
    block.vector(_p2z)
    DV_P2Z = em.n(sDv)

    def _p2zs(sync):
        sync.wait_ge(sDv, DV_P2Z)
        em.mark(sync.dma_start(out=o_z, in_=zloc_sb[:]), sSt, 16)
    block.sync(_p2zs)

    if phases <= 2.2:
        def _finP2a(sync):
            sync.wait_ge(sSxt, em.n(sSxt))
            sync.wait_ge(sSxn[0], em.n(sSxn[0]))
            sync.wait_ge(sSxn[1], em.n(sSxn[1]))
            sync.wait_ge(sSt, em.n(sSt))
            sync.wait_ge(sDv, em.n(sDv))
        block.sync(_finP2a)
        SA.close()
        G.close()
        es.close()
        return nc

    m_eev = {}
    for t_i in range(TILES):
        def _etr(tt, t_i=t_i):
            tt.wait_ge(sSe, SE_ET)
            if t_i == 0:
                tt.wait_ge(sDv, m_srTev[7])
            else:
                tt.wait_ge(sDv, m_eev[t_i - 1])
            em.mark(tt.transpose(psF[:, 0:HA], srT_sb[0:HA, t_i * 128:(t_i + 1) * 128],
                                 idf_sb[0:HA, 0:HA]), sPe)
        block.tensor(_etr)

        def _etv(v, t_i=t_i):
            v.wait_ge(sPe, em.n(sPe))
            em.mark(v.tensor_copy(e4_sb[:, t_i * 8:(t_i + 1) * 8], psF[:, 0:HA]), sDv)
            m_eev[t_i] = em.n(sDv)
        block.vector(_etv)
    m_eev[7] = m_eev[TILES - 1]

    if phases <= 2.5:
        def _finP2b(sync):
            sync.wait_ge(sSxt, em.n(sSxt))
            sync.wait_ge(sSxn[0], em.n(sSxn[0]))
            sync.wait_ge(sSxn[1], em.n(sSxn[1]))
            sync.wait_ge(sSt, em.n(sSt))
            sync.wait_ge(sDv, em.n(sDv))
        block.sync(_finP2b)
        SA.close()
        G.close()
        es.close()
        return nc

    m_sev = {}
    m_sst = {}
    for l in range(L):
        def _smm(tt, l=l):
            tt.wait_ge(sDv, m_eev[7])
            if l >= 1:
                tt.wait_ge(sDv, m_sev[l - 1])
            for h in range(HA):
                for c4 in range(C4):
                    t_i = l * C4 + c4
                    i = tt.matmul(psS[0:1, h * 128:(h + 1) * 128],
                                  e4_sb[:, t_i * 8 + h:t_i * 8 + h + 1],
                                  fr_sb[:, t_i * D + h * 128:t_i * D + (h + 1) * 128],
                                  start=(c4 == 0), stop=(c4 == C4 - 1),
                                  skip_group_check=True)
            em.mark(i, sPe)
        block.tensor(_smm)
        PE_S = em.n(sPe)

        def _sev(v, l=l, PE_S=PE_S):
            v.wait_ge(sPe, PE_S)
            if l >= 1:
                v.wait_ge(sSt, m_sst[l - 1])
            em.mark(v.tensor_copy(Sst[l % 2][0:1, :], psS[0:1, :]), sDv)
            m_sev[l] = em.n(sDv)
        block.vector(_sev)

        def _sst(sync, l=l):
            sync.wait_ge(sDv, m_sev[l])
            em.mark(sync.dma_start(out=o_s[0:1, l * D:(l + 1) * D],
                                   in_=Sst[l % 2][:]), sSt, 16)
            m_sst[l] = em.n(sSt)
        block.sync(_sst)

    ST_P2 = em.n(sSt)
    DV_P2 = em.n(sDv)
    SE_P2 = em.n(sSe)
    PE_P2 = em.n(sPe)
    S0.close()
    SA.close()

    if phases <= 3:
        def _finP2(sync):
            sync.wait_ge(sSt, ST_P2)
            sync.wait_ge(sSxt, em.n(sSxt))
            sync.wait_ge(sSxn[0], em.n(sSxn[0]))
            sync.wait_ge(sSxn[1], em.n(sSxn[1]))
            sync.wait_ge(sDv, DV_P2)
        block.sync(_finP2)
        SA.close()
        G.close()
        es.close()
        return nc

    barrier(block, em, [(sSt, ST_P2), (sDv, DV_P2), (sSe, SE_P2), (sPe, PE_P2),
                        (sSxt, em.n(sSxt)), (sSxn[0], em.n(sSxn[0])),
                        (sSxn[1], em.n(sSxn[1]))])

    SP = Scope(nc)
    ipT_sb = SP.T("ipT_sb", [128, 8 * 3 * D], BF16)
    opT_sb = SP.T("opT_sb", [128, 8 * D], BF16)
    opb_sb = SP.T("opb_sb", [1, D], BF16)
    ipb_sb = SP.T("ipb_sb", [1, 3 * D], BF16)
    qkv_sb = [SP.T(f"qkv{i}", [128, L * 3 * D], BF16) for i in range(2)]
    xnF2 = [SP.T(f"xnF2{i}", [128, D], BF16) for i in range(2)]
    xrow = [SP.T(f"xrow{i}", [128, D], BF16) for i in range(2)]
    oacc_sb = SP.T("oacc_sb", [128, D], F32)
    oF_sb = SP.T("oF_sb", [128, D], BF16)
    zsum_sb = SP.T("zsum_sb", [128, D], F32)
    xhF_sb = SP.T("xhF_sb", [128, D], F32)
    y1b_sb = SP.T("y1b_sb", [128, D], BF16)
    dmy_sb = SP.T("dmy_sb", [128, 4 * 256], BF16)
    dmy8_sb = SP.T("dmy8_sb", [128, 8], F32)
    scr_sc = SP.T("scr_sc", [128, HM, 8], F32)
    scre_sc = SP.T("scre_sc", [128, HM, 8], F32)
    mx4_sb = SP.T("mx4_sb", [128, HM], F32)
    negmx4_sb = SP.T("negmx4_sb", [128, HM], F32)
    zm4_sb = SP.T("zm4_sb", [128, HM], F32)
    zi4_sb = SP.T("zi4_sb", [128, HM], F32)
    bns2 = SP.T("bns2", [128, 12], F32)
    mv2_sb = SP.T("mv2_sb", [128, 2], F32)
    negmu1 = SP.T("negmu1", [128, 1], F32)
    vep1 = SP.T("vep1", [128, 1], F32)
    rstd1 = SP.T("rstd1", [128, 1], F32)
    logits_sb = SP.T("logits_sb", [128, E], F32)
    mx8_sb = SP.T("mx8_sb", [128, E], F32)
    oh_sb = SP.T("oh_sb", [128, E], F32)
    e8_sb = SP.T("e8_sb", [128, E], F32)
    z8_sb = SP.T("z8_sb", [128, 1], F32)
    negg_sb = SP.T("negg_sb", [128, 1], F32)
    psQ = [SP.P(f"psQ{i}", [128, 512], F32) for i in range(3)]

    def _gq(g):
        for k in range(8):
            em.mark(g.dma_start(out=opT_sb[:, k * D:(k + 1) * D],
                                in_=opT[k * 128:(k + 1) * 128, :]), sGq, 16)
        em.mark(g.dma_start(out=opb_sb[:], in_=opb), sGq, 16)
        em.mark(g.dma_start(out=ipb_sb[:], in_=ipb), sGq, 16)
        for k in range(8):
            em.mark(g.dma_start(out=ipT_sb[:, k * 3 * D:(k + 1) * 3 * D],
                                in_=ipT[k * 128:(k + 1) * 128, :]), sGq, 16)
    block.gpsimd(_gq)
    G_Q = em.n(sGq)

    m_xtl = {}
    m_qkv = {}
    m_qkv_ns = {}
    m_qev = {}
    m_qev_ns = {}
    m_xrl = {}
    m_scr = {}
    m_sC = {}
    m_oac = {}
    m_otr = {}
    m_oFe = {}
    m_omm = {}
    m_zs = {}
    m_vep = {}
    m_rst = {}
    m_r5 = {}
    m_xh2 = {}
    m_xtr = {}
    m_xhe = {}
    m_lg = {}
    m_lge = {}
    m_cnt = {}
    m_cte = {}
    m_y1 = {}
    m_y1s = {}
    m_gt = {}
    m_amk = {}
    m_e8x = {}
    m_dmySL = {}
    last_qkv = [None]

    def _emit_qkv(c, l):
        qb = qkv_sb[c % 2]
        prevq = last_qkv[0]
        last_qkv[0] = (c, l)
        qi = c * L + l

        def _lxt(sync, c=c, l=l, qi=qi):
            if qi - 2 >= 0:
                cc, ll = divmod(qi - 2, L)
                sync.wait_ge(sPe, m_qkv[(cc, ll)])
            t_i = l * C4 + c
            em.mark(sync.dma_start(out=xnF2[qi % 2][:],
                                   in_=xnT_d[t_i * 128:(t_i + 1) * 128, :]),
                    sLxt[qi % 2], 16)
            m_xtl[(c, l)] = em.n(sLxt[qi % 2])
        block.sync(_lxt)

        def _qmmA(tt, c=c, l=l, prevq=prevq, qi=qi):
            tt.wait_ge(sLxt[(c * L + l) % 2], m_xtl[(c, l)])
            if c == 0 and l == 0:
                tt.wait_ge(sGq, G_Q)
            if c >= 2 and l == 0:
                tt.wait_ge(sDv, m_oac[(c - 2, 7)])
            for ns in range(3):
                if prevq is not None:
                    tt.wait_ge(sSe, m_qev_ns[(prevq[0], prevq[1], ns + 3)])
                for k in range(8):
                    tt.matmul(psQ[ns][:, :],
                              xnF2[qi % 2][:, k * 128:(k + 1) * 128],
                              ipT_sb[:, k * 3 * D + ns * 512:k * 3 * D + (ns + 1) * 512],
                              start=(k == 0), stop=False)
                last = tt.matmul(psQ[ns][:, :], onesb_sb[0:1, 0:128],
                                 ipb_sb[0:1, ns * 512:(ns + 1) * 512],
                                 start=False, stop=True)
                em.mark(last, sPe)
                m_qkv_ns[(c, l, ns)] = em.n(sPe)
        block.tensor(_qmmA)

        def _qev_a(s, c=c, l=l):
            s.wait_ge(sPe, m_qkv_ns[(c, l, 2)])
            for ns in range(3):
                i = s.activation(qb[:, l * 3 * D + ns * 512:l * 3 * D + (ns + 1) * 512],
                                 psQ[ns][:], AF.Identity, bias=0.0, scale=1.0)
                em.mark(i, sSe)
                m_qev_ns[(c, l, ns)] = em.n(sSe)
        block.scalar(_qev_a)

        def _qmmB(tt, c=c, l=l, qi=qi):
            for ns in range(3, 6):
                tt.wait_ge(sSe, m_qev_ns[(c, l, ns - 3)])
                for k in range(8):
                    tt.matmul(psQ[ns % 3][:, :],
                              xnF2[qi % 2][:, k * 128:(k + 1) * 128],
                              ipT_sb[:, k * 3 * D + ns * 512:k * 3 * D + (ns + 1) * 512],
                              start=(k == 0), stop=False)
                last = tt.matmul(psQ[ns % 3][:, :], onesb_sb[0:1, 0:128],
                                 ipb_sb[0:1, ns * 512:(ns + 1) * 512],
                                 start=False, stop=True)
                em.mark(last, sPe)
                m_qkv_ns[(c, l, ns)] = em.n(sPe)
            m_qkv[(c, l)] = em.n(sPe)
        block.tensor(_qmmB)

        def _qev_b(s, c=c, l=l):
            s.wait_ge(sPe, m_qkv_ns[(c, l, 5)])
            for ns in range(3, 6):
                i = s.activation(qb[:, l * 3 * D + ns * 512:l * 3 * D + (ns + 1) * 512],
                                 psQ[ns % 3][:], AF.Identity, bias=0.0, scale=1.0)
                em.mark(i, sSe)
                m_qev_ns[(c, l, ns)] = em.n(sSe)
            m_qev[(c, l)] = em.n(sSe)
        block.scalar(_qev_b)

    def _emit_gate(prev, se_mark):
        tp = prev[1] * C4 + prev[0]

        def _gt(v, prev=prev, tp=tp, se_mark=se_mark):
            v.wait_ge(sSe, se_mark)
            n0 = em.mark(v.reduce_sum(z8_sb[:], e8_sb[:], axis=AX.X), sDv)
            v.wait_ge(sDv, n0)
            em.mark(v.reciprocal(gate_sb[:, tp:tp + 1], z8_sb[:]), sDv)
            m_gt[prev] = em.n(sDv)
        block.vector(_gt)

    for l in range(L):
        _emit_qkv(0, l)

    for c in range(C4):
        qc = qkv_sb[c % 2]

        for l in range(L):
            t_i = l * C4 + c
            prev = (c, l - 1) if l >= 1 else ((c - 1, 7) if c >= 1 else None)

            def _xr(sync, c=c, l=l, t_i=t_i):
                if (c * L + l) - 2 >= 0:
                    cc, ll = divmod(c * L + l - 2, L)
                    sync.wait_ge(sDv, m_zs[(cc, ll)])
                em.mark(sync.dma_start(
                    out=xrow[l % 2][:],
                    in_=o_xn[t_i * 128:(t_i + 1) * 128, :]), sLxn[l % 2], 16)
                m_xrl[(c, l)] = em.n(sLxn[l % 2])
            block.sync(_xr)

            def _sc(v, c=c, l=l, prev=prev):
                if l == 0:
                    v.wait_ge(sSe, m_qev[(c, 7)])
                if prev is not None:
                    v.wait_ge(sSe, m_sC[prev])
                for i2 in range(32):
                    mI, h = divmod(i2, HM)
                    sl = i2 % 4
                    if sl in m_dmySL:
                        v.wait_ge(sDv, m_dmySL[sl])
                    m_dmySL[sl] = em.mark(v.scalar_tensor_tensor(
                        dmy_sb[:, sl * 256:(sl + 1) * 256],
                        qc[:, l * 3 * D + h * 256:l * 3 * D + (h + 1) * 256],
                        0.0625,
                        qc[:, mI * 3 * D + D + h * 256:mI * 3 * D + D + (h + 1) * 256],
                        op0=ALU.mult, op1=ALU.mult,
                        accum_out=scr_sc[:, h, mI:mI + 1]), sDv)
                nn = em.n(sDv)
                v.wait_ge(sDv, nn)
                n0 = em.mark(v.reduce_max(mx4_sb[:], scr_sc[:], axis=AX.X), sDv)
                v.wait_ge(sDv, n0)
                em.mark(v.tensor_scalar(negmx4_sb[:], mx4_sb[:], -1.0, None,
                                        op0=ALU.mult), sDv)
                m_scr[(c, l)] = em.n(sDv)
            block.vector(_sc)

            def _sC(s, c=c, l=l, prev=prev):
                s.wait_ge(sDv, m_scr[(c, l)])
                if prev is not None:
                    s.wait_ge(sDv, m_oac[prev])
                for h in range(HM):
                    em.mark(s.activation(scre_sc[:, h, :], scr_sc[:, h, :], AF.Exp,
                                         bias=negmx4_sb[:, h:h + 1], scale=1.0), sSe)
                m_sC[(c, l)] = em.n(sSe)
            block.scalar(_sC)

            def _so(v, c=c, l=l, prev=prev):
                v.wait_ge(sSe, m_sC[(c, l)])
                if prev is not None:
                    v.wait_ge(sPe, m_otr[prev])
                n0 = em.mark(v.reduce_sum(zm4_sb[:], scre_sc[:], axis=AX.X), sDv)
                v.wait_ge(sDv, n0)
                n1 = em.mark(v.reciprocal(zi4_sb[:], zm4_sb[:]), sDv)
                v.wait_ge(sDv, n1)
                for h in range(HM):
                    em.mark(v.tensor_scalar(scre_sc[:, h, :], scre_sc[:, h, :],
                                            zi4_sb[:, h:h + 1], None, op0=ALU.mult), sDv)
                nn = em.n(sDv)
                v.wait_ge(sDv, nn)
                marks = {}
                for mI in range(L):
                    for h in range(HM):
                        oseg = oacc_sb[:, h * 256:(h + 1) * 256]
                        vseg = qc[:, mI * 3 * D + 2 * D + h * 256:
                                  mI * 3 * D + 2 * D + (h + 1) * 256]
                        if mI > 0:
                            v.wait_ge(sDv, marks[(mI - 1, h)])
                        if mI == 0:
                            marks[(mI, h)] = em.mark(v.tensor_scalar(
                                oseg, vseg, scre_sc[:, h, 0:1], None,
                                op0=ALU.mult), sDv)
                        else:
                            marks[(mI, h)] = em.mark(v.scalar_tensor_tensor(
                                oseg, vseg, scre_sc[:, h, mI:mI + 1], oseg,
                                op0=ALU.mult, op1=ALU.add), sDv)
                m_oac[(c, l)] = em.n(sDv)
            block.vector(_so)

            def _otr(tt, c=c, l=l, prev=prev):
                tt.wait_ge(sDv, m_oac[(c, l)])
                if prev is not None:
                    tt.wait_ge(sSe, m_xhe[prev])
                for k in range(8):
                    i = tt.transpose(psF[:, k * 128:(k + 1) * 128],
                                     oacc_sb[:, k * 128:(k + 1) * 128], idf_sb[:])
                em.mark(i, sPe)
                m_otr[(c, l)] = em.n(sPe)
            block.tensor(_otr)

            def _oFe(s, c=c, l=l, prev=prev):
                s.wait_ge(sPe, m_otr[(c, l)])
                if prev is not None:
                    s.wait_ge(sPe, m_omm[prev])
                em.mark(s.activation(oF_sb[:], psF[:], AF.Identity,
                                     bias=0.0, scale=1.0), sSe)
                m_oFe[(c, l)] = em.n(sSe)
            block.scalar(_oFe)

            def _omm(tt, c=c, l=l, prev=prev):
                tt.wait_ge(sSe, m_oFe[(c, l)])
                if c == 0 and l == 0:
                    tt.wait_ge(sGq, G_Q)
                if prev is not None:
                    tt.wait_ge(sDv, m_lge[prev])
                    tt.wait_ge(sDv, m_cte[prev])
                for k in range(8):
                    tt.matmul(psA[:, :], oF_sb[:, k * 128:(k + 1) * 128],
                              opT_sb[:, k * D:k * D + 512], start=(k == 0), stop=False)
                tt.matmul(psA[:, :], onesb_sb[0:1, 0:128], opb_sb[0:1, 0:512],
                          start=False, stop=True)
                for k in range(8):
                    tt.matmul(psB[:, :], oF_sb[:, k * 128:(k + 1) * 128],
                              opT_sb[:, k * D + 512:(k + 1) * D],
                              start=(k == 0), stop=False)
                i = tt.matmul(psB[:, :], onesb_sb[0:1, 0:128], opb_sb[0:1, 512:1024],
                              start=False, stop=True)
                em.mark(i, sPe)
                m_omm[(c, l)] = em.n(sPe)
            block.tensor(_omm)

            def _zs(v, c=c, l=l, prev=prev):
                v.wait_ge(sPe, m_omm[(c, l)])
                v.wait_ge(sLxn[l % 2], m_xrl[(c, l)])
                if prev is not None:
                    v.wait_ge(sPe, m_xtr[prev])
                    v.wait_ge(sGp, m_y1[prev])
                    v.wait_ge(sSe, m_rst[prev])
                    v.wait_ge(sDv, m_xh2[prev])
                v.scalar_tensor_tensor(zsum_sb[:, 0:512], psA[:, :], 1.0,
                                       xrow[l % 2][:, 0:512],
                                       op0=ALU.mult, op1=ALU.add)
                n0 = em.mark(v.scalar_tensor_tensor(zsum_sb[:, 512:1024], psB[:, :], 1.0,
                                                    xrow[l % 2][:, 512:1024],
                                                    op0=ALU.mult, op1=ALU.add), sDv)
                m_zs[(c, l)] = n0
                v.wait_ge(sDv, n0)
                v.bn_stats(bns2[:, 0:6], zsum_sb[:, 0:512])
                n1 = em.mark(v.bn_stats(bns2[:, 6:12], zsum_sb[:, 512:1024]), sDv)
                v.wait_ge(sDv, n1)
                n2 = em.mark(v.bn_aggr(mv2_sb[:], bns2[:]), sDv)
                v.wait_ge(sDv, n2)
                v.tensor_scalar(negmu1[:], mv2_sb[:, 0:1], -1.0, None, op0=ALU.mult)
                em.mark(v.tensor_scalar(vep1[:], mv2_sb[:, 1:2], EPS, None,
                                        op0=ALU.add), sDv)
                m_vep[(c, l)] = em.n(sDv)
            block.vector(_zs)

            def _ln1(s, c=c, l=l):
                s.wait_ge(sDv, m_vep[(c, l)])
                em.mark(s.activation(rstd1[:], vep1[:], AF.Ln, bias=0.0, scale=1.0),
                        sSe)
            block.scalar(_ln1)
            SE_LN1 = em.n(sSe)

            def _r5(v, c=c, l=l, SE_LN1=SE_LN1):
                v.wait_ge(sSe, SE_LN1)
                em.mark(v.tensor_scalar(rstd1[:], rstd1[:], -0.5, None,
                                        op0=ALU.mult), sDv)
                m_r5[(c, l)] = em.n(sDv)
            block.vector(_r5)

            def _rse(s, c=c, l=l, prev=prev):
                s.wait_ge(sDv, m_r5[(c, l)])
                if prev is not None:
                    v_prev2 = m_gt.get((prev[0], prev[1] - 1) if prev[1] >= 1
                                       else ((prev[0] - 1, 7) if prev[0] >= 1 else None))
                    if v_prev2 is not None:
                        s.wait_ge(sDv, v_prev2)
                    s.wait_ge(sDv, m_amk[prev])
                em.mark(s.activation(rstd1[:], rstd1[:], AF.Exp, bias=0.0, scale=1.0),
                        sSe)
                m_rst[(c, l)] = em.n(sSe)
                if prev is not None:
                    em.mark(s.activation(e8_sb[:], logits_sb[:], AF.Exp,
                                         bias=negg_sb[:, 0:1], scale=1.0), sSe)
                    m_e8x[(c, l)] = em.n(sSe)
            block.scalar(_rse)
            if prev is not None:
                _emit_gate(prev, m_e8x[(c, l)])

            def _xh2(v, c=c, l=l):
                v.wait_ge(sSe, m_rst[(c, l)])
                em.mark(v.tensor_scalar(zsum_sb[:], zsum_sb[:], negmu1[:, 0:1],
                                        rstd1[:, 0:1], op0=ALU.add, op1=ALU.mult), sDv)
                m_xh2[(c, l)] = em.n(sDv)
            block.vector(_xh2)

            def _y1(g, c=c, l=l, prev=prev):
                g.wait_ge(sDv, m_xh2[(c, l)])
                if prev is not None:
                    g.wait_ge(sSy1, m_y1s[prev])
                n0 = em.mark(g.tensor_tensor(y1b_sb[:], zsum_sb[:], g1B_sb[:],
                                             op=ALU.mult), sGp)
                g.wait_ge(sGp, n0)
                em.mark(g.tensor_tensor(y1b_sb[:], y1b_sb[:], b1B_sb[:],
                                        op=ALU.add), sGp)
                m_y1[(c, l)] = em.n(sGp)
            block.gpsimd(_y1)

            def _y1st(sync, c=c, l=l, t_i=t_i):
                sync.wait_ge(sGp, m_y1[(c, l)])
                em.mark(sync.dma_start(out=o_y1[t_i * 128:(t_i + 1) * 128, :],
                                       in_=y1b_sb[:]), sSy1, 16)
                m_y1s[(c, l)] = em.n(sSy1)
            block.sync(_y1st)

            def _xtr(tt, c=c, l=l, prev=prev):
                tt.wait_ge(sDv, m_xh2[(c, l)])
                tt.wait_ge(sSe, m_oFe[(c, l)])
                for k in range(8):
                    i = tt.transpose(psF[:, k * 128:(k + 1) * 128],
                                     zsum_sb[:, k * 128:(k + 1) * 128], idf_sb[:])
                em.mark(i, sPe)
                m_xtr[(c, l)] = em.n(sPe)
            block.tensor(_xtr)

            def _xhe(s, c=c, l=l, prev=prev):
                s.wait_ge(sPe, m_xtr[(c, l)])
                if prev is not None:
                    s.wait_ge(sPe, m_lg[prev])
                em.mark(s.activation(xhF_sb[:], psF[:], AF.Identity,
                                     bias=0.0, scale=1.0), sSe)
                m_xhe[(c, l)] = em.n(sSe)
            block.scalar(_xhe)

            def _lg(tt, c=c, l=l):
                tt.wait_ge(sSe, m_xhe[(c, l)])
                tt.wait_ge(sDv, m_zs[(c, l)])
                for k in range(8):
                    tt.matmul(psA[0:128, 0:E], xhF_sb[:, k * 128:(k + 1) * 128],
                              wgp_sb[:, k * E:(k + 1) * E],
                              start=(k == 0), stop=False, skip_group_check=True)
                i = tt.matmul(psA[0:128, 0:E], onesf_sb[0:1, 0:128], cvec_sb[0:1, :],
                              start=False, stop=True, skip_group_check=True)
                em.mark(i, sPe)
                m_lg[(c, l)] = em.n(sPe)
            block.tensor(_lg)

            def _am(v, c=c, l=l, t_i=t_i, prev=prev):
                v.wait_ge(sPe, m_lg[(c, l)])
                if prev is not None:
                    v.wait_ge(sSe, m_e8x[(c, l)])
                n0 = em.mark(v.tensor_copy(logits_sb[:], psA[0:128, 0:E]), sDv)
                m_lge[(c, l)] = n0
                v.wait_ge(sDv, n0)
                n1 = em.mark(v.max(mx8_sb[:], logits_sb[:]), sDv)
                v.wait_ge(sDv, n1)
                v.tensor_scalar(negg_sb[:], mx8_sb[:, 0:1], -1.0, None, op0=ALU.mult)
                n2 = em.mark(v.tensor_scalar(oh_sb[:], logits_sb[:], mx8_sb[:, 0:1],
                                             0.0, op0=ALU.subtract, op1=ALU.is_equal),
                             sDv)
                v.wait_ge(sDv, n2)
                em.mark(v.scalar_tensor_tensor(
                    dmy8_sb[:], oh_sb[:], 1.0, iotaE_sb[:],
                    op0=ALU.mult, op1=ALU.mult,
                    accum_out=idx_sb[:, t_i:t_i + 1]), sDv)
                m_amk[(c, l)] = em.n(sDv)
            block.vector(_am)

            def _cnt(tt, c=c, l=l):
                tt.wait_ge(sDv, m_amk[(c, l)])
                em.mark(tt.matmul(psB[0:1, 0:E], onescol_sb[:], oh_sb[:],
                                  start=True, stop=True, skip_group_check=True), sPe)
                m_cnt[(c, l)] = em.n(sPe)
            block.tensor(_cnt)

            def _cte(v, c=c, l=l):
                v.wait_ge(sPe, m_cnt[(c, l)])
                if c == 0:
                    em.mark(v.tensor_copy(cnt_sb[0:1, l * E:(l + 1) * E],
                                          psB[0:1, 0:E]), sDv)
                else:
                    em.mark(v.tensor_tensor(cnt_sb[0:1, l * E:(l + 1) * E],
                                            cnt_sb[0:1, l * E:(l + 1) * E],
                                            psB[0:1, 0:E], op=ALU.add), sDv)
                m_cte[(c, l)] = em.n(sDv)
            block.vector(_cte)

            if c + 1 < C4:
                _emit_qkv(c + 1, l)

    def _tailE(s):
        s.wait_ge(sDv, m_amk[(3, 7)])
        pp = m_gt.get((3, 6))
        if pp is not None:
            s.wait_ge(sDv, pp)
        em.mark(s.activation(e8_sb[:], logits_sb[:], AF.Exp, bias=negg_sb[:, 0:1],
                             scale=1.0), sSe)
    block.scalar(_tailE)
    _emit_gate((3, 7), em.n(sSe))

    def _fin(sync):
        sync.wait_ge(sDv, em.n(sDv))
        em.mark(sync.dma_start(out=o_cnt, in_=cnt_sb[:]), sSt, 16)
        em.mark(sync.dma_start(out=o_idx, in_=idx_sb[:]), sSt, 16)
        em.mark(sync.dma_start(out=o_gate, in_=gate_sb[:]), sSt, 16)
        sync.wait_ge(sSt, em.n(sSt))
        sync.wait_ge(sSxt, em.n(sSxt))
        sync.wait_ge(sSy1, em.n(sSy1))
    block.sync(_fin)

    SP.close()
    G.close()
    es.close()
    return nc


def build_launch2():
    nc = bass.Bass("TRN2", target_bir_lowering=False)

    x_in = nc.dram_tensor("x", [TOK, D], F32, kind="ExternalInput").ap()
    xn_in = nc.dram_tensor("xn", [TOK, D], BF16, kind="ExternalInput").ap()
    y1_in = nc.dram_tensor("y1", [TOK, D], BF16, kind="ExternalInput").ap()
    idx_in = nc.dram_tensor("idxv", [128, TILES], F32, kind="ExternalInput").ap()
    gate_in = nc.dram_tensor("gatev", [128, TILES], F32, kind="ExternalInput").ap()
    mz_in = nc.dram_tensor("mzall", [CORES, 2 * HA * L], F32, kind="ExternalInput").ap()
    s_in = nc.dram_tensor("sall", [CORES, L * D], F32, kind="ExternalInput").ap()
    cnt_in = nc.dram_tensor("cntall", [CORES * L, E], F32, kind="ExternalInput").ap()
    wcm_in = nc.dram_tensor("wcm", [CORES * L, L], F32, kind="ExternalInput").ap()
    w12_in = nc.dram_tensor("w12all", [E * D, D], BF16, kind="ExternalInput").ap()
    b12_in = nc.dram_tensor("b12all", [E, D], F32, kind="ExternalInput").ap()
    wfT_in = nc.dram_tensor("wfT", [D, D], F32, kind="ExternalInput").ap()
    lng_in = nc.dram_tensor("lng", [1, D], F32, kind="ExternalInput").ap()
    lnb_in = nc.dram_tensor("lnb", [1, D], F32, kind="ExternalInput").ap()
    g2_in = nc.dram_tensor("g2r", [1, D], F32, kind="ExternalInput").ap()
    b2r_in = nc.dram_tensor("b2r", [1, D], F32, kind="ExternalInput").ap()
    idf = nc.dram_tensor("idf", [128, 128], F32, kind="ExternalInput").ap()
    idb = nc.dram_tensor("idb", [128, 128], BF16, kind="ExternalInput").ap()
    onesrow_b = nc.dram_tensor("onesrow_b", [1, 128], BF16, kind="ExternalInput").ap()
    onesrow_f = nc.dram_tensor("onesrow_f", [1, 128], F32, kind="ExternalInput").ap()
    onescol_f = nc.dram_tensor("onescol_f", [128, 1], F32, kind="ExternalInput").ap()
    ltri_in = nc.dram_tensor("ltri", [128, 128], F32, kind="ExternalInput").ap()
    iotaE_in = nc.dram_tensor("iotaE", [128, E], F32, kind="ExternalInput").ap()
    lnc_in = nc.dram_tensor("lnconsts", [128, 4], F32, kind="ExternalInput").ap()

    o_out = nc.dram_tensor("o_out", [TOK, D], F32, kind="ExternalOutput").ap()

    staging = nc.dram_tensor("staging", [E * CPC, D], BF16)
    outstage = nc.dram_tensor("outstage", [E * CPC, D], BF16)

    from contextlib import ExitStack
    es = ExitStack()
    P = lambda name, shape, dt: es.enter_context(nc.psum_tensor(name, shape, dt))

    G = Scope(nc)
    xn_sb = G.T("xn_sb", [128, TILES * D], BF16)
    y1_sb = G.T("y1_sb", [128, TILES * D], BF16)
    fhB_sb = G.T("fhB_sb", [128, L * D], BF16)
    lngB = G.T("lngB", [128, D], BF16)
    lnbB = G.T("lnbB", [128, D], BF16)
    g2B = G.T("g2B", [128, D], BF16)
    b2B = G.T("b2B", [128, D], BF16)
    idf_sb = G.T("idf_sb", [128, 128], F32)
    idb_sb = G.T("idb_sb", [128, 128], BF16)
    onesb_sb = G.T("onesb_sb", [1, 128], BF16)
    onesf_sb = G.T("onesf_sb", [1, 128], F32)
    onescol_sb = G.T("onescol_sb", [128, 1], F32)
    ltri_sb = G.T("ltri_sb", [128, 128], F32)
    iotaE_sb = G.T("iotaE_sb", [128, E], F32)
    idx_sb = G.T("idx_sb", [128, TILES], F32)
    gate_sb = G.T("gate_sb", [128, TILES], F32)
    slotu_sb = G.T("slotu_sb", [128, TILES], U32)
    ln = {k: G.T("ln_" + k, [128, 1], F32) for k in
          ("sx", "negmu", "svar", "vep", "rstd", "nmr")}
    lnB = {k: G.T("lnB_" + k, [128, 1], F32) for k in
           ("sx", "negmu", "svar", "vep", "rstd", "nmr")}
    lnc_sb = G.T("lnc_sb", [128, 4], F32)
    nc.const_aps.register(F32, EPS, lnc_sb[:, 0:1])
    nc.const_aps.register(F32, -0.5, lnc_sb[:, 1:2])
    nc.const_aps.register(F32, -1.0 / D, lnc_sb[:, 2:3])
    nc.const_aps.register(F32, 1.0 / D, lnc_sb[:, 3:4])
    smalls = {k: G.T("sm_" + k, [128, 1], F32) for k in
              ("loc", "lpos", "keep", "gv", "slot", "ovf")}
    oh_sb = G.T("oh_sb", [128, E], F32)
    ohrun_sb = G.T("ohrun_sb", [128, E], F32)

    psA = P("psA", [128, 512], F32)
    psB = P("psB", [128, 512], F32)
    psT = P("psT", [128, D], BF16)
    psF = P("psF", [128, D], F32)

    sLd = es.enter_context(nc.semaphore("sLd"))
    sG = es.enter_context(nc.semaphore("sG"))
    sGz = es.enter_context(nc.semaphore("sGz"))
    sGsc = es.enter_context(nc.semaphore("sGsc"))
    sGb = es.enter_context(nc.semaphore("sGb"))
    sGg = es.enter_context(nc.semaphore("sGg"))
    sLdsl = es.enter_context(nc.semaphore("sLdsl"))
    sLdfh = es.enter_context(nc.semaphore("sLdfh"))
    sLdb = es.enter_context(nc.semaphore("sLdb"))
    sLdw = es.enter_context(nc.semaphore("sLdw"))
    sLdst = es.enter_context(nc.semaphore("sLdst"))
    sLdS = es.enter_context(nc.semaphore("sLdS"))
    sLdX = [es.enter_context(nc.semaphore(f"sLdX{i}")) for i in range(2)]
    sPe = es.enter_context(nc.semaphore("sPe"))
    sDv = es.enter_context(nc.semaphore("sDv"))
    sSe = es.enter_context(nc.semaphore("sSe"))
    sPo = es.enter_context(nc.semaphore("sPo"))
    sSt = es.enter_context(nc.semaphore("sSt"))
    block = es.enter_context(nc.Block())
    em = Emit()

    AW = Scope(nc)
    wbig_sb = AW.T("wbig_sb", [128, 8 * D], BF16)
    SA = Scope(nc)
    psS = SA.P("psS", [1, D], F32)
    psC = SA.P("psC", [L, 512], F32)
    zero_sb = SA.T("zero_sb", [128, D], BF16)
    cs_sb = SA.T("cs_sb", [1, TILES * E], F32)
    runl_sb = SA.T("runl_sb", [1, TILES * E], F32)
    runa_sb = SA.T("runa_sb", [1, TILES * E], F32)
    mz_sb = SA.T("mz_sb", [CORES, 2 * HA * L], F32)
    mzT_sb = SA.T("mzT_sb", [HA * L, CORES], F32)
    zT0_sb = SA.T("zT0_sb", [HA * L, CORES], F32)
    mg_sb = SA.T("mg_sb", [HA * L, 1], F32)
    negmg_sb = SA.T("negmg_sb", [HA * L, 1], F32)
    wT_sb = SA.T("wT_sb", [HA * L, CORES], F32)
    zt_sb = SA.T("zt_sb", [HA * L, CORES], F32)
    zg_sb = SA.T("zg_sb", [HA * L, 1], F32)
    zgi_sb = SA.T("zgi_sb", [HA * L, 1], F32)
    w8_sb = SA.T("w8_sb", [CORES, HA * L], F32)
    s_l = SA.T("s_l", [CORES, D], F32)
    ctx_l = SA.T("ctx_l", [1, D], F32)
    ctxF_sb = SA.T("ctxF_sb", [128, 8 * L], BF16)
    fh_sb = SA.T("fh_sb", [L, D], F32)
    fhrow_sb = ctx_l
    cnt2_sb = SA.T("cnt2_sb", [CORES * L, E], F32)
    wcm_sb = SA.T("wcm_sb", [CORES * L, L], F32)
    base_sb = SA.T("base_sb", [L, E], F32)
    baser_sb = SA.T("baser_sb", [1, L * E], F32)

    def _ld(sync):
        for dst, src in ((idf_sb[:], idf), (idb_sb[:], idb), (onesb_sb[:], onesrow_b),
                         (onesf_sb[:], onesrow_f), (onescol_sb[:], onescol_f),
                         (ltri_sb[:], ltri_in), (iotaE_sb[:], iotaE_in),
                         (idx_sb[:], idx_in), (gate_sb[:], gate_in),
                         (mz_sb[:], mz_in), (wcm_sb[:], wcm_in), (cnt2_sb[:], cnt_in),
                         (lnc_sb[:], lnc_in)):
            em.mark(sync.dma_start(out=dst, in_=src), sLdS, 16)
    block.sync(_ld)
    LD0 = em.n(sLdS)

    def _ldbig(sync):
        for t_i in range(TILES):
            em.mark(sync.dma_start(out=xn_sb[:, t_i * D:(t_i + 1) * D],
                                   in_=xn_in[t_i * 128:(t_i + 1) * 128, :]), sLd, 16)
            em.mark(sync.dma_start(out=y1_sb[:, t_i * D:(t_i + 1) * D],
                                   in_=y1_in[t_i * 128:(t_i + 1) * 128, :]), sLd, 16)
    block.sync(_ldbig)
    LD_BIG = em.n(sLd)

    def _gld(g):
        for k in range(8):
            em.mark(g.dma_start(out=wbig_sb[:, k * D:(k + 1) * D],
                                in_=wfT_in[k * 128:(k + 1) * 128, :]), sG, 16)
        for dst, src in ((lngB, lng_in), (lnbB, lnb_in), (g2B, g2_in), (b2B, b2r_in)):
            em.mark(g.dma_start(out=dst[0:1, :], in_=src), sG, 16)
    block.gpsimd(_gld)
    G0 = em.n(sG)

    def _z(v):
        em.mark(v.memset(zero_sb[:], 0.0), sDv)
    block.vector(_z)

    bc_cell = []

    def _zs(g):
        bc_cell.append(g.to_reg(E * CPC - 1))
        g.wait_ge(sDv, em.n(sDv))
        for r in range(E * CPC // 128):
            em.mark(g.dma_start(out=staging.ap()[r * 128:(r + 1) * 128, :],
                                in_=zero_sb[:]), sGz, 16)
    block.gpsimd(_zs)
    G_ZERO = em.n(sGz)

    for dst in (lngB, lnbB, g2B, b2B):
        def _bct(t, dst=dst):
            t.wait_ge(sG, G0)
            t.wait_ge(sLdS, LD0)
            t.wait_ge(sDv, em.n(sDv))
            t.matmul(psF[:, 0:512], onesb_sb[0:1, :], dst[0:1, 0:512],
                     start=True, stop=True)
            em.mark(t.matmul(psF[:, 512:1024], onesb_sb[0:1, :], dst[0:1, 512:1024],
                             start=True, stop=True), sPe)
        block.tensor(_bct)

        def _bcv(v, dst=dst):
            v.wait_ge(sPe, em.n(sPe))
            em.mark(v.tensor_copy(dst[:], psF[:]), sDv)
        block.vector(_bcv)

    def _t1(t):
        t.wait_ge(sDv, em.n(sDv))
        em.mark(t.transpose(psF[0:128, 0:CORES], mz_sb[:], idf_sb[0:CORES, 0:CORES]), sPe)
    block.tensor(_t1)

    def _v1(v):
        v.wait_ge(sPe, em.n(sPe))
        em.mark(v.tensor_copy(mzT_sb[:], psF[0:HA * L, 0:CORES]), sDv)
        n0 = em.mark(v.tensor_copy(zT0_sb[:], psF[HA * L:2 * HA * L, 0:CORES]), sDv)
        v.wait_ge(sDv, n0)
        n1 = em.mark(v.reduce_max(mg_sb[:], mzT_sb[0:HA * L, :], axis=AX.X), sDv)
        v.wait_ge(sDv, n1)
        em.mark(v.tensor_scalar(negmg_sb[:], mg_sb[:], -1.0, None, op0=ALU.mult), sDv)
    block.vector(_v1)

    def _s1(s):
        s.wait_ge(sDv, em.n(sDv))
        em.mark(s.activation(wT_sb[:], mzT_sb[0:HA * L, :], AF.Exp,
                             bias=negmg_sb[:, 0:1], scale=1.0), sSe)
    block.scalar(_s1)

    def _v2(v):
        v.wait_ge(sSe, em.n(sSe))
        n0 = em.mark(v.tensor_tensor(zt_sb[:], wT_sb[:], zT0_sb[:], op=ALU.mult), sDv)
        v.wait_ge(sDv, n0)
        n1 = em.mark(v.reduce_sum(zg_sb[:], zt_sb[:], axis=AX.X), sDv)
        v.wait_ge(sDv, n1)
        n2 = em.mark(v.reciprocal(zgi_sb[:], zg_sb[:]), sDv)
        v.wait_ge(sDv, n2)
        em.mark(v.tensor_scalar(wT_sb[:], wT_sb[:], zgi_sb[:, 0:1], None,
                                op0=ALU.mult), sDv)
    block.vector(_v2)

    def _t2(t):
        t.wait_ge(sDv, em.n(sDv))
        em.mark(t.transpose(psF[0:CORES, 0:HA * L], wT_sb[:],
                            idf_sb[0:HA * L, 0:HA * L]), sPe)
    block.tensor(_t2)

    def _v3(v):
        v.wait_ge(sPe, em.n(sPe))
        em.mark(v.tensor_copy(w8_sb[:], psF[0:CORES, 0:HA * L]), sDv)
    block.vector(_v3)

    for l in range(L):
        def _sll(sync, l=l):
            if l >= 1:
                sync.wait_ge(sPe, em.n(sPe))
            em.mark(sync.dma_start(out=s_l[:], in_=s_in[:, l * D:(l + 1) * D]), sLdsl, 16)
        block.sync(_sll)
        ld_sl = em.n(sLdsl)

        def _ctx(t, l=l, ld_sl=ld_sl):
            t.wait_ge(sLdsl, ld_sl)
            t.wait_ge(sDv, em.n(sDv))
            for h in range(HA):
                i = t.matmul(psS[0:1, h * 128:(h + 1) * 128],
                             w8_sb[:, l * HA + h:l * HA + h + 1],
                             s_l[:, h * 128:(h + 1) * 128],
                             start=True, stop=True, skip_group_check=True)
            em.mark(i, sPe)
        block.tensor(_ctx)

        def _ctxv(v, l=l):
            v.wait_ge(sPe, em.n(sPe))
            em.mark(v.tensor_copy(ctx_l[0:1, :], psS[0:1, :]), sDv)
        block.vector(_ctxv)

        def _cf(t, l=l):
            t.wait_ge(sDv, em.n(sDv))
            for k in range(8):
                i = t.transpose(psF[:, k:k + 1],
                                ctx_l[0:1, k * 128:(k + 1) * 128],
                                idf_sb[0:1, 0:1])
            em.mark(i, sPe)
        block.tensor(_cf)

        def _cfv(v, l=l):
            v.wait_ge(sPe, em.n(sPe))
            em.mark(v.tensor_copy(
                ctxF_sb[:].rearrange("p (k j) -> p k j", j=L)[:, :, l],
                psF[:, 0:8]), sDv)
        block.vector(_cfv)

    def _fh(t):
        t.wait_ge(sG, G0)
        t.wait_ge(sDv, em.n(sDv))
        for k in range(8):
            t.matmul(psA[0:L, :], ctxF_sb[:, k * L:(k + 1) * L],
                     wbig_sb[:, k * D:k * D + 512], start=(k == 0), stop=(k == 7))
        for k in range(8):
            i = t.matmul(psB[0:L, :], ctxF_sb[:, k * L:(k + 1) * L],
                         wbig_sb[:, k * D + 512:(k + 1) * D],
                         start=(k == 0), stop=(k == 7))
        em.mark(i, sPe)
    block.tensor(_fh)
    PE_FH = em.n(sPe)

    def _fhv(v):
        v.wait_ge(sPe, PE_FH)
        em.mark(v.tensor_copy(fh_sb[:, 0:512], psA[0:L, :]), sDv)
        em.mark(v.tensor_copy(fh_sb[:, 512:1024], psB[0:L, :]), sDv)
    block.vector(_fhv)
    DV_FH = em.n(sDv)

    for l in range(L):
        def _fhr(sync, l=l):
            sync.wait_ge(sDv, DV_FH)
            if l >= 1:
                sync.wait_ge(sPe, em.n(sPe))
            em.mark(sync.dma_start(out=fhrow_sb[0:1, :], in_=fh_sb[l:l + 1, :]), sLdfh, 16)
        block.sync(_fhr)
        ld_f = em.n(sLdfh)

        def _fb(t, l=l, ld_f=ld_f):
            t.wait_ge(sLdfh, ld_f)
            t.wait_ge(sDv, em.n(sDv))
            t.matmul(psA[:, :], onesf_sb[0:1, :], fhrow_sb[0:1, 0:512],
                     start=True, stop=True)
            em.mark(t.matmul(psB[:, :], onesf_sb[0:1, :], fhrow_sb[0:1, 512:1024],
                             start=True, stop=True), sPe)
        block.tensor(_fb)

        def _fbv(v, l=l):
            v.wait_ge(sPe, em.n(sPe))
            em.mark(v.tensor_copy(fhB_sb[:, l * D:l * D + 512], psA[:]), sDv)
            em.mark(v.tensor_copy(fhB_sb[:, l * D + 512:(l + 1) * D], psB[:]), sDv)
        block.vector(_fbv)

    def _base(t):
        t.wait_ge(sDv, em.n(sDv))
        em.mark(t.matmul(psC[0:L, 0:E], wcm_sb[:], cnt2_sb[:],
                         start=True, stop=True), sPe)
    block.tensor(_base)

    def _basev(v):
        v.wait_ge(sPe, em.n(sPe))
        em.mark(v.tensor_copy(base_sb[:], psC[0:L, 0:E]), sDv)
    block.vector(_basev)

    def _baser(sync):
        sync.wait_ge(sDv, em.n(sDv))
        em.mark(sync.dma_start(out=baser_sb[:], in_=base_sb[:]), sLdb, 16)
    block.sync(_baser)
    LD_BASE = em.n(sLdb)

    for t_i in range(TILES):
        def _oh(v, t_i=t_i):
            v.wait_ge(sPe, em.n(sPe))
            v.wait_ge(sDv, em.n(sDv))
            em.mark(v.tensor_scalar(oh_sb[:], iotaE_sb[:], idx_sb[:, t_i:t_i + 1],
                                    0.0, op0=ALU.subtract, op1=ALU.is_equal), sDv)
        block.vector(_oh)

        def _cs(t, t_i=t_i):
            t.wait_ge(sDv, em.n(sDv))
            em.mark(t.matmul(psC[0:1, t_i * E:(t_i + 1) * E], onescol_sb[:], oh_sb[:],
                             start=True, stop=True, skip_group_check=True), sPe)
        block.tensor(_cs)

    def _csv(v):
        v.wait_ge(sPe, em.n(sPe))
        em.mark(v.tensor_copy(cs_sb[0:1, :], psC[0:1, 0:TILES * E]), sDv)
    block.vector(_csv)

    def _pre(v):
        v.wait_ge(sDv, em.n(sDv))
        v.wait_ge(sLdb, LD_BASE)
        for t_i in range(TILES):
            l, c4 = divmod(t_i, C4)
            if c4 == 0:
                em.mark(v.tensor_copy(runl_sb[0:1, t_i * E:(t_i + 1) * E],
                                      baser_sb[0:1, l * E:(l + 1) * E]), sDv)
            else:
                v.wait_ge(sDv, em.n(sDv))
                em.mark(v.tensor_tensor(runl_sb[0:1, t_i * E:(t_i + 1) * E],
                                        runl_sb[0:1, (t_i - 1) * E:t_i * E],
                                        cs_sb[0:1, (t_i - 1) * E:t_i * E],
                                        op=ALU.add), sDv)
            if t_i == 0:
                em.mark(v.memset(runa_sb[0:1, 0:E], 0.0), sDv)
            else:
                v.wait_ge(sDv, em.n(sDv))
                em.mark(v.tensor_tensor(runa_sb[0:1, t_i * E:(t_i + 1) * E],
                                        runa_sb[0:1, (t_i - 1) * E:t_i * E],
                                        cs_sb[0:1, (t_i - 1) * E:t_i * E],
                                        op=ALU.add), sDv)
    block.vector(_pre)

    for t_i in range(TILES):
        def _oh2(v, t_i=t_i):
            v.wait_ge(sDv, em.n(sDv))
            v.wait_ge(sPe, em.n(sPe))
            em.mark(v.tensor_scalar(oh_sb[:], iotaE_sb[:], idx_sb[:, t_i:t_i + 1],
                                    0.0, op0=ALU.subtract, op1=ALU.is_equal), sDv)
        block.vector(_oh2)

        def _lt(t, t_i=t_i):
            t.wait_ge(sDv, em.n(sDv))
            t.matmul(psF[0:128, 0:E], ltri_sb[:], oh_sb[:],
                     start=True, stop=False, skip_group_check=True)
            t.matmul(psF[0:128, 0:E], onesf_sb[0:1, 0:128],
                     runl_sb[0:1, t_i * E:(t_i + 1) * E],
                     start=False, stop=True, skip_group_check=True)
            t.matmul(psF[0:128, E:2 * E], ltri_sb[:], oh_sb[:],
                     start=True, stop=False, skip_group_check=True)
            em.mark(t.matmul(psF[0:128, E:2 * E], onesf_sb[0:1, 0:128],
                             runa_sb[0:1, t_i * E:(t_i + 1) * E],
                             start=False, stop=True, skip_group_check=True), sPe)
        block.tensor(_lt)

        def _sel(v, t_i=t_i):
            v.wait_ge(sPe, em.n(sPe))
            v.wait_ge(sSe, em.n(sSe))
            n0 = em.mark(v.tensor_tensor(ohrun_sb[:], psF[0:128, 0:E], oh_sb[:],
                                         op=ALU.mult), sDv)
            v.wait_ge(sDv, n0)
            n1 = em.mark(v.reduce_sum(smalls["loc"][:], ohrun_sb[:], axis=AX.X), sDv)
            v.wait_ge(sDv, n1)
            n2 = em.mark(v.tensor_tensor(ohrun_sb[:], psF[0:128, E:2 * E], oh_sb[:],
                                         op=ALU.mult), sDv)
            v.wait_ge(sDv, n2)
            n3 = em.mark(v.reduce_sum(smalls["lpos"][:], ohrun_sb[:], axis=AX.X), sDv)
            v.wait_ge(sDv, n3)
            n4 = em.mark(v.tensor_scalar(smalls["keep"][:], smalls["loc"][:], float(CAP),
                                         None, op0=ALU.is_lt), sDv)
            v.wait_ge(sDv, n4)
            em.mark(v.tensor_tensor(smalls["gv"][:], smalls["keep"][:],
                                    gate_sb[:, t_i:t_i + 1], op=ALU.mult), sDv)
            n6 = em.mark(v.tensor_scalar(smalls["slot"][:], idx_sb[:, t_i:t_i + 1],
                                         float(CPC), smalls["lpos"][:, 0:1],
                                         op0=ALU.mult, op1=ALU.add), sDv)
            n7 = em.mark(v.tensor_scalar(smalls["ovf"][:], smalls["lpos"][:], float(CPC),
                                         None, op0=ALU.is_ge), sDv)
            v.wait_ge(sDv, n7)
            n8 = em.mark(v.scalar_tensor_tensor(smalls["slot"][:], smalls["ovf"][:], OOB,
                                                smalls["slot"][:], op0=ALU.mult,
                                                op1=ALU.add), sDv)
            v.wait_ge(sDv, n8)
            em.mark(v.tensor_copy(gate_sb[:, t_i:t_i + 1], smalls["gv"][:]), sDv)
        block.vector(_sel)

        def _cvt(s, t_i=t_i):
            s.wait_ge(sDv, em.n(sDv))
            em.mark(s.activation(slotu_sb[:, t_i:t_i + 1], smalls["slot"][:], AF.Identity,
                                 bias=0.0, scale=1.0), sSe)
        block.scalar(_cvt)

        def _scat(g, t_i=t_i):
            g.wait_ge(sSe, em.n(sSe))
            g.wait_ge(sGz, G_ZERO)
            g.wait_ge(sLd, LD_BIG)
            em.mark(g.indirect_dma_start(
                out=staging.ap(),
                out_offset=IndirectOffsetOnAxis(slotu_sb[:, t_i:t_i + 1], axis=0),
                in_=y1_sb[:, t_i * D:(t_i + 1) * D],
                in_offset=None,
                bounds_check=bc_cell[0],
                oob_is_err=False), sGsc, 16)
        block.gpsimd(_scat)

    G_SCAT = em.n(sGsc)

    SA.close()

    A2 = Scope(nc)
    psT2 = A2.P("psT2", [128, D], BF16)
    stage2 = [A2.T(f"stage{i}", [128, D], BF16) for i in range(2)]
    stF2 = [A2.T(f"stF{i}", [128, D], BF16) for i in range(2)]
    eo2 = [A2.T(f"eo{i}", [128, D], BF16) for i in range(2)]
    wbig2_sb = A2.T("wbig2_sb", [128, 8 * D], BF16)
    b12e_sb = A2.T("b12e_sb", [1, D], BF16)

    wpair = [wbig_sb, wbig2_sb]
    tpair = [psT, psT2]
    NI = E * (CPC // 128)
    m_T = {}
    m_stF = {}
    m_mm = {-1: 0, -2: 0}
    m_eo = {-1: 0, -2: 0}
    st_out = {-1: 0, -2: 0}
    ld_stage = {}
    ld_w = {}
    g_b = {}

    def emit_wld(e):
        be = e % 2

        def _wld(sync, e=e, be=be):
            sync.wait_ge(sGsc, G_SCAT)
            if e == 0:
                sync.wait_ge(sPe, PE_FH)
            elif e >= 2:
                sync.wait_ge(sPe, m_mm[(e - 2) * 8 + 7])
            for k in range(8):
                em.mark(sync.dma_start(
                    out=wpair[be][:, k * D:(k + 1) * D],
                    in_=w12_in[e * D + k * 128:e * D + (k + 1) * 128, :]), sLdw, 16)
        block.sync(_wld)
        ld_w[e] = em.n(sLdw)

    def emit_bld(e):
        def _bld(g, e=e):
            g.wait_ge(sGsc, G_SCAT)
            if e >= 1:
                g.wait_ge(sPe, m_mm[(e - 1) * 8 + 7])
            em.mark(g.dma_start(out=b12e_sb[0:1, :],
                                in_=b12_in[e:e + 1, :]), sGb, 16)
        block.gpsimd(_bld)
        g_b[e] = em.n(sGb)

    def emit_sld(i):
        e, r = divmod(i, CPC // 128)
        p = i % 2

        def _sld(sync, i=i, e=e, r=r, p=p):
            sync.wait_ge(sGsc, G_SCAT)
            if i >= 2:
                sync.wait_ge(sPe, m_T[i - 2])
            em.mark(sync.dma_start(
                out=stage2[p][:],
                in_=staging.ap()[e * CPC + r * 128:e * CPC + (r + 1) * 128, :]),
                sLdst, 16)
        block.sync(_sld)
        ld_stage[i] = em.n(sLdst)

    def emit_T(i):
        p = i % 2

        def _T(t, i=i, p=p):
            t.wait_ge(sLdst, ld_stage[i])
            if i >= 2:
                t.wait_ge(sDv, m_stF[i - 2])
            for k in range(8):
                ins = t.transpose(tpair[p][:, k * 128:(k + 1) * 128],
                                  stage2[p][:, k * 128:(k + 1) * 128], idb_sb[:])
            em.mark(ins, sPe)
        block.tensor(_T)
        m_T[i] = em.n(sPe)

        def _stv(v, i=i, p=p):
            v.wait_ge(sPe, m_T[i])
            v.wait_ge(sPe, m_mm[i - 2])
            em.mark(v.tensor_copy(stF2[p][:], tpair[p][:]), sDv)
        block.vector(_stv)
        m_stF[i] = em.n(sDv)

    def emit_mm(i):
        e = i // 8
        p = i % 2
        be = e % 2

        def _emm(t, i=i, e=e, p=p, be=be):
            t.wait_ge(sDv, m_stF[i])
            t.wait_ge(sLdw, ld_w[e])
            t.wait_ge(sGb, g_b[e])
            t.wait_ge(sDv, m_eo[i - 2])
            tA = psA[:, :] if p == 0 else psF[0:128, 0:512]
            tB = psB[:, :] if p == 0 else psF[0:128, 512:1024]
            for k in range(8):
                t.matmul(tA, stF2[p][:, k * 128:(k + 1) * 128],
                         wpair[be][:, k * D:k * D + 512], start=(k == 0), stop=False)
            t.matmul(tA, onesb_sb[0:1, 0:128], b12e_sb[0:1, 0:512],
                     start=False, stop=True)
            for k in range(8):
                t.matmul(tB, stF2[p][:, k * 128:(k + 1) * 128],
                         wpair[be][:, k * D + 512:(k + 1) * D],
                         start=(k == 0), stop=False)
            em.mark(t.matmul(tB, onesb_sb[0:1, 0:128], b12e_sb[0:1, 512:1024],
                             start=False, stop=True), sPe)
        block.tensor(_emm)
        m_mm[i] = em.n(sPe)

        def _eev(v, i=i, p=p):
            v.wait_ge(sPe, m_mm[i])
            v.wait_ge(sSt, st_out[i - 2])
            tA = psA[:, :] if p == 0 else psF[0:128, 0:512]
            tB = psB[:, :] if p == 0 else psF[0:128, 512:1024]
            em.mark(v.tensor_copy(eo2[p][:, 0:512], tA), sDv)
            em.mark(v.tensor_copy(eo2[p][:, 512:1024], tB), sDv)
        block.vector(_eev)
        m_eo[i] = em.n(sDv)

        def _est(sync, i=i, p=p):
            e, r = divmod(i, CPC // 128)
            sync.wait_ge(sDv, m_eo[i])
            em.mark(sync.dma_start(
                out=outstage.ap()[e * CPC + r * 128:e * CPC + (r + 1) * 128, :],
                in_=eo2[p][:]), sSt, 16)
        block.sync(_est)
        st_out[i] = em.n(sSt)

    emit_wld(0)
    emit_bld(0)
    emit_sld(0)
    emit_T(0)
    for i in range(NI):
        e, r = divmod(i, CPC // 128)
        if r == 0 and e > 0:
            emit_wld(e)
            emit_bld(e)
        if i + 1 < NI:
            emit_sld(i + 1)
            emit_T(i + 1)
        emit_mm(i)

    ST_EXP = em.n(sSt)
    DV_EXP = em.n(sDv)
    A2.close()
    AW.close()

    SC = Scope(nc)
    xbuf = [SC.T(f"xbuf{i}", [128, D], F32) for i in range(2)]
    gz2 = [SC.T(f"gz{i}", [128, D], BF16) for i in range(2)]
    z2b = [SC.T(f"z2_{i}", [128, D], F32) for i in range(2)]
    xh2b = [SC.T(f"xh2_{i}", [128, D], BF16) for i in range(2)]
    xhab = [SC.T(f"xha_{i}", [128, D], BF16) for i in range(2)]
    y2b = [SC.T(f"y2_{i}", [128, D], F32) for i in range(2)]
    aob = [SC.T(f"ao_{i}", [128, D], BF16) for i in range(2)]
    outb = [SC.T(f"out_{i}", [128, D], F32) for i in range(2)]
    ln["dummy"] = SC.T("ln_dummy", [128, D], BF16)
    lnB["dummy"] = SC.T("lnB_dummy", [128, D], BF16)

    barrier(block, em, [(sSt, ST_EXP), (sDv, DV_EXP)])

    def _gzinit(v):
        em.mark(v.memset(gz2[0][:], 0.0), sDv)
        em.mark(v.memset(gz2[1][:], 0.0), sDv)
    block.vector(_gzinit)

    m_x = {}
    m_g = {}
    m_z2 = {}
    se2 = {-1: (0, 0), -2: (0, 0)}
    seA = {-1: (0, 0), -2: (0, 0)}
    m_y2 = {}
    m_ya = {}
    m_ao = {}
    m_sum = {-1: 0, -2: 0}
    st_o = {-1: 0, -2: 0}
    for t_i in range(TILES):
        l = t_i // C4
        q = t_i % 2

        def _xl(sync, t_i=t_i, q=q):
            if t_i >= 2:
                sync.wait_ge(sDv, m_sum[t_i - 2])
            em.mark(sync.dma_start(out=xbuf[q][:],
                                   in_=x_in[t_i * 128:(t_i + 1) * 128, :]), sLdX[q], 16)
        block.sync(_xl)
        m_x[t_i] = em.n(sLdX[q])

        def _gth(g, t_i=t_i, q=q):
            g.wait_ge(sSt, ST_EXP)
            if t_i >= 2:
                g.wait_ge(sDv, m_z2[t_i - 2])
            else:
                g.wait_ge(sDv, em.n(sDv))
            em.mark(g.indirect_dma_start(
                out=gz2[q][:],
                out_offset=None,
                in_=outstage.ap(),
                in_offset=IndirectOffsetOnAxis(slotu_sb[:, t_i:t_i + 1], axis=0),
                bounds_check=bc_cell[0],
                oob_is_err=False), sGg, 16)
        block.gpsimd(_gth)
        m_g[t_i] = em.n(sGg)

        def _s1(v, t_i=t_i, q=q):
            v.wait_ge(sGg, m_g[t_i])
            v.wait_ge(sSe, seA[t_i - 2][1])
            v.wait_ge(sDv, seA[t_i - 2][0])
            em.mark(v.scalar_tensor_tensor(
                z2b[q][:], gz2[q][:], gate_sb[:, t_i:t_i + 1],
                y1_sb[:, t_i * D:(t_i + 1) * D], op0=ALU.mult, op1=ALU.add), sDv)
        block.vector(_s1)
        m_z2[t_i] = em.n(sDv)

        se2[t_i] = _ln_pipe(block, em, sDv, sSe, z2b[q][:], xh2b[q][:], ln,
                            pre_dv=m_z2[t_i], pre_se=se2[t_i - 1][1],
                            pre_dv2=se2[t_i - 1][0])

        def _y2(v, t_i=t_i, q=q):
            v.wait_ge(sDv, se2[t_i][0])
            if t_i >= 2:
                v.wait_ge(sDv, m_sum[t_i - 2])
            n0 = em.mark(v.tensor_tensor(y2b[q][:], xh2b[q][:], g2B[:],
                                         op=ALU.mult), sDv)
            v.wait_ge(sDv, n0)
            em.mark(v.tensor_tensor(y2b[q][:], y2b[q][:], b2B[:], op=ALU.add), sDv)
        block.vector(_y2)
        m_y2[t_i] = em.n(sDv)

        def _ya(v, t_i=t_i, q=q, l=l):
            v.wait_ge(sDv, se2[t_i][0])
            v.wait_ge(sSe, se2[t_i][1])
            em.mark(v.tensor_tensor(z2b[q][:], xn_sb[:, t_i * D:(t_i + 1) * D],
                                    fhB_sb[:, l * D:(l + 1) * D], op=ALU.add), sDv)
        block.vector(_ya)
        m_ya[t_i] = em.n(sDv)

        seA[t_i] = _ln_pipe(block, em, sDv, sSe, z2b[q][:], xhab[q][:], lnB,
                            pre_dv=m_ya[t_i], pre_se=seA[t_i - 1][1],
                            pre_dv2=seA[t_i - 1][0])

        def _ao(g, t_i=t_i, q=q):
            g.wait_ge(sDv, seA[t_i][0])
            if t_i >= 2:
                g.wait_ge(sDv, m_sum[t_i - 2])
            n0 = em.mark(g.tensor_tensor(aob[q][:], xhab[q][:], lngB[:],
                                         op=ALU.mult), sPo)
            g.wait_ge(sPo, n0)
            em.mark(g.tensor_tensor(aob[q][:], aob[q][:], lnbB[:], op=ALU.add), sPo)
        block.gpsimd(_ao)
        m_ao[t_i] = em.n(sPo)

        def _sum(v, t_i=t_i, q=q):
            v.wait_ge(sLdX[q], m_x[t_i])
            v.wait_ge(sPo, m_ao[t_i])
            v.wait_ge(sSt, st_o[t_i - 2])
            n0 = em.mark(v.tensor_tensor(outb[q][:], xbuf[q][:], y2b[q][:],
                                         op=ALU.add), sDv)
            v.wait_ge(sDv, n0)
            em.mark(v.tensor_tensor(outb[q][:], outb[q][:], aob[q][:],
                                    op=ALU.add), sDv)
        block.vector(_sum)
        m_sum[t_i] = em.n(sDv)

        def _ost(sync, t_i=t_i, q=q):
            sync.wait_ge(sDv, m_sum[t_i])
            em.mark(sync.dma_start(out=o_out[t_i * 128:(t_i + 1) * 128, :],
                                   in_=outb[q][:]), sSt, 16)
        block.sync(_ost)
        st_o[t_i] = em.n(sSt)

    def _end(sync):
        sync.wait_ge(sSt, em.n(sSt))
    block.sync(_end)

    SC.close()
    G.close()
    es.close()
    return nc



def _consts():
    idf = np.eye(128, dtype=np.float32)
    idb = np.eye(128, dtype=BF16NP)
    onesrow_b = np.ones((1, 128), dtype=BF16NP)
    onesrow_f = np.ones((1, 128), dtype=np.float32)
    onescol_f = np.ones((128, 1), dtype=np.float32)
    iotaE = np.tile(np.arange(E, dtype=np.float32), (128, 1))
    ltri = np.triu(np.ones((128, 128), dtype=np.float32), 1)
    return idf, idb, onesrow_b, onesrow_f, onescol_f, iotaE, ltri


def kernel(L0_attn_h, L0_attn_Wl, L0_attn_Wr, L0_attn_att_l, L0_attn_att_r,
           L0_attn_Wf, L0_attn_ln_gamma, L0_attn_ln_beta, L0_ffn_in_proj_w,
           L0_ffn_in_proj_b, L0_ffn_out_proj_w, L0_ffn_out_proj_b, L0_ffn_ln1_g,
           L0_ffn_ln1_b, L0_ffn_ln2_g, L0_ffn_ln2_b, L0_ffn_wg, L0_ffn_w1,
           L0_ffn_b1, L0_ffn_w2, L0_ffn_b2):
    f32 = np.float32
    x = np.asarray(L0_attn_h, f32)
    idf, idb, onesrow_b, onesrow_f, onescol_f, iotaE, ltri = _consts()

    wrT = np.ascontiguousarray(np.asarray(L0_attn_Wr, f32).T)
    attr = np.ascontiguousarray(np.asarray(L0_attn_att_r, f32).reshape(1, 128))
    ipT = np.ascontiguousarray(np.asarray(L0_ffn_in_proj_w, f32).T)
    ipb = np.asarray(L0_ffn_in_proj_b, f32).reshape(1, 3 * D)
    opT = np.ascontiguousarray(np.asarray(L0_ffn_out_proj_w, f32).T)
    opb = np.asarray(L0_ffn_out_proj_b, f32).reshape(1, D)
    g1 = np.asarray(L0_ffn_ln1_g, f32).reshape(1, D)
    b1r = np.asarray(L0_ffn_ln1_b, f32).reshape(1, D)
    wgA = np.ascontiguousarray(np.asarray(L0_ffn_wg, f32))
    w1 = np.asarray(L0_ffn_w1, f32)
    b1 = np.asarray(L0_ffn_b1, f32)
    w2 = np.asarray(L0_ffn_w2, f32)
    b2 = np.asarray(L0_ffn_b2, f32)
    wfT = np.ascontiguousarray(np.asarray(L0_attn_Wf, f32).T)
    lng = np.asarray(L0_attn_ln_gamma, f32).reshape(1, D)
    lnb = np.asarray(L0_attn_ln_beta, f32).reshape(1, D)
    g2r = np.asarray(L0_ffn_ln2_g, f32).reshape(1, D)
    b2r = np.asarray(L0_ffn_ln2_b, f32).reshape(1, D)

    xs = [np.ascontiguousarray(x[:, c * NLOC:(c + 1) * NLOC, :].reshape(TOK, D))
          for c in range(CORES)]

    nc1 = build_launch1()
    in_maps1 = []
    for c in range(CORES):
        in_maps1.append({
            "x": xs[c], "wrT": wrT, "attr": attr, "ipT": ipT, "ipb": ipb,
            "opT": opT, "opb": opb, "g1": g1, "b1r": b1r, "wg": wgA,
            "w1T": np.ascontiguousarray(w1[c].T),
            "w2e": np.ascontiguousarray(w2[c]),
            "b1e": np.ascontiguousarray(b1[c].reshape(FF // 128, 128).T),
            "b2e": b2[c].reshape(1, D),
            "idf": idf, "idb": idb, "onesrow_b": onesrow_b,
            "onesrow_f": onesrow_f, "onescol_f": onescol_f, "iotaE": iotaE,
        })
    res1 = run_bass_kernel_spmd(nc1, in_maps1, core_ids=list(range(CORES)))
    r1 = res1.results

    mz_all = np.stack([
        np.concatenate([np.asarray(r1[c]["o_m"], f32).T.reshape(-1),
                        np.asarray(r1[c]["o_z"], f32).T.reshape(-1)])
        for c in range(CORES)], axis=0).astype(f32)
    s_all = np.ascontiguousarray(
        np.stack([np.asarray(r1[c]["o_s"], f32).reshape(-1) for c in range(CORES)]))
    cnt_all = np.ascontiguousarray(
        np.stack([np.asarray(r1[c]["o_cnt"], f32).reshape(L, E)
                  for c in range(CORES)]).reshape(CORES * L, E))
    w12_all = np.ascontiguousarray(
        np.concatenate([np.asarray(r1[c]["o_w12"]) for c in range(CORES)], axis=0))
    b12_all = np.ascontiguousarray(
        np.stack([np.asarray(r1[c]["o_b12"], f32).reshape(-1) for c in range(CORES)]))

    nc2 = build_launch2()
    in_maps2 = []
    for c in range(CORES):
        wcm = np.zeros((CORES * L, L), dtype=f32)
        for cp in range(CORES):
            for lp in range(L):
                for l in range(L):
                    if lp < l or (lp == l and cp < c):
                        wcm[cp * L + lp, l] = 1.0
        in_maps2.append({
            "x": xs[c],
            "xn": np.asarray(r1[c]["o_xn"]),
            "y1": np.asarray(r1[c]["o_y1"]),
            "idxv": np.asarray(r1[c]["o_idx"], f32),
            "gatev": np.asarray(r1[c]["o_gate"], f32),
            "mzall": mz_all, "sall": s_all, "cntall": cnt_all, "wcm": wcm,
            "w12all": w12_all, "b12all": b12_all,
            "wfT": wfT, "lng": lng, "lnb": lnb, "g2r": g2r, "b2r": b2r,
            "idf": idf, "idb": idb, "onesrow_b": onesrow_b,
            "onesrow_f": onesrow_f, "onescol_f": onescol_f,
            "ltri": ltri, "iotaE": iotaE,
            "lnconsts": np.tile(np.array([[EPS, -0.5, -1.0 / D, 1.0 / D]],
                                         dtype=f32), (128, 1)),
        })
    res2 = run_bass_kernel_spmd(nc2, in_maps2, core_ids=list(range(CORES)))

    out = np.empty((L, NSEQ, D), dtype=f32)
    for c in range(CORES):
        oc = np.asarray(res2.results[c]["o_out"], f32).reshape(L, NLOC, D)
        out[:, c * NLOC:(c + 1) * NLOC, :] = oc
    return out
